# version 02
# baseline (speedup 1.0000x reference)
import numpy as np
import ml_dtypes
from contextlib import ExitStack
import concourse.bass as bass
import concourse.mybir as mybir
from concourse.bass_utils import run_bass_kernel_spmd

F32 = mybir.dt.float32
BF16 = mybir.dt.bfloat16
I32 = mybir.dt.int32
AF = mybir.ActivationFunctionType
ALU = mybir.AluOpType
AX = mybir.AxisListType.X
BF = ml_dtypes.bfloat16

D = 1024
T = 4096
NCH = 8
XBW = 4104
XBO = 4
TT = 512
NTT = T // TT

ENGINES = ("t", "s", "v", "g", "q")


def cdiv(a, b):
    return (a + b - 1) // b


def chunks_of(n, c=128):
    return [(i, min(c, n - i)) for i in range(0, n, c)]


class KB:

    def __init__(self):
        self.ops = {e: [] for e in ENGINES}
        self.cnt = {}
        self.pend = {e: {} for e in ENGINES}

    def _sem(self, s):
        if s not in self.cnt:
            self.cnt[s] = 0

    def on(self, e, fn, deps=(), inc=None, inc_amt=1):
        waits = dict(self.pend[e])
        self.pend[e] = {}
        for d in deps:
            if d is None:
                continue
            s, v = d
            if v > waits.get(s, -1):
                waits[s] = v
        tok = None
        incs = []
        if inc is not None:
            self._sem(inc)
            self.cnt[inc] += inc_amt
            incs.append((inc, inc_amt))
            tok = (inc, self.cnt[inc])
        self.ops[e].append((fn, sorted(waits.items()), incs))
        return tok

    def tok(self, s):
        self._sem(s)
        return (s, self.cnt[s])

    def dma(self, e, out_ap, in_ap, deps=()):
        return self.on(e, lambda eng, o=out_ap, i=in_ap: eng.dma_start(o, i),
                       deps, inc="d" + e, inc_amt=16)

    def fence(self, engines=ENGINES):
        snap = list(self.cnt.items())
        for e in engines:
            for s, c in snap:
                if c > self.pend[e].get(s, -1):
                    self.pend[e][s] = c

    def emit(self, nc, es):
        sems = {s: es.enter_context(nc.semaphore(s)) for s in self.cnt}
        block = es.enter_context(nc.Block())

        def runner(ename):
            def go(eng):
                for fn, waits, incs in self.ops[ename]:
                    for s, v in waits:
                        eng.wait_ge(sems[s], v)
                    ins = fn(eng)
                    for s, amt in incs:
                        ins.then_inc(sems[s], amt)
            return go
        block.tensor(runner("t"))
        block.scalar(runner("s"))
        block.vector(runner("v"))
        block.gpsimd(runner("g"))
        block.sync(runner("q"))


class Pool:
    def __init__(self, tensors):
        self.slots = [[tns, None] for tns in tensors]
        self.i = 0

    def take(self):
        slot = self.slots[self.i % len(self.slots)]
        self.i += 1
        return slot


class Ctx:
    def __init__(self, nc, kb, es):
        self.nc, self.kb, self.es = nc, kb, es
        sb = lambda name, shape, dt: es.enter_context(nc.sbuf_tensor(name, shape, dt))
        self.BIG = sb("BIG", [128, 49152], BF16)
        self.FW = sb("FW", [128, 4096], F32)
        self.FW2 = sb("FW2", [128, 2048], F32)
        self.CAR = sb("CAR", [128, 16384], BF16)
        self.CF = sb("CF", [128, 3072], F32)
        self.STATS = sb("STATS", [16, 8192], F32)
        self.ITMP = sb("ITMP", [128, 32], I32)
        self.ones_colf = sb("ones_colf", [128, 1], F32)
        self.ones_rowf = sb("ones_rowf", [1, 128], F32)
        self.ones_row = sb("ones_row", [1, 128], BF16)
        self.ones_colb = sb("ones_colb", [128, 1], BF16)
        self.ident = sb("ident", [128, 128], BF16)
        self.identf = sb("identf", [128, 128], F32)
        self.ps = [es.enter_context(nc.psum_tensor(f"ps{i}", [128, 512], F32))
                   for i in range(8)]
        self.pool = Pool(self.ps[:5])
        self.spool = Pool(self.ps[5:7])
        self.opool = Pool(self.ps[7:8])
        kb._sem("init")
        self.wg_users = {}

    def setup(self):
        kb = self.kb
        last = [None]

        def g(fn):
            last[0] = kb.on("g", fn, deps=[last[0]], inc="init")
        g(lambda gg: gg.memset(self.ones_colf[:, :], 1.0))
        g(lambda gg: gg.memset(self.ones_rowf[:, :], 1.0))
        g(lambda gg: gg.memset(self.ones_row[:, :], 1.0))
        g(lambda gg: gg.memset(self.ones_colb[:, :], 1.0))
        g(lambda gg: gg.memset(self.ident[:, :], 1.0))
        g(lambda gg: gg.affine_select(self.ident[:, :], self.ident[:, :],
          pattern=[[1, 128]], compare_op=ALU.is_equal, fill=0.0, base=0,
          channel_multiplier=-1))
        g(lambda gg: gg.memset(self.identf[:, :], 1.0))
        g(lambda gg: gg.affine_select(self.identf[:, :], self.identf[:, :],
          pattern=[[1, 128]], compare_op=ALU.is_equal, fill=0.0, base=0,
          channel_multiplier=-1))
        g(lambda gg: gg.memset(self.CF[:, 3040:3041], np.pi / 2))
        g(lambda gg: gg.memset(self.CF[:, 3041:3042], 1e-5))
        self.cpi2 = self.CF[:, 3040:3041]
        self.ceps = self.CF[:, 3041:3042]
        self.init_tok = ("init", kb.cnt["init"])
        self.kb.fence()


def load_w_group(ctx, W_ap, din, g0, gw, slab_off, deps=()):
    kb = ctx.kb
    chs = chunks_of(din)
    toks = list(deps)
    nfull = sum(1 for _, sz in chs if sz == 128)
    out_toks = []
    if nfull:
        src = W_ap[0:nfull * 128, g0:g0 + gw].rearrange("(n p) w -> p n w", p=128)
        dst = ctx.BIG[:, slab_off:slab_off + nfull * gw].rearrange("p (n w) -> p n w", w=gw)
        out_toks.append(kb.dma("g", dst, src, deps))
    for ci, (st, sz) in enumerate(chs):
        if sz == 128:
            continue
        src = W_ap[st:st + sz, g0:g0 + gw]
        dst = ctx.BIG[0:sz, slab_off + ci * gw:slab_off + ci * gw + gw]
        out_toks.append(kb.dma("g", dst, src, deps))
    return out_toks


def slab_lhsT(ctx, slab_off, gw, ci, c0, cw, psz):
    return ctx.BIG[0:psz, slab_off + ci * gw + c0:slab_off + ci * gw + c0 + cw]


def mmA(ctx, tag, W_ap, din, dout, xsrc, evac, slab_off, gw=512, tt=TT,
        ntok=T, extra_deps=()):
    kb = ctx.kb
    chs = chunks_of(din)
    slab_sz = len(chs) * gw
    for g in range(cdiv(dout, gw)):
        g0, gww = g * gw, min(gw, dout - g * gw)
        par = g % 2
        off = slab_off + par * slab_sz
        deps = list(extra_deps)
        pu = ctx.wg_users.get((tag, par))
        if pu is not None:
            deps.append(pu)
        wtoks = load_w_group(ctx, W_ap, din, g0, gww, off, deps)
        last_mm = None
        for dc0, dcw in chunks_of(gww):
            for ti in range(cdiv(ntok, tt)):
                t0 = ti * tt
                slot = ctx.pool.take()
                ps = slot[0]
                for ci, (cst, csz) in enumerate(chs):
                    xa, xd = xsrc(ci, t0, tt)
                    mdeps = (wtoks + [slot[1]] + list(xd)) if ci == 0 else ()
                    fn = (lambda te, o=ps[0:dcw, 0:tt],
                          l=slab_lhsT(ctx, off, gww, ci, dc0, dcw, csz), r=xa,
                          st=(ci == 0), sp=(ci == len(chs) - 1):
                          te.matmul(o, l, r, start=st, stop=sp))
                    tok = kb.on("t", fn, deps=mdeps,
                                inc="t" if ci == len(chs) - 1 else None)
                    if ci == len(chs) - 1:
                        last_mm = tok
                evac(g0 + dc0, dcw, t0, tt, slot, [last_mm])
        ctx.wg_users[(tag, par)] = last_mm


def mmB(ctx, tag, W_ap, din, dout, xload, evac, woff, xoff, tt=TT, ntok=T,
        extra_deps=(), wdeps_extra=(), xbufs=1):
    kb = ctx.kb
    chs = chunks_of(din)
    wtoks = load_w_group(ctx, W_ap, din, 0, dout, woff, list(wdeps_extra))
    xfree = {}
    for ti in range(cdiv(ntok, tt)):
        t0 = ti * tt
        par = ti % xbufs
        xtoks = []
        xaps = []
        for ci, (cst, csz) in enumerate(chs):
            src, dst, deps = xload(ci, t0, tt, par)
            if src is not None:
                xtoks.append(kb.dma("q", dst, src,
                                    list(deps) + ([xfree[par]] if par in xfree else [])))
            else:
                xtoks += list(deps)
            xaps.append((dst, csz))
        last_mm = None
        for dc0, dcw in chunks_of(dout):
            slot = ctx.pool.take()
            ps = slot[0]
            for ci, (cst, csz) in enumerate(chs):
                mdeps = (wtoks + xtoks + [slot[1]] + list(extra_deps)) if ci == 0 else ()
                fn = (lambda te, o=ps[0:dcw, 0:tt],
                      l=slab_lhsT(ctx, woff, dout, ci, dc0, dcw, csz),
                      r=xaps[ci][0],
                      st=(ci == 0), sp=(ci == len(chs) - 1):
                      te.matmul(o, l, r, start=st, stop=sp))
                tok = kb.on("t", fn, deps=mdeps,
                            inc="t" if ci == len(chs) - 1 else None)
                if ci == len(chs) - 1:
                    last_mm = tok
            evac(dc0, dcw, t0, tt, slot, [last_mm])
        xfree[par] = last_mm


class PostLN:

    def __init__(self, ctx, X, XB, bias_col=None, scale=1.0):
        self.ctx, self.X, self.XB = ctx, X, XB
        self.bias_col = bias_col
        self.scale = scale
        self.fw_free = {}
        self.x_free = {}
        self.stat_done = []

    def evac(self, dc0, dcw, t0, tw, slot, deps):
        ctx, kb = self.ctx, self.ctx.kb
        ps = slot[0]
        dc = dc0 // 128
        s_tile = ctx.FW[0:dcw, (dc % 4) * 512:(dc % 4) * 512 + tw]
        sq_tile = ctx.FW[0:dcw, 2048 + (dc % 4) * 512:2048 + (dc % 4) * 512 + tw]
        xt = ctx.FW2[0:dcw, (dc % 4) * 512:(dc % 4) * 512 + tw]
        ltok = kb.dma("q", xt, self.X[dc0:dc0 + dcw, t0:t0 + tw],
                      deps=[self.x_free.get(dc % 4)])
        b = 0.0 if self.bias_col is None else self.bias_col(dc0, dcw)
        t1 = kb.on("s", lambda s, o=s_tile, i=ps[0:dcw, 0:tw], bb=b, sc=self.scale:
                   s.activation(o, i, AF.Identity, bias=bb, scale=sc),
                   list(deps) + (self.fw_free.get(dc % 4) or []), inc="s")
        t2 = kb.on("v", lambda v, o=s_tile, a=s_tile, x=xt: v.tensor_add(o, a, x),
                   [t1, ltok], inc="v")
        self.x_free[dc % 4] = t2
        slot[1] = t2
        st1 = kb.dma("q", self.X[dc0:dc0 + dcw, t0:t0 + tw], s_tile, [t2])
        t3 = kb.on("s", lambda s, o=sq_tile, i=s_tile: s.activation(o, i, AF.Square),
                   [t2], inc="s")
        if dc == 0:
            self.sA = ctx.spool.take()
            self.sB = ctx.spool.take()
        first = (dc == 0)
        dA = [self.sA[1]] if first else []
        dB = [self.sB[1]] if first else []
        ts1 = kb.on("t", lambda te, o=self.sA[0][0:1, 0:tw], l=ctx.ones_colf[0:dcw, 0:1],
                    r=s_tile: te.matmul(o, l, r, start=first, stop=(dc == NCH - 1)),
                    [t2] + dA, inc="t")
        ts2 = kb.on("t", lambda te, o=self.sB[0][0:1, 0:tw], l=ctx.ones_colf[0:dcw, 0:1],
                    r=sq_tile: te.matmul(o, l, r, start=first, stop=(dc == NCH - 1)),
                    [t3] + dB, inc="t")
        self.fw_free[dc % 4] = [st1, ts2]
        if dc == NCH - 1:
            ST = self.ctx.STATS
            c1 = kb.on("v", lambda v, o=ST[0:1, t0:t0 + tw], i=self.sA[0][0:1, 0:tw]:
                       v.tensor_copy(o, i), [ts1, ts2], inc="v")
            c2 = kb.on("v", lambda v, o=ST[0:1, 4096 + t0:4096 + t0 + tw],
                       i=self.sB[0][0:1, 0:tw]: v.tensor_copy(o, i), [c1], inc="v")
            self.sA[1] = c1
            self.sB[1] = c2
            self.stat_done.append(c2)

    def apply(self, gain_row=None, bias_col=None, xb_extra=None):
        ctx, kb = self.ctx, self.ctx.kb
        ST = ctx.STATS
        deps = list(self.stat_done)
        t1 = kb.on("v", lambda v: v.tensor_scalar_mul(ST[0:1, 0:T], ST[0:1, 0:T], 1.0 / D),
                   deps, inc="v")
        t2 = kb.on("v", lambda v: v.tensor_scalar_mul(ST[0:1, 4096:4096 + T],
                   ST[0:1, 4096:4096 + T], 1.0 / D), [t1], inc="v")
        mrow = ctx.FW[0:1, 0:T]
        t3 = kb.on("v", lambda v: v.tensor_mul(mrow, ST[0:1, 0:T], ST[0:1, 0:T]),
                   [t2], inc="v")
        t4 = kb.on("v", lambda v: v.tensor_sub(mrow, ST[0:1, 4096:4096 + T], mrow),
                   [t3], inc="v")
        t5 = kb.on("s", lambda s: s.activation(mrow, mrow, AF.Sqrt, bias=ctx.ceps[0:1, 0:1]),
                   [t4], inc="s")
        t6 = kb.on("v", lambda v: v.reciprocal(mrow, mrow), [t5], inc="v")
        t7 = kb.on("v", lambda v: v.tensor_mul(ST[0:1, 0:T], ST[0:1, 0:T], mrow),
                   [t6], inc="v")
        kb.fence()
        xfree = {}
        for dc in range(NCH):
            d0 = dc * 128
            grow = (ctx.ones_rowf[0:1, 0:128] if gain_row is None
                    else gain_row(d0, 128))
            for ti in range(NTT):
                t0 = ti * TT
                slot = ctx.spool.take()
                sps = slot[0]
                m1 = kb.on("t", lambda te, o=sps[:, 0:TT], l=grow,
                           r=mrow[0:1, t0:t0 + TT]: te.matmul(o, l, r, start=True, stop=True),
                           [slot[1]], inc="t")
                pslot = ctx.pool.take()
                pps = pslot[0]
                m2 = kb.on("t", lambda te, o=pps[:, 0:TT], l=grow,
                           r=ST[0:1, t0:t0 + TT]: te.matmul(o, l, r, start=True, stop=True),
                           [pslot[1]], inc="t")
                xt = ctx.FW2[:, (dc % 4) * 512:(dc % 4) * 512 + TT]
                lt = kb.dma("q", xt, self.X[d0:d0 + 128, t0:t0 + TT],
                            deps=[xfree.get(dc % 4)])
                v1 = kb.on("v", lambda v, o=xt, a=xt, b=sps[:, 0:TT]:
                           v.tensor_mul(o, a, b), [m1, lt], inc="v")
                bcol = 0.0 if bias_col is None else bias_col(d0, 128)
                v2 = kb.on("v", lambda v, o=xt, a=xt, sc=bcol, b=pps[:, 0:TT]:
                           v.scalar_tensor_tensor(o, a, sc, b, ALU.add, ALU.subtract),
                           [m2, v1], inc="v")
                slot[1] = v2
                pslot[1] = v2
                st1 = kb.dma("q", self.X[d0:d0 + 128, t0:t0 + TT], xt, [v2])
                st2 = kb.dma("g", self.XB[:, dc * XBW + XBO + t0:dc * XBW + XBO + t0 + TT],
                             xt, [v2])
                if xb_extra is not None:
                    xb_extra(dc, t0, xt, [v2])
                xfree[dc % 4] = kb.tok("dg")
        kb.fence()


XRES_SZ = NCH * XBW


def xres_load(ctx, xoff, XB, deps=()):
    kb = ctx.kb
    return kb.dma("q", ctx.BIG[:, xoff:xoff + NCH * XBW], XB[:, 0:NCH * XBW], deps)


def xres_src(ctx, xoff, shift_hi=False):
    def src(ci, t0, tw):
        off = xoff + ci * XBW + XBO + t0
        if shift_hi and ci >= 4:
            off -= 1
        return ctx.BIG[:, off:off + tw], []
    return src


def emit_L1_ffn(ctx, io, X, XB, H):
    nc, kb = ctx.nc, ctx.kb
    DF = 4096
    thc = ctx.CF[:, 2048:2048 + 32]
    wrc = ctx.CF[:, 2080:2080 + 32]
    wic = ctx.CF[:, 2112:2112 + 32]
    P1 = ctx.CF[:, 2144:2144 + 32]
    P2 = ctx.CF[:, 2176:2176 + 32]
    b1c = ctx.CF[:, 2208:2208 + 32]
    b2c = ctx.CF[:, 2240:2240 + 8]
    tmp = ctx.CF[:, 2248:2248 + 32]
    l1 = kb.dma("q", thc, io["L1_ffn_quantum_phases_c"])
    l2 = kb.dma("q", wrc, io["L1_ffn_quantum_weights_real_c"])
    l3 = kb.dma("q", wic, io["L1_ffn_quantum_weights_imag_c"])
    l4 = kb.dma("q", b1c, io["L1_ffn_b1_c"])
    l5 = kb.dma("q", b2c, io["L1_ffn_b2_c"])
    pi2 = 2 * np.pi
    iti = ctx.ITMP[:, 0:32]
    r0 = kb.on("v", lambda v: v.tensor_scalar_mul(tmp, thc, 1.0 / pi2), [l1], inc="v")
    ra = kb.on("v", lambda v: v.tensor_copy(iti, tmp), [r0], inc="v")
    rb = kb.on("v", lambda v: v.tensor_copy(tmp, iti), [ra], inc="v")
    r1 = kb.on("v", lambda v: v.scalar_tensor_tensor(thc, tmp, -pi2, thc, ALU.mult, ALU.add),
               [rb], inc="v")
    r2a = kb.on("v", lambda v: v.tensor_single_scalar(tmp, thc, np.pi, ALU.is_gt), [r1], inc="v")
    r2b = kb.on("v", lambda v: v.scalar_tensor_tensor(thc, tmp, -pi2, thc, ALU.mult, ALU.add),
                [r2a], inc="v")
    r2c = kb.on("v", lambda v: v.tensor_single_scalar(tmp, thc, -np.pi, ALU.is_lt), [r2b], inc="v")
    r3 = kb.on("v", lambda v: v.scalar_tensor_tensor(thc, tmp, pi2, thc, ALU.mult, ALU.add),
               [r2c], inc="v")
    snc = ctx.CF[:, 2280:2280 + 32]
    csc = ctx.CF[:, 2312:2312 + 32]
    r4 = kb.on("s", lambda s: s.activation(snc, thc, AF.Sin), [r3], inc="s")
    r5 = kb.on("s", lambda s: s.activation(csc, thc, AF.Sin, bias=ctx.cpi2), [r3], inc="s")
    r6 = kb.on("v", lambda v: v.tensor_mul(tmp, csc, wrc), [r5, l2], inc="v")
    r7 = kb.on("v", lambda v: v.scalar_tensor_tensor(P1, snc, 1.0, wic, ALU.mult, ALU.mult),
               [r4, l3], inc="v")
    r8 = kb.on("v", lambda v: v.tensor_add(tmp, tmp, P1), [r6, r7], inc="v")
    r9 = kb.on("v", lambda v: v.scalar_tensor_tensor(P1, tmp, 0.1, wic, ALU.mult, ALU.mult),
               [r8], inc="v")
    rA = kb.on("v", lambda v: v.tensor_mul(P2, P1, csc), [r9], inc="v")
    rB = kb.on("v", lambda v: v.tensor_mul(P1, P1, snc), [rA], inc="v")
    kb.fence()

    xoff = 0
    xtok = xres_load(ctx, xoff, XB)
    xsrc = xres_src(ctx, xoff)
    hfree = {}
    b1s = ctx.CF[:, 2344:2344 + 32]
    b1sc = ctx.CF[:, 2376:2376 + 32]
    u1 = kb.on("v", lambda v: v.tensor_scalar_mul(b1s, b1c, 0.1), [l4], inc="v")
    kb.on("v", lambda v: v.tensor_scalar_add(b1sc, b1s, np.pi / 2), [u1], inc="v")
    kb.fence()

    def evac1(d0, dw, t0, tw, slot, deps):
        ps = slot[0]
        hc = d0 // 128
        par = ctx._l1h_par = (getattr(ctx, "_l1h_par", 0) + 1) % 2
        boff = XRES_SZ + par * 2048
        cosb = ctx.BIG[0:dw, boff:boff + tw]
        sinb = ctx.BIG[0:dw, boff + 512:boff + 512 + tw]
        qb = ctx.BIG[0:dw, boff + 1024:boff + 1024 + tw]
        dd = list(deps) + [hfree.get(par)]
        t1 = kb.on("s", lambda s, o=cosb, i=ps[0:dw, 0:tw], bc=b1sc[0:dw, hc:hc + 1]:
                   s.activation(o, i, AF.Sin, bias=bc, scale=0.1), dd, inc="s")
        t2 = kb.on("s", lambda s, o=sinb, i=ps[0:dw, 0:tw], bc=b1s[0:dw, hc:hc + 1]:
                   s.activation(o, i, AF.Sin, bias=bc, scale=0.1), [t1], inc="s")
        t3 = kb.on("v", lambda v, o=qb, a=cosb, sc=P1[0:dw, hc:hc + 1], b=ps[0:dw, 0:tw]:
                   v.scalar_tensor_tensor(o, a, sc, b, ALU.mult, ALU.add), [t2], inc="v")
        t4 = kb.on("v", lambda v, o=qb, a=sinb, sc=P2[0:dw, hc:hc + 1], b=qb:
                   v.scalar_tensor_tensor(o, a, sc, b, ALU.mult, ALU.add), [t3], inc="v")
        slot[1] = t4
        t5 = kb.on("g", lambda g, o=qb, a=qb, bc=b1c[0:dw, hc:hc + 1]:
                   g.tensor_scalar(o, a, bc, 0.0, ALU.add, ALU.max), [t4], inc="g")
        htok = kb.dma("g", H[d0:d0 + dw, t0:t0 + tw], qb, [t5])
        hfree[par] = htok

    mmA(ctx, "l1w1", io["L1_ffn_W1"], D, DF, xsrc, evac1,
        slab_off=XRES_SZ + 4096, gw=512, extra_deps=[xtok])
    kb.fence()

    pl = PostLN(ctx, X, XB, bias_col=lambda d0, dw: b2c[0:dw, d0 // 128:d0 // 128 + 1])

    def xload(ci, t0, tw, par):
        dst = ctx.BIG[:, 32768 + ci * 512:32768 + ci * 512 + tw]
        return H[ci * 128:ci * 128 + 128, t0:t0 + tw], dst, []

    mmB(ctx, "l1w2", io["L1_ffn_W2"], DF, D, xload, pl.evac, woff=0, xoff=32768)
    kb.fence()
    pl.apply()


def norms_finish(ctx, deps, blkw):
    kb = ctx.kb
    nb = T // blkw
    ST, FW = ctx.STATS, ctx.FW
    t1 = kb.on("s", lambda s: s.activation(ST[0:16, 0:8192], ST[0:16, 0:8192], AF.Sqrt),
               deps, inc="s")
    qb = FW[0:16, 3072:3072 + nb]
    kbx = FW[0:16, 3072 + nb:3072 + 2 * nb]
    t2 = kb.on("v", lambda v: v.reduce_max(qb, ST[0:16, 0:4096].rearrange(
        "h (b w) -> h b w", w=blkw), axis=AX), [t1], inc="v")
    t3 = kb.on("v", lambda v: v.reduce_max(kbx, ST[0:16, 4096:8192].rearrange(
        "h (b w) -> h b w", w=blkw), axis=AX), [t2], inc="v")
    return qb, kbx, [t3]


def negm_col(ctx, sel_ap, mcol_ap, dst_col, bank, deps, extra_scale=-1.0):
    kb = ctx.kb
    m1 = kb.on("t", lambda te, o=bank[0][0:128, 0:1], l=sel_ap, r=mcol_ap:
               te.matmul(o, l, r, start=True, stop=True), list(deps) + [bank[1]], inc="t")
    c1 = kb.on("s", lambda sE, o=dst_col, i=bank[0][0:128, 0:1], sc=extra_scale:
               sE.activation(o, i, AF.Identity, scale=sc), [m1], inc="s")
    bank[1] = c1
    return c1


def emit_L3_ffn(ctx, io, X, XB, H):
    nc, kb = ctx.nc, ctx.kb
    s127 = 1.0 / 127.0
    xoff = 0
    xtok = xres_load(ctx, xoff, XB)
    xsrc = xres_src(ctx, xoff)
    stash_off = XRES_SZ
    hb_off = XRES_SZ + 4096
    slab_off = XRES_SZ + 4096 + 1024
    hfree = {}
    stash_free = {}

    def evac(d0, dw, t0, tw, slot, deps):
        ps = slot[0]
        dc = d0 // 128
        hc = dc // 2
        ti = t0 // TT
        stash = ctx.BIG[0:dw, stash_off + ti * 512:stash_off + ti * 512 + tw]
        if dc % 2 == 0:
            tok = kb.on("s", lambda sE, o=stash, i=ps[0:dw, 0:tw]:
                        sE.activation(o, i, AF.Silu, scale=s127),
                        list(deps) + (stash_free.get(ti) or []), inc="s")
            slot[1] = tok
            stash_free[ti] = [tok]
        else:
            par = (hc * NTT + ti) % 2
            hb = ctx.BIG[0:dw, hb_off + par * 512:hb_off + par * 512 + tw]
            tok = kb.on("v", lambda v, o=hb, a=ps[0:dw, 0:tw], b=stash:
                        v.tensor_mul(o, a, b),
                        list(deps) + [hfree.get(par)] + (stash_free.get(ti) or []),
                        inc="v")
            slot[1] = tok
            stash_free[ti] = (stash_free.get(ti) or []) + [tok]
            htok = kb.dma("g", H[hc * 128:hc * 128 + dw, t0:t0 + tw], hb, [tok])
            hfree[par] = htok

    mmA(ctx, "l3gu", io["L3_ffn_wgu"], D, 6144, xsrc, evac,
        slab_off=slab_off, gw=512, extra_deps=[xtok])
    kb.fence()

    b0 = ctx.CF[:, 2240:2248]
    z = kb.on("g", lambda g: g.memset(ctx.CF[:, 2240:2248], 0.0), inc="g")
    kb.fence()
    pl = PostLN(ctx, X, XB, scale=s127 * s127)

    def xload(ci, t0, tw, par):
        dst = ctx.BIG[:, 24576 + ci * 512:24576 + ci * 512 + tw]
        return H[ci * 128:ci * 128 + 128, t0:t0 + tw], dst, []

    mmB(ctx, "l3dn", io["L3_ffn_wd"], 3072, D, xload, pl.evac, woff=0, xoff=24576)
    kb.fence()
    pl.apply()


def emit_L3_attn(ctx, io, X, XB, H):
    nc, kb = ctx.nc, ctx.kb
    QS, KS, VS, AO = io["_QS"], io["_KS"], io["_VS"], io["_AO"]
    bqkv = ctx.CF[:, 2048:2048 + 24]
    lb = kb.dma("q", bqkv, io["L3_attn_b_qkv_c"])
    bq8 = ctx.CF[:, 2072:2072 + 8]
    u0 = kb.on("v", lambda v: v.tensor_scalar_mul(bq8, ctx.CF[:, 2048:2056], 0.125),
               [lb], inc="v")
    ind16 = ctx.CF[:, 2112:2112 + 128]
    sel = ctx.FW[0:16, 1024:1024 + 2048]
    li = kb.dma("q", ind16, io["_ind16"])
    ls = kb.dma("q", sel, io["_sel16"])
    bvf = ctx.CF[0:1, 2040:2048]
    bvrow_f = ctx.CF[0:1, 1024:2048]
    lv = kb.dma("q", bvrow_f, io["L3_attn_b_qkv"].rearrange("(a n) -> a n", a=1)[0:1, 2048:3072])
    bvrow = ctx.CAR[0:1, 0:1024]
    uv = kb.on("v", lambda v: v.tensor_copy(bvrow, bvrow_f), [lv], inc="v")
    kb.fence()
    wtoks = load_w_group(ctx, io["L3_attn_W_qkv"], D, 0, 3072, 0)
    xoff, stg = 24576, 33280
    xfree = {}
    stg_free = {}
    for ti in range(NTT):
        t0 = ti * TT
        par = ti % 2
        xo = xoff + par * 4096
        src = XB[:, 0:NCH * XBW].rearrange("p (c t) -> p c t", t=XBW)[:, :, XBO + t0:XBO + t0 + TT]
        dst = ctx.BIG[:, xo:xo + 4096].rearrange("p (c t) -> p c t", t=TT)
        xt = kb.dma("q", dst, src, [xfree.get(par)])
        xa = lambda ci, xo=xo: ctx.BIG[:, xo + ci * TT:xo + ci * TT + TT]
        nq = ctx.spool.take()
        nk = ctx.spool.take()
        for dc in range(16):
            slot = ctx.pool.take()
            ps = slot[0]
            for ci in range(8):
                mdeps = (wtoks + [xt, slot[1]]) if ci == 0 else ()
                tok = kb.on("t", lambda te, o=ps[:, 0:TT],
                            l=slab_lhsT(ctx, 0, 3072, ci, dc * 128, 128, 128),
                            r=xa(ci), st=(ci == 0), sp=(ci == 7):
                            te.matmul(o, l, r, start=st, stop=sp),
                            mdeps, inc="t" if ci == 7 else None)
            isq = dc < 8
            c = dc % 8
            stgb = ctx.BIG[:, stg + (dc % 2) * 512:stg + (dc % 2) * 512 + TT]
            sq = ctx.FW[:, (dc % 2) * 512:(dc % 2) * 512 + TT]
            bias = bq8[:, c:c + 1] if isq else bqkv[:, 8 + c:9 + c]
            e1 = kb.on("s", lambda sE, o=stgb, i=ps[:, 0:TT], bb=bias,
                       sc=(0.125 if isq else 1.0):
                       sE.activation(o, i, AF.Identity, bias=bb, scale=sc),
                       [tok, u0] + (stg_free.get(dc % 2) or []), inc="s")
            d1 = kb.dma("g", (QS if isq else KS)[c * 128:c * 128 + 128, t0:t0 + TT],
                        stgb, [e1])
            e2 = kb.on("s", lambda sE, o=sq, i=stgb: sE.activation(o, i, AF.Square),
                       [e1], inc="s")
            npsum = nq if isq else nk
            first = (c == 0)
            n1 = kb.on("t", lambda te, o=npsum[0][0:16, 0:TT],
                       l=ind16[:, c * 16:c * 16 + 16], r=sq,
                       st=first, sp=(c == 7): te.matmul(o, l, r, start=st, stop=sp),
                       [e2, li] + ([npsum[1]] if first else []), inc="t")
            stg_free[dc % 2] = [d1, e2]
            slot[1] = e1
            if dc == 7:
                cq = kb.on("v", lambda v, o=ctx.STATS[0:16, t0:t0 + TT],
                           i=nq[0][0:16, 0:TT]: v.tensor_copy(o, i), [n1], inc="v")
                nq[1] = cq
            if dc == 15:
                ck = kb.on("v", lambda v, o=ctx.STATS[0:16, 4096 + t0:4096 + t0 + TT],
                           i=nk[0][0:16, 0:TT]: v.tensor_copy(o, i), [n1], inc="v")
                nk[1] = ck
        for sub in range(4):
            for sl in range(2):
                slot = ctx.pool.take()
                ps = slot[0]
                kb.on("t", lambda te, o=ps[:, 0:TT], l=ctx.ones_row[0:1, 0:128],
                      r=bvrow[0:1, sl * 512:sl * 512 + 512]:
                      te.matmul(o, l, r, start=True, stop=False),
                      [slot[1], uv], inc=None)
                for ci in range(8):
                    tok = kb.on("t", lambda te, o=ps[:, 0:TT],
                                l=ctx.BIG[:, xo + ci * TT + sub * 128:
                                          xo + ci * TT + sub * 128 + 128],
                                r=slab_lhsT(ctx, 0, 3072, ci, 2048 + sl * 512, TT, 128),
                                sp=(ci == 7): te.matmul(o, l, r, start=False, stop=sp),
                                (), inc="t" if ci == 7 else None)
                stgb = ctx.BIG[:, stg + 1024 + (sub % 2) * 512:stg + 1024 + (sub % 2) * 512 + TT]
                e1 = kb.on("s", lambda sE, o=stgb, i=ps[:, 0:TT]:
                           sE.activation(o, i, AF.Identity),
                           [tok] + (stg_free.get(10 + sub % 2) or []), inc="s")
                d1 = kb.dma("g", VS[t0 + sub * 128:t0 + sub * 128 + 128,
                                    sl * 512:sl * 512 + 512], stgb, [e1])
                stg_free[10 + sub % 2] = [d1]
                slot[1] = e1
        xfree[par] = kb.tok("t")
    kb.fence()
    qb, kbx, ntoks = norms_finish(ctx, [], 512)
    m1 = ctx.FW[0:16, 3104:3112]
    m2 = ctx.FW[0:16, 3112:3120]
    t1 = kb.on("v", lambda v: v.tensor_mul(m1, qb, kbx), ntoks, inc="v")
    t2 = kb.on("v", lambda v: v.tensor_mul(m2[0:16, 0:7], qb[0:16, 0:7], kbx[0:16, 1:8]),
               [t1], inc="v")
    t3 = kb.on("v", lambda v: v.tensor_mul(m2[0:16, 7:8], qb[0:16, 7:8], kbx[0:16, 0:1]),
               [t2], inc="v")
    kb.fence()
    bk = {i: None for i in range(8)}
    ufp = {}
    for b in range(8):
        for hp in range(8):
            par = (b * 8 + hp) % 2
            base = par * 8192
            QT = ctx.BIG[:, base:base + 512]
            KT = ctx.BIG[:, base + 512:base + 1536]
            VT = ctx.BIG[:, base + 1536:base + 2560]
            PT = ctx.BIG[:, base + 2560:base + 3072]
            OT = ctx.BIG[:, base + 3072:base + 3584]
            TMP = ctx.BIG[:, base + 3584:base + 4096]
            TMP2 = ctx.BIG[:, base + 4096:base + 4608]
            uf = ufp.get(par) or []
            bn = (b + 1) % 8
            lq = kb.dma("q", QT, QS[hp * 128:hp * 128 + 128, b * 512:b * 512 + 512], uf)
            lk1 = kb.dma("q", KT[:, 0:512], KS[hp * 128:hp * 128 + 128, b * 512:b * 512 + 512], [])
            lk2 = kb.dma("q", KT[:, 512:1024], KS[hp * 128:hp * 128 + 128, bn * 512:bn * 512 + 512], [])
            lv1 = kb.dma("q", VT[:, 0:512].rearrange("p (sb c) -> p sb c", c=128),
                         VS[b * 512:b * 512 + 512, hp * 128:hp * 128 + 128].rearrange(
                             "(sb p) c -> p sb c", p=128), [])
            lv2 = kb.dma("q", VT[:, 512:1024].rearrange("p (sb c) -> p sb c", c=128),
                         VS[bn * 512:bn * 512 + 512, hp * 128:hp * 128 + 128].rearrange(
                             "(sb p) c -> p sb c", p=128), [])
            loads = [lq, lk1, lk2, lv1, lv2]
            nm1 = ctx.CF[:, 2900 + par * 2:2901 + par * 2]
            nm2 = ctx.CF[:, 2901 + par * 2:2902 + par * 2]
            for h2 in range(2):
                hb = h2 * 64
                selh = sel[0:16, (2 * hp + h2) * 128:(2 * hp + h2) * 128 + 128]
                bank6 = [ctx.ps[6], bk[6]]
                c1 = negm_col(ctx, selh, m1[0:16, b:b + 1], nm1, bank6,
                              [t3, ls] + uf + (bk.get(("nm", par)) or []))
                c2 = negm_col(ctx, selh, m2[0:16, b:b + 1], nm2, bank6, [])
                bk[6] = bank6[1]
                fin_deps = []
                for sidx in range(2):
                    psL = ctx.ps[4 + sidx]
                    psO = ctx.ps[2 + sidx]
                    lm = pv = None
                    for kc in range(4):
                        psS = ctx.ps[kc % 2]
                        mm = kb.on("t", lambda te, o=psS[:, 0:512],
                                   l=KT[hb:hb + 64, sidx * 512 + kc * 128:sidx * 512 + kc * 128 + 128],
                                   r=QT[hb:hb + 64, 0:512], tp=(hb, 0):
                                   te.matmul(o, l, r, start=True, stop=True, tile_position=tp),
                                   loads + [c1, c2, bk[kc % 2]], inc="t")
                        ex = kb.on("s", lambda sE, o=PT[:, 0:512], i=psS[:, 0:512],
                                   bb=(nm1 if sidx == 0 else nm2):
                                   sE.activation(o, i, AF.Exp, bias=bb),
                                   [mm] + (bk.get("pt") or []), inc="s")
                        bk[kc % 2] = ex
                        lm = kb.on("t", lambda te, o=psL[0:1, 0:512], l=ctx.ones_colb[:, 0:1],
                                   r=PT[:, 0:512], st=(kc == 0), sp=(kc == 3):
                                   te.matmul(o, l, r, start=st, stop=sp),
                                   [ex] + ([bk[4 + sidx]] if kc == 0 else []), inc="t")
                        pv = kb.on("t", lambda te, o=psO[hb:hb + 64, 0:512],
                                   l=VT[:, sidx * 512 + kc * 128 + hb:sidx * 512 + kc * 128 + hb + 64],
                                   r=PT[:, 0:512], tp=(0, hb),
                                   st=(kc == 0), sp=(kc == 3):
                                   te.matmul(o, l, r, start=st, stop=sp, tile_position=tp),
                                   [ex] + ([bk[2 + sidx]] if kc == 0 else []), inc="t")
                        bk["pt"] = [lm, pv]
                        bk[("nm", par)] = [ex]
                    fin_deps.append((lm, pv))
                for sidx in range(2):
                    lmf, pvf = fin_deps[sidx]
                    lrow = ctx.FW2[0:1, sidx * 512:sidx * 512 + 512]
                    a1 = kb.on("v", lambda v, o=lrow, i=ctx.ps[4 + sidx][0:1, 0:512]:
                               v.tensor_scalar_add(o, i, 1e-30),
                               [lmf, bk.get(("lr", sidx))], inc="v")
                    a2 = kb.on("v", lambda v, o=lrow, i=lrow: v.reciprocal(o, i),
                               [a1], inc="v")
                    bk[4 + sidx] = a2
                    bc = kb.on("t", lambda te, o=ctx.ps[6][hb:hb + 64, 0:512],
                               l=ctx.ones_rowf[0:1, 0:64], r=lrow, tp=(0, hb):
                               te.matmul(o, l, r, start=True, stop=True, tile_position=tp),
                               [a2, bk.get(6)], inc="t")
                    bk[("lr", sidx)] = bc
                    dstR = (TMP if sidx == 0 else TMP2)
                    cp = kb.on("s", lambda sE, o=dstR[hb:hb + 64, 0:512],
                               i=ctx.ps[6][hb:hb + 64, 0:512]:
                               sE.activation(o, i, AF.Identity),
                               [bc, bk.get(("tmp", par, sidx))], inc="s")
                    bk[6] = cp
                    if sidx == 0:
                        mA = kb.on("v", lambda v, o=OT[hb:hb + 64, 0:512],
                                   a=ctx.ps[2][hb:hb + 64, 0:512], bb=TMP[hb:hb + 64, 0:512]:
                                   v.tensor_mul(o, a, bb),
                                   [cp, pvf, bk.get(("ot", par))], inc="v")
                        bk[2] = mA
                        bk[("tmp", par, 0)] = mA
                    else:
                        mB = kb.on("v", lambda v, o=TMP2[hb:hb + 64, 0:512],
                                   a=ctx.ps[3][hb:hb + 64, 0:512], bb=TMP2[hb:hb + 64, 0:512]:
                                   v.tensor_mul(o, a, bb), [cp, pvf], inc="v")
                        bk[3] = mB
                        bk[("tmp", par, 1)] = mB
                        kb.on("v", lambda v, o=OT[hb:hb + 64, 0:512],
                              a=OT[hb:hb + 64, 0:512], bb=TMP2[hb:hb + 64, 0:512]:
                              v.tensor_add(o, a, bb), [mB], inc="v")
            otok = kb.dma("g", AO[hp * 128:hp * 128 + 128, b * 512:b * 512 + 512], OT,
                          [kb.tok("v")])
            bk[("ot", par)] = otok
            ufp[par] = [kb.tok("t"), kb.tok("s"), kb.tok("v"), otok]
    kb.fence()
    bout = ctx.CF[:, 2048:2056]
    lb2 = kb.dma("q", bout, io["L3_attn_b_out_c"])
    kb.fence()
    pl = PostLN(ctx, X, XB, bias_col=lambda d0, dw: bout[0:dw, d0 // 128:d0 // 128 + 1])

    def xload(ci, t0, tw, par):
        o = 8192 + par * 4096 + ci * 512
        dst = ctx.BIG[:, o:o + tw]
        return AO[ci * 128:ci * 128 + 128, t0:t0 + tw], dst, []

    mmB(ctx, "l3o", io["L3_attn_W_out"], D, D, xload, pl.evac, woff=0, xoff=8192,
        xbufs=2)
    kb.fence()
    pl.apply()


def emit_L0_attn(ctx, io, X, XB, H):
    nc, kb = ctx.nc, ctx.kb
    QS, KS, VS, AO = io["_QS"], io["_KS"], io["_VS"], io["_AO"]
    Q2S, K2S = io["_Q2S"], io["_K2S"]
    ind16 = ctx.CF[:, 2112:2112 + 128]
    sel = ctx.FW[0:16, 1024:1024 + 2048]
    li = kb.dma("q", ind16, io["_ind16"])
    ls = kb.dma("q", sel, io["_sel16"])
    t_t1q = kb.dma("g", ctx.CAR[:, 0:4096], io["_r0T1q"])
    t_t2q = kb.dma("g", ctx.CAR[:, 4096:8192], io["_r0T2q"])
    t_t1k = kb.dma("g", ctx.CAR[:, 8192:12288], io["_r0T1k"])
    t_t2k = kb.dma("g", ctx.CAR[:, 12288:16384], io["_r0T2k"])
    p1 = kb.dma("g", ctx.ident[:, :] if False else ctx.CAR[:, 0:0], io["_r0P1"]) \
        if False else None
    P1b = ctx.BIG[:, 48640:48768]
    P2b = ctx.BIG[:, 48768:48896]
    p1 = kb.dma("g", P1b, io["_r0P1"])
    p2 = kb.dma("g", P2b, io["_r0P2"])
    selkv = ctx.CF[0:2, 2240:2240 + 16]
    lkv = kb.dma("q", selkv, io["_selkv"])
    lrow = ctx.CF[0:1, 2260:2260 + 64]
    lrow2 = ctx.CF[0:1, 2324:2324 + 64]
    nlam = ctx.CF[:, 2390:2391]
    ldeps = [kb.dma("q", lrow, io["L0_attn_lq1"].rearrange("(a n) -> a n", a=1)),
             kb.dma("q", lrow2, io["L0_attn_lk1"].rearrange("(a n) -> a n", a=1))]
    v1 = kb.on("v", lambda v: v.tensor_mul(lrow, lrow, lrow2), ldeps, inc="v")
    v2 = kb.on("v", lambda v: v.reduce_sum(ctx.CF[0:1, 2388:2389], lrow, axis=AX),
               [v1], inc="v")
    ldeps2 = [kb.dma("q", lrow, io["L0_attn_lq2"].rearrange("(a n) -> a n", a=1), [v2]),
              kb.dma("q", lrow2, io["L0_attn_lk2"].rearrange("(a n) -> a n", a=1))]
    v3 = kb.on("v", lambda v: v.tensor_mul(lrow, lrow, lrow2), ldeps2, inc="v")
    v4 = kb.on("v", lambda v: v.reduce_sum(ctx.CF[0:1, 2389:2390], lrow, axis=AX),
               [v3], inc="v")
    e1 = kb.on("s", lambda sE: sE.activation(ctx.CF[0:1, 2388:2389],
               ctx.CF[0:1, 2388:2389], AF.Exp), [v2], inc="s")
    e2 = kb.on("s", lambda sE: sE.activation(ctx.CF[0:1, 2389:2390],
               ctx.CF[0:1, 2389:2390], AF.Exp), [v4, e1], inc="s")
    v5 = kb.on("v", lambda v: v.tensor_sub(ctx.CF[0:1, 2388:2389],
               ctx.CF[0:1, 2388:2389], ctx.CF[0:1, 2389:2390]), [e2], inc="v")
    v6 = kb.on("v", lambda v: v.tensor_scalar(ctx.CF[0:1, 2388:2389],
               ctx.CF[0:1, 2388:2389], 0.8, 0.9, ALU.add, ALU.min), [v5], inc="v")
    v7 = kb.on("v", lambda v: v.tensor_scalar(ctx.CF[0:1, 2388:2389],
               ctx.CF[0:1, 2388:2389], 0.1, -1.0, ALU.max, ALU.mult), [v6], inc="v")
    bl = kb.on("t", lambda te: te.matmul(ctx.ps[6][:, 0:1], ctx.ones_rowf[0:1, 0:128],
               ctx.CF[0:1, 2388:2389], start=True, stop=True), [v7], inc="t")
    bl2 = kb.on("s", lambda sE: sE.activation(nlam, ctx.ps[6][:, 0:1], AF.Identity),
                [bl], inc="s")
    bout_c = ctx.CF[:, 2392:2400]
    lb2 = kb.dma("q", bout_c, io["L0_attn_bout_c"])
    kb.fence()
    w1 = load_w_group(ctx, io["L0_attn_Wq1"], D, 0, 1024, 0)
    w2 = load_w_group(ctx, io["L0_attn_Wq2"], D, 0, 1024, 8192)
    w3 = load_w_group(ctx, io["L0_attn_Wk1"], D, 0, 128, 16384)
    w4 = load_w_group(ctx, io["L0_attn_Wk2"], D, 0, 128, 17408)
    w5 = load_w_group(ctx, io["L0_attn_Wv"], D, 0, 128, 18432)
    wtoks = w1 + w2 + w3 + w4 + w5
    xoff, stg = 19456, 29184
    xfree = {}
    stg_free = {}
    mxfree = {}
    for ti in range(NTT):
        t0 = ti * TT
        par = ti % 2
        xo = xoff + par * 4096
        src = XB[:, 0:NCH * XBW].rearrange("p (c t) -> p c t", t=XBW)[:, :, XBO + t0:XBO + t0 + TT]
        dst = ctx.BIG[:, xo:xo + 4096].rearrange("p (c t) -> p c t", t=TT)
        xt = kb.dma("q", dst, src, [xfree.get(par)])
        xa = lambda ci, xo=xo: ctx.BIG[:, xo + ci * TT:xo + ci * TT + TT]
        nrm = {}
        for sysi, (woff, dsts, tq1, tq2, nchunks) in enumerate([
                (0, QS, 0, 4096, 8), (8192, Q2S, 0, 4096, 8),
                (16384, KS, 8192, 12288, 1), (17408, K2S, 8192, 12288, 1)]):
            isq = sysi < 2
            nq = ctx.spool.take() if isq else ctx.pool.take()
            for dc in range(nchunks):
                slot = ctx.pool.take()
                ps = slot[0]
                for ci in range(8):
                    mdeps = (wtoks + [xt, slot[1]]) if ci == 0 else ()
                    tok = kb.on("t", lambda te, o=ps[:, 0:TT],
                                l=slab_lhsT(ctx, woff, nchunks * 128, ci, dc * 128, 128, 128),
                                r=xa(ci), st=(ci == 0), sp=(ci == 7):
                                te.matmul(o, l, r, start=st, stop=sp),
                                mdeps, inc="t" if ci == 7 else None)
                sA = ctx.BIG[:, stg + 0 * 512:stg + 0 * 512 + 512]
                sB = ctx.BIG[:, stg + 1024:stg + 1024 + 512]
                sC = ctx.BIG[:, stg + 1536:stg + 1536 + 512]
                e0 = kb.on("s", lambda sE, o=sA, i=ps[:, 0:TT]:
                           sE.activation(o, i, AF.Identity),
                           [tok] + (stg_free.get(0) or []), inc="s")
                slot[1] = e0
                pslot = ctx.pool.take()
                pslot2 = ctx.pool.take()
                m1 = kb.on("t", lambda te, o=pslot[0][:, 0:TT], l=P1b, r=sA:
                           te.matmul(o, l, r, start=True, stop=True),
                           [e0, p1, p2, pslot[1]], inc="t")
                m2 = kb.on("t", lambda te, o=pslot2[0][:, 0:TT], l=P2b, r=sA:
                           te.matmul(o, l, r, start=True, stop=True),
                           [pslot2[1]], inc="t")
                tq = (tq1 if isq or True else 0)
                T1 = ctx.CAR[:, (0 if isq else 8192) + t0:(0 if isq else 8192) + t0 + TT]
                T2 = ctx.CAR[:, (4096 if isq else 12288) + t0:(4096 if isq else 12288) + t0 + TT]
                d1 = kb.on("v", lambda v, o=sB, a=pslot[0][:, 0:TT], b=T1:
                           v.tensor_mul(o, a, b),
                           [m1, t_t1q, t_t1k] + (stg_free.get(1) or []), inc="v")
                d2 = kb.on("v", lambda v, o=sC, a=pslot2[0][:, 0:TT], b=T2:
                           v.tensor_mul(o, a, b),
                           [m2, t_t2q, t_t2k] + (stg_free.get(2) or []), inc="v")
                d3 = kb.on("v", lambda v, o=sB, a=sB, b=sC: v.tensor_add(o, a, b),
                           [d1, d2], inc="v")
                pslot[1] = d1
                pslot2[1] = d2
                dd = kb.dma("g", dsts[dc * 128:dc * 128 + 128, t0:t0 + TT], sB, [d3])
                sq = ctx.FW[:, 512:1024]
                e2_ = kb.on("s", lambda sE, o=sq, i=sB: sE.activation(o, i, AF.Square),
                            [d3, kb.tok("s")] + (stg_free.get(3) or []), inc="s")
                first = (dc == 0)
                n1 = kb.on("t", lambda te, o=nq[0][0:16, 0:TT],
                           l=ind16[:, (dc if isq else 0) * 16:(dc if isq else 0) * 16 + 16],
                           r=sq, st=first, sp=(dc == nchunks - 1):
                           te.matmul(o, l, r, start=st, stop=sp),
                           [e2_, li] + ([nq[1]] if first else []), inc="t")
                stg_free[0] = [m1, m2]
                stg_free[1] = [d3]
                stg_free[2] = [d3]
                stg_free[3] = [n1]
            mxo = 3184 + sysi * 32 + ti * 4
            rows = 16 if isq else 2
            rd = kb.on("v", lambda v, o=ctx.FW[0:rows, mxo:mxo + 4],
                       i=nq[0][0:rows, 0:TT].rearrange("h (b w) -> h b w", w=128),
                       deps=None: v.reduce_max(o, i, axis=AX),
                       [n1, mxfree.get(mxo)], inc="v")
            nq[1] = rd
            mxfree[mxo] = rd
        for sub in range(4):
            slot = ctx.pool.take()
            ps = slot[0]
            for ci in range(8):
                mdeps = (wtoks + [xt, slot[1]]) if ci == 0 else ()
                tok = kb.on("t", lambda te, o=ps[:, 0:128],
                            l=ctx.BIG[:, xo + ci * TT + sub * 128:xo + ci * TT + sub * 128 + 128],
                            r=slab_lhsT(ctx, 18432, 128, ci, 0, 128, 128),
                            st=(ci == 0), sp=(ci == 7):
                            te.matmul(o, l, r, start=st, stop=sp),
                            mdeps, inc="t" if ci == 7 else None)
            stgb = ctx.BIG[:, stg + 2048 + (sub % 2) * 512:stg + 2048 + (sub % 2) * 512 + 128]
            ev = kb.on("s", lambda sE, o=stgb, i=ps[:, 0:128]:
                       sE.activation(o, i, AF.Identity),
                       [tok] + (stg_free.get(10 + sub % 2) or []), inc="s")
            dv = kb.dma("g", VS[t0 + sub * 128:t0 + sub * 128 + 128, 0:128], stgb, [ev])
            stg_free[10 + sub % 2] = [dv]
            slot[1] = ev
        xfree[par] = kb.tok("t")
    kb.fence()
    sq1 = kb.on("s", lambda sE: sE.activation(ctx.FW[0:16, 3184:3184 + 128],
                ctx.FW[0:16, 3184:3184 + 128], AF.Sqrt), [], inc="s")
    mw = {}
    for sysi in range(4):
        rows = 16 if sysi < 2 else 2
        base = 3184 + sysi * 32
        o = 3312 + sysi * 16
        tokm = kb.on("v", lambda v, b=base, oo=o, r=rows:
                     v.tensor_max(ctx.FW[0:r, oo:oo + 16], ctx.FW[0:r, b:b + 16],
                                  ctx.FW[0:r, b + 1:b + 17]), [sq1], inc="v")
        mw[sysi] = tokm
    mb = {}
    for sysn, (qi, ki) in enumerate([(0, 2), (1, 3)]):
        pslot = ctx.spool.take()
        t1 = kb.on("t", lambda te, o=pslot[0][0:16, 0:16], l=selkv,
                   r=ctx.FW[0:2, 3312 + ki * 16:3312 + ki * 16 + 16]:
                   te.matmul(o, l, r, start=True, stop=True),
                   [mw[ki], lkv, pslot[1]], inc="t")
        dst = ctx.FW[0:16, 3376 + sysn * 16:3376 + sysn * 16 + 16]
        t2 = kb.on("v", lambda v, o=dst, a=ctx.FW[0:16, 3312 + qi * 16:3312 + qi * 16 + 16],
                   b=pslot[0][0:16, 0:16]: v.tensor_mul(o, a, b), [t1, mw[qi]], inc="v")
        pslot[1] = t2
        mb[sysn] = (dst, t2)
    kb.fence()
    bk = {}
    ufp = {}
    for w in range(16):
        for hp in range(8):
            par = (w * 8 + hp) % 2
            base = par * 8192
            QT1 = ctx.BIG[:, base:base + 256]
            QT2 = ctx.BIG[:, base + 256:base + 512]
            KT1 = ctx.BIG[:, base + 512:base + 768]
            KT2 = ctx.BIG[:, base + 768:base + 1024]
            VT = ctx.BIG[:, base + 1024:base + 1280]
            PT = ctx.BIG[:, base + 1280:base + 1536]
            OT = ctx.BIG[:, base + 1536:base + 1792]
            TMP = ctx.BIG[:, base + 1792:base + 2048]
            TMP2 = ctx.BIG[:, base + 2048:base + 2304]
            uf = ufp.get(par) or []
            c0 = 128 * w
            lq1 = kb.dma("q", QT1, QS[hp * 128:hp * 128 + 128, c0:c0 + 256], uf)
            lq2 = kb.dma("q", QT2, Q2S[hp * 128:hp * 128 + 128, c0:c0 + 256], [])
            lk1 = kb.dma("q", KT1, KS[0:128, c0:c0 + 256], [])
            lk2 = kb.dma("q", KT2, K2S[0:128, c0:c0 + 256], [])
            lv = kb.dma("q", VT.rearrange("p (sb c) -> p sb c", c=128),
                        VS[c0:c0 + 256, 0:128].rearrange("(sb p) c -> p sb c", p=128), [])
            loads = [lq1, lq2, lk1, lk2, lv]
            for h2 in range(2):
                hb = h2 * 64
                kvb = ((2 * hp + h2) % 2) * 64
                selh = sel[0:16, (2 * hp + h2) * 128:(2 * hp + h2) * 128 + 128]
                nm1 = ctx.CF[:, 2900 + par * 2:2901 + par * 2]
                nm2 = ctx.CF[:, 2901 + par * 2:2902 + par * 2]
                bank6 = [ctx.ps[6], bk.get(6)]
                c1 = negm_col(ctx, selh, mb[0][0][0:16, w:w + 1], nm1, bank6,
                              [mb[0][1], ls] + uf + (bk.get(("nm", par)) or []))
                c2 = negm_col(ctx, selh, mb[1][0][0:16, w:w + 1], nm2, bank6,
                              [mb[1][1]])
                bk[6] = bank6[1]
                fin = []
                for sidx in range(2):
                    QT = QT1 if sidx == 0 else QT2
                    KT = KT1 if sidx == 0 else KT2
                    psL = ctx.ps[4 + sidx]
                    psO = ctx.ps[2 + sidx]
                    lm = pv = None
                    for kc in range(2):
                        psS = ctx.ps[kc % 2]
                        mm = kb.on("t", lambda te, o=psS[:, 0:256],
                                   l=KT[kvb:kvb + 64, kc * 128:kc * 128 + 128],
                                   r=QT[hb:hb + 64, 0:256], tp=(0, 0):
                                   te.matmul(o, l, r, start=True, stop=True),
                                   loads + [c1, c2, bk.get(kc % 2)], inc="t")
                        ex = kb.on("s", lambda sE, o=PT[:, 0:256], i=psS[:, 0:256],
                                   bb=(nm1 if sidx == 0 else nm2):
                                   sE.activation(o, i, AF.Exp, bias=bb),
                                   [mm] + (bk.get("pt") or []), inc="s")
                        bk[kc % 2] = ex
                        lm = kb.on("t", lambda te, o=psL[0:1, 0:256],
                                   l=ctx.ones_colb[:, 0:1], r=PT[:, 0:256],
                                   st=(kc == 0), sp=(kc == 1):
                                   te.matmul(o, l, r, start=st, stop=sp),
                                   [ex] + ([bk.get(4 + sidx)] if kc == 0 else []), inc="t")
                        pv = kb.on("t", lambda te, o=psO[hb:hb + 64, 0:256],
                                   l=VT[:, kc * 128 + kvb:kc * 128 + kvb + 64],
                                   r=PT[:, 0:256], tp=(0, hb),
                                   st=(kc == 0), sp=(kc == 1):
                                   te.matmul(o, l, r, start=st, stop=sp, tile_position=tp),
                                   [ex] + ([bk.get(2 + sidx)] if kc == 0 else []), inc="t")
                        bk["pt"] = [lm, pv]
                        bk[("nm", par)] = [ex]
                    fin.append((lm, pv))
                for sidx in range(2):
                    lmf, pvf = fin[sidx]
                    lr = ctx.FW2[0:1, sidx * 512:sidx * 512 + 256]
                    a1 = kb.on("v", lambda v, o=lr, i=ctx.ps[4 + sidx][0:1, 0:256]:
                               v.tensor_scalar_add(o, i, 1e-30),
                               [lmf, bk.get(("lr", sidx))], inc="v")
                    a2 = kb.on("v", lambda v, o=lr, i=lr: v.reciprocal(o, i), [a1], inc="v")
                    bk[4 + sidx] = a2
                    bc = kb.on("t", lambda te, o=ctx.ps[6][hb:hb + 64, 0:256],
                               l=ctx.ones_rowf[0:1, 0:64], r=lr, tp=(0, hb):
                               te.matmul(o, l, r, start=True, stop=True, tile_position=tp),
                               [a2, bk.get(6)], inc="t")
                    bk[("lr", sidx)] = bc
                    dstR = (TMP if sidx == 0 else TMP2)
                    cp = kb.on("s", lambda sE, o=dstR[hb:hb + 64, 0:256],
                               i=ctx.ps[6][hb:hb + 64, 0:256]:
                               sE.activation(o, i, AF.Identity),
                               [bc, bk.get(("tmp", par, sidx))], inc="s")
                    bk[6] = cp
                    if sidx == 0:
                        mA = kb.on("v", lambda v, o=OT[hb:hb + 64, 0:256],
                                   a=ctx.ps[2][hb:hb + 64, 0:256], bb=TMP[hb:hb + 64, 0:256]:
                                   v.tensor_mul(o, a, bb),
                                   [cp, pvf, bk.get(("ot", par))], inc="v")
                        bk[2] = mA
                        bk[("tmp", par, 0)] = mA
                    else:
                        mB = kb.on("v", lambda v, o=TMP2[hb:hb + 64, 0:256],
                                   a=ctx.ps[3][hb:hb + 64, 0:256], bb=TMP2[hb:hb + 64, 0:256]:
                                   v.tensor_mul(o, a, bb), [cp, pvf], inc="v")
                        bk[3] = mB
                        bk[("tmp", par, 1)] = mB
                        kb.on("v", lambda v, o=OT[hb:hb + 64, 0:256],
                              a=TMP2[hb:hb + 64, 0:256], sc=nlam[hb:hb + 64, 0:1],
                              b=OT[hb:hb + 64, 0:256]:
                              v.scalar_tensor_tensor(o, a, sc, b, ALU.mult, ALU.add),
                              [mB, bl2], inc="v")
            otok = kb.dma("g", AO[hp * 128:hp * 128 + 128, 256 * w:256 * w + 256], OT,
                          [kb.tok("v")])
            bk[("ot", par)] = otok
            ufp[par] = [kb.tok("t"), kb.tok("s"), kb.tok("v"), otok]
    kb.fence()
    pl = PostLN(ctx, X, XB, bias_col=lambda d0, dw: bout_c[0:dw, d0 // 128:d0 // 128 + 1])

    def xload(ci, t0, tw, par):
        o = 8192 + par * 4096 + ci * 512
        dst = ctx.BIG[:, o:o + tw]
        return AO[ci * 128:ci * 128 + 128, t0:t0 + tw], dst, []

    mmB(ctx, "l0o", io["L0_attn_Wout"], D, D, xload, pl.evac, woff=0, xoff=8192,
        xbufs=2)
    kb.fence()
    pl.apply()


def emit_L1_attn(ctx, io, X, XB, H):
    nc, kb = ctx.nc, ctx.kb
    QmS, QpS, QRS = io["_QS"], io["_Q2S"], io["_QRS"]
    KpS, KmS, KRS = io["_KS"], io["_K2S"], io["_KRS"]
    VS, AO = io["_VS"], io["_AO"]
    ind16 = ctx.CF[:, 2112:2112 + 128]
    sel = ctx.FW[0:16, 1024:1024 + 2048]
    li = kb.dma("q", ind16, io["_ind16"])
    ls = kb.dma("q", sel, io["_sel16"])
    dt_ = kb.dma("q", ctx.CF[:, 0:2048], io["_l1D"])
    kb.fence()

    wq = load_w_group(ctx, io["L1_attn_W_q"], D, 0, 1024, 0)
    xfree, stg_free, nfree = {}, {}, {}
    fb_free = {}
    for ti in range(NTT):
        t0 = ti * TT
        par = ti % 2
        xo = 8192 + par * 4096
        src = XB[:, 0:NCH * XBW].rearrange("p (c t) -> p c t", t=XBW)[:, :, XBO + t0:XBO + t0 + TT]
        dst = ctx.BIG[:, xo:xo + 4096].rearrange("p (c t) -> p c t", t=TT)
        xt = kb.dma("q", dst, src, [xfree.get(par)])
        xa = lambda ci, xo=xo: ctx.BIG[:, xo + ci * TT:xo + ci * TT + TT]
        frm = ctx.spool.take()
        fr1 = kb.dma("q", ctx.FW[0:1, 0:TT], io["_l1fqm"][0:1, t0:t0 + TT],
                     [fb_free.get("m")])
        b1 = kb.on("t", lambda te, o=frm[0][:, 0:TT], l=ctx.ones_rowf[0:1, 0:128],
                   r=ctx.FW[0:1, 0:TT]: te.matmul(o, l, r, start=True, stop=True),
                   [fr1, frm[1]], inc="t")
        frp = ctx.spool.take()
        fr2 = kb.dma("q", ctx.FW[0:1, 512:512 + TT], io["_l1fqp"][0:1, t0:t0 + TT],
                     [fb_free.get("p")])
        b2 = kb.on("t", lambda te, o=frp[0][:, 0:TT], l=ctx.ones_rowf[0:1, 0:128],
                   r=ctx.FW[0:1, 512:512 + TT]: te.matmul(o, l, r, start=True, stop=True),
                   [fr2, frp[1]], inc="t")
        fb_free["m"] = b1
        fb_free["p"] = b2
        nslot = ctx.pool.take()
        for dc in range(8):
            slot = ctx.pool.take()
            ps = slot[0]
            for ci in range(8):
                mdeps = (wq + [xt, slot[1]]) if ci == 0 else ()
                tok = kb.on("t", lambda te, o=ps[:, 0:TT],
                            l=slab_lhsT(ctx, 0, 1024, ci, dc * 128, 128, 128),
                            r=xa(ci), st=(ci == 0), sp=(ci == 7):
                            te.matmul(o, l, r, start=st, stop=sp),
                            mdeps, inc="t" if ci == 7 else None)
            sR = ctx.BIG[:, 16384 + 0 * 512:16384 + 512]
            sM = ctx.BIG[:, 16896:16896 + 512]
            sP = ctx.BIG[:, 17408:17408 + 512]
            e0 = kb.on("s", lambda sE, o=sR, i=ps[:, 0:TT]:
                       sE.activation(o, i, AF.Identity),
                       [tok] + (stg_free.get(0) or []), inc="s")
            slot[1] = e0
            d0 = kb.dma("g", QRS[dc * 128:dc * 128 + 128, t0:t0 + TT], sR, [e0])
            v1 = kb.on("v", lambda v, o=sM, a=sR, b=frm[0][:, 0:TT]:
                       v.tensor_mul(o, a, b), [e0, b1] + (stg_free.get(1) or []), inc="v")
            d1 = kb.dma("g", QmS[dc * 128:dc * 128 + 128, t0:t0 + TT], sM, [v1])
            v2 = kb.on("v", lambda v, o=sP, a=sR, b=frp[0][:, 0:TT]:
                       v.tensor_mul(o, a, b), [e0, b2] + (stg_free.get(2) or []), inc="v")
            d2 = kb.dma("g", QpS[dc * 128:dc * 128 + 128, t0:t0 + TT], sP, [v2])
            sq = ctx.FW[:, 3584:3584 + 512]
            e2 = kb.on("s", lambda sE, o=sq, i=sR: sE.activation(o, i, AF.Square),
                       [e0] + (stg_free.get(3) or []), inc="s")
            n1 = kb.on("t", lambda te, o=nslot[0][0:16, 0:TT],
                       l=ind16[:, dc * 16:dc * 16 + 16], r=sq,
                       st=(dc == 0), sp=(dc == 7): te.matmul(o, l, r, start=(dc == 0), stop=(dc == 7)),
                       [e2, li] + ([nslot[1]] if dc == 0 else []), inc="t")
            stg_free[0] = [d0, v1, v2, e2]
            stg_free[1] = [d1]
            stg_free[2] = [d2]
            stg_free[3] = [n1]
        rd = kb.on("v", lambda v, o=ctx.FW[0:16, 3200 + ti:3200 + ti + 1],
                   i=nslot[0][0:16, 0:TT]: v.reduce_max(o, i, axis=AX),
                   [n1, nfree.get(ti)], inc="v")
        nslot[1] = rd
        nfree[ti] = rd
        frm[1] = v1
        frp[1] = v2
        xfree[par] = kb.tok("t")
    kb.fence()
    wk = load_w_group(ctx, io["L1_attn_W_k"], D, 0, 1024, 0)
    wv = load_w_group(ctx, io["L1_attn_W_v"], D, 0, 1024, 8192)
    ad = kb.on("v", lambda v: v.tensor_add(ctx.BIG[:, 0:8192], ctx.BIG[:, 0:8192],
               ctx.BIG[:, 8192:16384]), wk + wv, inc="v")
    kb.fence()
    for ti in range(NTT):
        t0 = ti * TT
        par = ti % 2
        xo = 8192 + par * 4096
        src = XB[:, 0:NCH * XBW].rearrange("p (c t) -> p c t", t=XBW)[:, :, XBO + t0:XBO + t0 + TT]
        dst = ctx.BIG[:, xo:xo + 4096].rearrange("p (c t) -> p c t", t=TT)
        xt = kb.dma("q", dst, src, [xfree.get(10 + par)])
        xa = lambda ci, xo=xo: ctx.BIG[:, xo + ci * TT:xo + ci * TT + TT]
        for dc in range(8):
            slot = ctx.pool.take()
            ps = slot[0]
            for ci in range(8):
                mdeps = ([ad, xt, slot[1]]) if ci == 0 else ()
                tok = kb.on("t", lambda te, o=ps[:, 0:TT],
                            l=slab_lhsT(ctx, 0, 1024, ci, dc * 128, 128, 128),
                            r=xa(ci), st=(ci == 0), sp=(ci == 7):
                            te.matmul(o, l, r, start=st, stop=sp),
                            mdeps, inc="t" if ci == 7 else None)
            kvdst = ctx.BIG[:, 16384 + dc * 4096 + t0:16384 + dc * 4096 + t0 + TT]
            e0 = kb.on("s", lambda sE, o=kvdst, i=ps[:, 0:TT]:
                       sE.activation(o, i, AF.Identity), [tok], inc="s")
            slot[1] = e0
        xfree[10 + par] = kb.tok("t")
    kb.fence()
    wdn = load_w_group_car(ctx, io["L1_attn_W_down_kv"], D, 0, 256, 0)
    for ti in range(NTT):
        t0 = ti * TT
        for dc in range(2):
            slot = ctx.pool.take()
            ps = slot[0]
            for ci in range(8):
                mdeps = (wdn + [slot[1]]) if ci == 0 else ()
                tok = kb.on("t", lambda te, o=ps[:, 0:TT],
                            l=ctx.CAR[0:128, ci * 256 + dc * 128:ci * 256 + dc * 128 + 128],
                            r=ctx.BIG[:, 16384 + ci * 4096 + t0:16384 + ci * 4096 + t0 + TT],
                            st=(ci == 0), sp=(ci == 7):
                            te.matmul(o, l, r, start=st, stop=sp),
                            mdeps, inc="t" if ci == 7 else None)
            e0 = kb.on("s", lambda sE, o=ctx.CAR[:, 2048 + dc * 4096 + t0:2048 + dc * 4096 + t0 + TT],
                       i=ps[:, 0:TT]: sE.activation(o, i, AF.Identity), [tok], inc="s")
            slot[1] = e0
    kb.fence()
    wuk = load_w_group(ctx, io["L1_attn_W_up_k"], 256, 0, 1024, 0)
    wuv = load_w_group(ctx, io["L1_attn_W_up_v"], 256, 0, 1024, 2048)
    lat = lambda ci, t0: ctx.CAR[0:128, 2048 + ci * 4096 + t0:2048 + ci * 4096 + t0 + TT]
    for ti in range(NTT):
        t0 = ti * TT
        frm = ctx.spool.take()
        fr1 = kb.dma("q", ctx.FW[0:1, 0:TT], io["_l1fkm"][0:1, t0:t0 + TT],
                     [fb_free.get("m")])
        b1 = kb.on("t", lambda te, o=frm[0][:, 0:TT], l=ctx.ones_rowf[0:1, 0:128],
                   r=ctx.FW[0:1, 0:TT]: te.matmul(o, l, r, start=True, stop=True),
                   [fr1, frm[1]], inc="t")
        frp = ctx.spool.take()
        fr2 = kb.dma("q", ctx.FW[0:1, 512:512 + TT], io["_l1fkp"][0:1, t0:t0 + TT],
                     [fb_free.get("p")])
        b2 = kb.on("t", lambda te, o=frp[0][:, 0:TT], l=ctx.ones_rowf[0:1, 0:128],
                   r=ctx.FW[0:1, 512:512 + TT]: te.matmul(o, l, r, start=True, stop=True),
                   [fr2, frp[1]], inc="t")
        fb_free["m"] = b1
        fb_free["p"] = b2
        nslot = ctx.pool.take()
        for dc in range(8):
            slot = ctx.pool.take()
            ps = slot[0]
            for ci in range(2):
                mdeps = (wuk + [slot[1]]) if ci == 0 else ()
                tok = kb.on("t", lambda te, o=ps[:, 0:TT],
                            l=slab_lhsT(ctx, 0, 1024, ci, dc * 128, 128, 128),
                            r=lat(ci, t0), st=(ci == 0), sp=(ci == 1):
                            te.matmul(o, l, r, start=st, stop=sp),
                            mdeps, inc="t" if ci == 1 else None)
            sR = ctx.BIG[:, 4096:4096 + 512]
            sM = ctx.BIG[:, 4608:4608 + 512]
            sP = ctx.BIG[:, 5120:5120 + 512]
            e0 = kb.on("s", lambda sE, o=sR, i=ps[:, 0:TT]:
                       sE.activation(o, i, AF.Identity),
                       [tok] + (stg_free.get(0) or []), inc="s")
            slot[1] = e0
            d0 = kb.dma("g", KRS[dc * 128:dc * 128 + 128, t0:t0 + TT], sR, [e0])
            v1 = kb.on("v", lambda v, o=sM, a=sR, b=frm[0][:, 0:TT]:
                       v.tensor_mul(o, a, b), [e0, b1] + (stg_free.get(1) or []), inc="v")
            d1 = kb.dma("g", KmS[dc * 128:dc * 128 + 128, t0:t0 + TT], sM, [v1])
            v2 = kb.on("v", lambda v, o=sP, a=sR, b=frp[0][:, 0:TT]:
                       v.tensor_mul(o, a, b), [e0, b2] + (stg_free.get(2) or []), inc="v")
            d2 = kb.dma("g", KpS[dc * 128:dc * 128 + 128, t0:t0 + TT], sP, [v2])
            sq = ctx.FW[:, 3584:3584 + 512]
            e2 = kb.on("s", lambda sE, o=sq, i=sR: sE.activation(o, i, AF.Square),
                       [e0] + (stg_free.get(3) or []), inc="s")
            n1 = kb.on("t", lambda te, o=nslot[0][0:16, 0:TT],
                       l=ind16[:, dc * 16:dc * 16 + 16], r=sq,
                       st=(dc == 0), sp=(dc == 7): te.matmul(o, l, r, start=(dc == 0), stop=(dc == 7)),
                       [e2] + ([nslot[1]] if dc == 0 else []), inc="t")
            stg_free[0] = [d0, v1, v2, e2]
            stg_free[1] = [d1]
            stg_free[2] = [d2]
            stg_free[3] = [n1]
            frm[1] = v1
            frp[1] = v2
        rd = kb.on("v", lambda v, o=ctx.FW[0:16, 3216 + ti:3216 + ti + 1],
                   i=nslot[0][0:16, 0:TT]: v.reduce_max(o, i, axis=AX),
                   [n1, nfree.get(20 + ti)], inc="v")
        nslot[1] = rd
        nfree[20 + ti] = rd
        for sub in range(4):
            stgb = ctx.BIG[:, 5632 + (sub % 2) * 1024:5632 + (sub % 2) * 1024 + 1024]
            evs = []
            for sl in range(2):
                slot = ctx.pool.take()
                ps = slot[0]
                for ci in range(2):
                    mdeps = (wuv + [slot[1]]) if ci == 0 else ()
                    tok = kb.on("t", lambda te, o=ps[:, 0:512],
                                l=ctx.CAR[0:128, 2048 + ci * 4096 + t0 + sub * 128:
                                          2048 + ci * 4096 + t0 + sub * 128 + 128],
                                r=slab_lhsT(ctx, 2048, 1024, ci, sl * 512, 512, 128),
                                st=(ci == 0), sp=(ci == 1):
                                te.matmul(o, l, r, start=st, stop=sp),
                                mdeps, inc="t" if ci == 1 else None)
                ev = kb.on("s", lambda sE, o=stgb[:, sl * 512:sl * 512 + 512], i=ps[:, 0:512]:
                           sE.activation(o, i, AF.Identity),
                           [tok] + (stg_free.get(10 + sub % 2) or []), inc="s")
                evs.append(ev)
                slot[1] = ev
            dv = kb.dma("g", VS[t0 + sub * 128:t0 + sub * 128 + 128, 0:1024], stgb, evs)
            stg_free[10 + sub % 2] = [dv]
    kb.fence()
    s1 = kb.on("s", lambda sE: sE.activation(ctx.FW[0:16, 3200:3208],
               ctx.FW[0:16, 3200:3208], AF.Sqrt), [], inc="s")
    s2 = kb.on("v", lambda v: v.reduce_max(ctx.FW[0:16, 3230:3231],
               ctx.FW[0:16, 3216:3224], axis=AX), [s1], inc="v")
    s3 = kb.on("s", lambda sE: sE.activation(ctx.FW[0:16, 3230:3231],
               ctx.FW[0:16, 3230:3231], AF.Sqrt), [s2], inc="s")
    mtab = ctx.FW[0:16, 3232:3240]
    s4 = kb.on("v", lambda v: v.tensor_scalar(mtab, ctx.FW[0:16, 3200:3208],
               ctx.FW[0:16, 3230:3231], 1.0 / 32.0, ALU.mult, ALU.mult), [s3], inc="v")
    kb.fence()
    bk = {}
    ufk = {}
    for hp in range(8):
        ukf = ufk.get("kv") or []
        lkp = kb.dma("q", ctx.BIG[:, 0:4096], KpS[hp * 128:hp * 128 + 128, 0:T], ukf)
        lkm = kb.dma("q", ctx.BIG[:, 4096:8192], KmS[hp * 128:hp * 128 + 128, 0:T], [])
        lkr = kb.dma("q", ctx.BIG[:, 8192:12288], KRS[hp * 128:hp * 128 + 128, 0:T], [])
        lvt = kb.dma("q", ctx.BIG[:, 12288:16384].rearrange("p (sb c) -> p sb c", c=128),
                     VS[0:T, hp * 128:hp * 128 + 128].rearrange("(sb p) c -> p sb c", p=128),
                     [])
        kloads = [lkp, lkm, lkr, lvt]
        for qt in range(8):
            qt0 = qt * 512
            par = qt % 2
            qb = 16384 + par * 2048
            QM = ctx.BIG[:, qb:qb + 512]
            QP = ctx.BIG[:, qb + 512:qb + 1024]
            QR = ctx.BIG[:, qb + 1024:qb + 1536]
            uf = ufk.get(par) or []
            l1_ = kb.dma("q", QM, QmS[hp * 128:hp * 128 + 128, qt0:qt0 + 512], uf)
            l2_ = kb.dma("q", QP, QpS[hp * 128:hp * 128 + 128, qt0:qt0 + 512], [])
            l3_ = kb.dma("q", QR, QRS[hp * 128:hp * 128 + 128, qt0:qt0 + 512], [])
            qloads = [l1_, l2_, l3_]
            PT = ctx.BIG[:, 20480 + par * 512:20480 + par * 512 + 512]
            OT = ctx.BIG[:, 21504 + par * 512:21504 + par * 512 + 512]
            for h2 in range(2):
                hb = h2 * 64
                selh = sel[0:16, (2 * hp + h2) * 128:(2 * hp + h2) * 128 + 128]
                nm1 = ctx.CF[:, 2900 + par:2901 + par]
                bank6 = [ctx.ps[6], bk.get(6)]
                c1 = negm_col(ctx, selh, mtab[0:16, qt:qt + 1], nm1, bank6,
                              [s4, ls] + uf + (bk.get(("nm", par)) or []))
                bk[6] = bank6[1]
                lm = pv = None
                for kc in range(32):
                    k0 = kc * 128
                    if k0 + 128 <= qt0:
                        reg = "lo"
                    elif k0 > qt0 + 511:
                        reg = "hi"
                    else:
                        reg = "st"
                    psS = ctx.ps[kc % 2]
                    if reg != "st":
                        koff = (0 if reg == "lo" else 4096) + k0
                        mm = kb.on("t", lambda te, o=psS[:, 0:512],
                                   l=ctx.BIG[hb:hb + 64, koff:koff + 128],
                                   r=(QM if reg == "lo" else QP)[hb:hb + 64, 0:512]:
                                   te.matmul(o, l, r, start=True, stop=True),
                                   kloads + qloads + [c1, bk.get(kc % 2)], inc="t")
                        ex = kb.on("s", lambda sE, o=PT, i=psS[:, 0:512], bb=nm1:
                                   sE.activation(o, i, AF.Exp, bias=bb),
                                   [mm] + (bk.get("pt") or []), inc="s")
                    else:
                        mm = kb.on("t", lambda te, o=psS[:, 0:512],
                                   l=ctx.BIG[hb:hb + 64, 8192 + k0:8192 + k0 + 128],
                                   r=QR[hb:hb + 64, 0:512]:
                                   te.matmul(o, l, r, start=True, stop=True),
                                   kloads + qloads + [c1, bk.get(kc % 2)], inc="t")
                        di = (k0 - qt0) // 128
                        fstg = ctx.FW[:, 3584:3584 + 512]
                        sm = kb.on("v", lambda v, o=fstg, a=psS[:, 0:512],
                                   b=ctx.CF[:, di * 512:di * 512 + 512]:
                                   v.tensor_mul(o, a, b),
                                   [mm, dt_] + (bk.get("fstg") or []), inc="v")
                        ex = kb.on("s", lambda sE, o=PT, i=fstg, bb=nm1:
                                   sE.activation(o, i, AF.Exp, bias=bb),
                                   [sm] + (bk.get("pt") or []), inc="s")
                        bk["fstg"] = [ex]
                    bk[kc % 2] = ex
                    lm = kb.on("t", lambda te, o=ctx.ps[4][0:1, 0:512],
                               l=ctx.ones_colb[:, 0:1], r=PT,
                               st=(kc == 0), sp=(kc == 31):
                               te.matmul(o, l, r, start=st, stop=sp),
                               [ex] + ([bk.get(4)] if kc == 0 else []), inc="t")
                    pv = kb.on("t", lambda te, o=ctx.ps[2][hb:hb + 64, 0:512],
                               l=ctx.BIG[:, 12288 + k0 + hb:12288 + k0 + hb + 64],
                               r=PT, tp=(0, hb), st=(kc == 0), sp=(kc == 31):
                               te.matmul(o, l, r, start=st, stop=sp, tile_position=tp),
                               [ex] + ([bk.get(2)] if kc == 0 else []), inc="t")
                    bk["pt"] = [lm, pv]
                    bk[("nm", par)] = [ex]
                lr = ctx.FW2[0:1, 0:512]
                a1 = kb.on("v", lambda v, o=lr, i=ctx.ps[4][0:1, 0:512]:
                           v.tensor_scalar_add(o, i, 1e-30),
                           [lm, bk.get("lr")], inc="v")
                a2 = kb.on("v", lambda v, o=lr, i=lr: v.reciprocal(o, i), [a1], inc="v")
                bk[4] = a2
                bc = kb.on("t", lambda te, o=ctx.ps[6][hb:hb + 64, 0:512],
                           l=ctx.ones_rowf[0:1, 0:64], r=lr, tp=(0, hb):
                           te.matmul(o, l, r, start=True, stop=True, tile_position=tp),
                           [a2, bk.get(6)], inc="t")
                bk["lr"] = bc
                TMPc = ctx.BIG[:, 22528:23040]
                cp = kb.on("s", lambda sE, o=TMPc[hb:hb + 64, 0:512],
                           i=ctx.ps[6][hb:hb + 64, 0:512]:
                           sE.activation(o, i, AF.Identity),
                           [bc, bk.get(("tmp", h2))], inc="s")
                bk[6] = cp
                mA = kb.on("v", lambda v, o=OT[hb:hb + 64, 0:512],
                           a=ctx.ps[2][hb:hb + 64, 0:512], bb=TMPc[hb:hb + 64, 0:512]:
                           v.tensor_mul(o, a, bb),
                           [cp, pv, bk.get(("ot", par))], inc="v")
                bk[2] = mA
                bk[("tmp", h2)] = mA
            otok = kb.dma("g", AO[hp * 128:hp * 128 + 128, qt0:qt0 + 512], OT,
                          [kb.tok("v")])
            bk[("ot", par)] = otok
            ufk[par] = [kb.tok("t"), kb.tok("s"), kb.tok("v"), otok]
        ufk["kv"] = [kb.tok("t"), kb.tok("s"), kb.tok("v")]
    kb.fence()
    pl = PostLN(ctx, X, XB)

    def xload(ci, t0, tw, par):
        o = 8192 + par * 4096 + ci * 512
        dst = ctx.BIG[:, o:o + tw]
        return AO[ci * 128:ci * 128 + 128, t0:t0 + tw], dst, []

    mmB(ctx, "l1o", io["L1_attn_W_o"], D, D, xload, pl.evac, woff=0, xoff=8192,
        xbufs=2)
    kb.fence()
    pl.apply()


def load_w_group_car(ctx, W_ap, din, g0, gw, slab_off, deps=()):
    kb = ctx.kb
    chs = chunks_of(din)
    nfull = sum(1 for _, sz in chs if sz == 128)
    out_toks = []
    if nfull:
        src = W_ap[0:nfull * 128, g0:g0 + gw].rearrange("(n p) w -> p n w", p=128)
        dst = ctx.CAR[:, slab_off:slab_off + nfull * gw].rearrange("p (n w) -> p n w", w=gw)
        out_toks.append(kb.dma("g", dst, src, deps))
    return out_toks


class ResAdd:

    def __init__(self, ctx, X, XB, bias_col=None):
        self.ctx, self.X, self.XB = ctx, X, XB
        self.bias_col = bias_col
        self.fw_free = {}
        self.x_free = {}

    def evac(self, dc0, dcw, t0, tw, slot, deps):
        ctx, kb = self.ctx, self.ctx.kb
        ps = slot[0]
        dc = dc0 // 128
        sl = dc % 4
        s_tile = ctx.FW[0:dcw, sl * 512:sl * 512 + tw]
        xt = ctx.FW2[0:dcw, sl * 512:sl * 512 + tw]
        ltok = kb.dma("q", xt, self.X[dc0:dc0 + dcw, t0:t0 + tw],
                      deps=[self.x_free.get(sl)])
        b = 0.0 if self.bias_col is None else self.bias_col(dc0, dcw)
        t1 = kb.on("s", lambda sE, o=s_tile, i=ps[0:dcw, 0:tw], bb=b:
                   sE.activation(o, i, AF.Identity, bias=bb),
                   list(deps) + (self.fw_free.get(sl) or []), inc="s")
        t2 = kb.on("v", lambda v, o=s_tile, a=s_tile, x=xt: v.tensor_add(o, a, x),
                   [t1, ltok], inc="v")
        self.x_free[sl] = t2
        slot[1] = t2
        d1 = kb.dma("q", self.X[dc0:dc0 + dcw, t0:t0 + tw], s_tile, [t2])
        d2 = kb.dma("g", self.XB[:, dc * XBW + XBO + t0:dc * XBW + XBO + t0 + tw],
                    s_tile, [t2])
        self.fw_free[sl] = [d1, d2]


def postln_sum(ctx, X, X0, XB):
    kb = ctx.kb
    pl = PostLN(ctx, X, XB)
    fw_free = {}
    x_free = {}
    for ti in range(NTT):
        t0 = ti * TT
        sA = sB = None
        for dc in range(NCH):
            d0 = dc * 128
            sl = dc % 4
            s_tile = ctx.FW[:, sl * 512:sl * 512 + TT]
            sq_tile = ctx.FW[:, 2048 + sl * 512:2048 + sl * 512 + TT]
            xt = ctx.FW2[:, sl * 512:sl * 512 + TT]
            l1 = kb.dma("q", s_tile, X[d0:d0 + 128, t0:t0 + TT],
                        (fw_free.get(sl) or []))
            l2 = kb.dma("q", xt, X0[d0:d0 + 128, t0:t0 + TT], [x_free.get(sl)])
            t2 = kb.on("v", lambda v, o=s_tile, a=s_tile, x=xt: v.tensor_add(o, a, x),
                       [l1, l2], inc="v")
            x_free[sl] = t2
            d1 = kb.dma("q", X[d0:d0 + 128, t0:t0 + TT], s_tile, [t2])
            t3 = kb.on("s", lambda sE, o=sq_tile, i=s_tile: sE.activation(o, i, AF.Square),
                       [t2], inc="s")
            if dc == 0:
                sA = ctx.spool.take()
                sB = ctx.spool.take()
            first = (dc == 0)
            ts1 = kb.on("t", lambda te, o=sA[0][0:1, 0:TT], l=ctx.ones_colf[:, 0:1],
                        r=s_tile, st=first, sp=(dc == NCH - 1):
                        te.matmul(o, l, r, start=st, stop=sp),
                        [t2] + ([sA[1]] if first else []), inc="t")
            ts2 = kb.on("t", lambda te, o=sB[0][0:1, 0:TT], l=ctx.ones_colf[:, 0:1],
                        r=sq_tile, st=first, sp=(dc == NCH - 1):
                        te.matmul(o, l, r, start=st, stop=sp),
                        [t3] + ([sB[1]] if first else []), inc="t")
            fw_free[sl] = [d1, ts1, ts2]
            if dc == NCH - 1:
                c1 = kb.on("v", lambda v, o=ctx.STATS[0:1, t0:t0 + TT], i=sA[0][0:1, 0:TT]:
                           v.tensor_copy(o, i), [ts1, ts2], inc="v")
                c2 = kb.on("v", lambda v, o=ctx.STATS[0:1, 4096 + t0:4096 + t0 + TT],
                           i=sB[0][0:1, 0:TT]: v.tensor_copy(o, i), [c1], inc="v")
                sA[1] = c1
                sB[1] = c2
                pl.stat_done.append(c2)
    kb.fence()
    pl.apply()


def emit_L0_ffn(ctx, io, X, XB, H):
    nc, kb = ctx.nc, ctx.kb
    QS, KS, VS, AO = io["_QS"], io["_KS"], io["_VS"], io["_AO"]
    X0 = io["_X0"]
    kb.dma("q", X0[:, :], X[:, :])
    ind16 = ctx.CF[:, 2112:2112 + 128]
    sel = ctx.FW[0:16, 1024:1024 + 2048]
    selc = ctx.CF[0:16, 1024:1024 + 1024]
    li = kb.dma("q", ind16, io["_ind16"])
    ls = kb.dma("q", sel, io["_sel16"])
    lc = kb.dma("q", selc, io["_selc"])
    Prb = ctx.BIG[:, 48896:49024]
    lpr = kb.dma("g", Prb, io["_rhP"])
    kb.fence()
    for ly in range(L0FFN_LAYERS):
        sfx = f"_{ly}"
        qs_c = ctx.CF[:, 2240:2241]
        ks_c = ctx.CF[:, 2241:2242]
        qs2_c = ctx.CF[:, 2242:2243]
        ks2_c = ctx.CF[:, 2243:2244]
        lq = kb.dma("q", qs_c, io["L0_ffn_qs_c"][ly:ly + 1, :].rearrange(
            "a (p one) -> (a p) one", one=1))
        lk = kb.dma("q", ks_c, io["L0_ffn_ks_c"][ly:ly + 1, :].rearrange(
            "a (p one) -> (a p) one", one=1))
        lq2 = kb.dma("q", qs2_c, io["L0_ffn_qs2_c"][ly:ly + 1, :].rearrange(
            "a (p one) -> (a p) one", one=1))
        lk2 = kb.dma("q", ks2_c, io["L0_ffn_ks2_c"][ly:ly + 1, :].rearrange(
            "a (p one) -> (a p) one", one=1))
        tb = []
        for j, (nm, col) in enumerate([("_xC1s", qs_c), ("_xC2s", qs2_c),
                                       ("_xC1i", ks_c), ("_xC2i", ks2_c)]):
            tt_ = kb.dma("g", ctx.CAR[:, j * 4096:(j + 1) * 4096], io[nm],
                         [kb.tok("v")])
            tb.append(kb.on("v", lambda v, o=ctx.CAR[:, j * 4096:(j + 1) * 4096],
                            cc=col: v.tensor_scalar_mul(o, o, cc[0:128, 0:1]),
                            [tt_, lq, lk, lq2, lk2], inc="v"))
        qrow = ctx.CF[0:1, 2304:2368]
        krow = ctx.CF[0:1, 2368:2432]
        lqr = kb.dma("q", qrow, io["L0_ffn_qs_r"][ly:ly + 1, 0:64])
        lkr = kb.dma("q", krow, io["L0_ffn_ks_r"][ly:ly + 1, 0:64])
        mq = ctx.CF[0:1, 2432:2433]
        mk = ctx.CF[0:1, 2433:2434]
        r1_ = kb.on("v", lambda v, o=mq, i=qrow: v.reduce_max(o, i, axis=AX,
                    apply_absolute_value=True), [lqr], inc="v")
        r2_ = kb.on("v", lambda v, o=mk, i=krow: v.reduce_max(o, i, axis=AX,
                    apply_absolute_value=True), [lkr, r1_], inc="v")
        r3_ = kb.on("v", lambda v, o=mq, b=mk: v.scalar_tensor_tensor(o, o, -8.0, b,
                    ALU.mult, ALU.mult), [r2_], inc="v")
        nmly = ctx.CF[:, 2434:2435]
        bslot = ctx.spool.take()
        bm_ = kb.on("t", lambda te, o=bslot[0][:, 0:1], l=ctx.ones_rowf[0:1, 0:128],
                    r=ctx.CF[0:1, 2432:2433]: te.matmul(o, l, r, start=True, stop=True),
                    [r3_, bslot[1]], inc="t")
        bm2_ = kb.on("s", lambda sE, o=nmly, i=bslot[0][:, 0:1]:
                     sE.activation(o, i, AF.Identity), [bm_], inc="s")
        bslot[1] = bm2_
        kb.fence()
        wq = load_w_group(ctx, io["L0_ffn_Wqkv" + sfx], D, 0, 3072, 0)
        xfree, stgf = {}, {}
        for ti in range(NTT):
            t0 = ti * TT
            par = ti % 2
            xo = 24576 + par * 4096
            src = XB[:, 0:NCH * XBW].rearrange("p (c t) -> p c t", t=XBW)
            src = bass.AP(src.tensor, src.offset, list(src.ap))
            d1 = kb.dma("q", ctx.BIG[:, xo:xo + 2048].rearrange("p (c t) -> p c t", t=TT),
                        XB[:, 0:4 * XBW].rearrange("p (c t) -> p c t", t=XBW)
                        [:, :, XBO + t0:XBO + t0 + TT], [xfree.get(par)])
            d2 = kb.dma("q", ctx.BIG[:, xo + 2048:xo + 4096].rearrange("p (c t) -> p c t", t=TT),
                        XB[:, 4 * XBW:8 * XBW].rearrange("p (c t) -> p c t", t=XBW)
                        [:, :, XBO + t0 - 1:XBO + t0 - 1 + TT], [xfree.get(par)])
            xt = [d1, d2]
            xa = lambda ci, xo=xo: ctx.BIG[:, xo + ci * TT:xo + ci * TT + TT]
            for dc in range(16):
                isq = dc < 8
                c = dc % 8
                slot = ctx.pool.take()
                ps = slot[0]
                for ci in range(8):
                    mdeps = (wq + xt + [slot[1]]) if ci == 0 else ()
                    tok = kb.on("t", lambda te, o=ps[:, 0:TT],
                                l=slab_lhsT(ctx, 0, 3072, ci, dc * 128, 128, 128),
                                r=xa(ci), st=(ci == 0), sp=(ci == 7):
                                te.matmul(o, l, r, start=st, stop=sp),
                                mdeps, inc="t" if ci == 7 else None)
                sR = ctx.BIG[:, 32768:32768 + 512]
                s2 = ctx.BIG[:, 33280:33280 + 512]
                s3 = ctx.BIG[:, 33792:33792 + 512]
                s4 = ctx.BIG[:, 34304:34304 + 512]
                sq = ctx.FW[:, 3584:3584 + 512]
                fr = ctx.FW[0:16, 0:512]
                e0 = kb.on("s", lambda sE, o=sR, i=ps[:, 0:TT]:
                           sE.activation(o, i, AF.Identity),
                           [tok] + (stgf.get(0) or []), inc="s")
                slot[1] = e0
                e1 = kb.on("s", lambda sE, o=sq, i=sR: sE.activation(o, i, AF.Square),
                           [e0] + (stgf.get(1) or []), inc="s")
                nslot = ctx.pool.take()
                n1 = kb.on("t", lambda te, o=nslot[0][0:16, 0:TT],
                           l=ind16[:, c * 16:c * 16 + 16], r=sq:
                           te.matmul(o, l, r, start=True, stop=True),
                           [e1, li, nslot[1]], inc="t")
                f1 = kb.on("s", lambda sE, o=fr, i=nslot[0][0:16, 0:TT]:
                           sE.activation(o, i, AF.Sqrt),
                           [n1] + (stgf.get(2) or []), inc="s")
                f2 = kb.on("v", lambda v, o=fr: v.tensor_scalar_max(o, o, 1e-12),
                           [f1], inc="v")
                f3 = kb.on("v", lambda v, o=fr: v.reciprocal(o, o), [f2], inc="v")
                nslot[1] = f1
                fslot = ctx.pool.take()
                fb = kb.on("t", lambda te, o=fslot[0][:, 0:TT],
                           l=selc[0:16, c * 128:c * 128 + 128], r=fr:
                           te.matmul(o, l, r, start=True, stop=True),
                           [f3, lc, fslot[1]], inc="t")
                v0 = kb.on("v", lambda v, o=s2, a=sR, b=fslot[0][:, 0:TT]:
                           v.tensor_mul(o, a, b), [fb, e0] + (stgf.get(3) or []), inc="v")
                fslot[1] = v0
                rslot = ctx.pool.take()
                rm = kb.on("t", lambda te, o=rslot[0][:, 0:TT], l=Prb, r=s2:
                           te.matmul(o, l, r, start=True, stop=True),
                           [v0, lpr, rslot[1]], inc="t")
                C1 = ctx.CAR[:, (0 if isq else 8192) + t0:(0 if isq else 8192) + t0 + TT]
                C2 = ctx.CAR[:, (4096 if isq else 12288) + t0:(4096 if isq else 12288) + t0 + TT]
                w1_ = kb.on("v", lambda v, o=s3, a=s2, b=C1: v.tensor_mul(o, a, b),
                            [v0, tb[0], tb[2]] + (stgf.get(4) or []), inc="v")
                w2_ = kb.on("v", lambda v, o=s4, a=rslot[0][:, 0:TT], b=C2:
                            v.tensor_mul(o, a, b),
                            [rm, tb[1], tb[3]] + (stgf.get(5) or []), inc="v")
                rslot[1] = w2_
                w3_ = kb.on("v", lambda v, o=s3, a=s3, b=s4: v.tensor_add(o, a, b),
                            [w1_, w2_], inc="v")
                dd = kb.dma("g", (QS if isq else KS)[c * 128:c * 128 + 128, t0:t0 + TT],
                            s3, [w3_])
                stgf[0] = [e1, v0]
                stgf[1] = [n1]
                stgf[2] = [fb]
                stgf[3] = [rm, w1_]
                stgf[4] = [w3_, dd]
                stgf[5] = [w3_]
            for sub in range(4):
                stgb = ctx.BIG[:, 34816 + (sub % 2) * 1024:34816 + (sub % 2) * 1024 + 1024]
                evs = []
                for sl in range(2):
                    slot = ctx.pool.take()
                    ps = slot[0]
                    for ci in range(8):
                        mdeps = (wq + xt + [slot[1]]) if ci == 0 else ()
                        tok = kb.on("t", lambda te, o=ps[:, 0:512],
                                    l=ctx.BIG[:, xo + ci * TT + sub * 128:xo + ci * TT + sub * 128 + 128],
                                    r=slab_lhsT(ctx, 0, 3072, ci, 2048 + sl * 512, 512, 128),
                                    st=(ci == 0), sp=(ci == 7):
                                    te.matmul(o, l, r, start=st, stop=sp),
                                    mdeps, inc="t" if ci == 7 else None)
                    ev = kb.on("s", lambda sE, o=stgb[:, sl * 512:sl * 512 + 512],
                               i=ps[:, 0:512]: sE.activation(o, i, AF.Identity),
                               [tok] + (stgf.get(10 + sub % 2) or []), inc="s")
                    evs.append(ev)
                    slot[1] = ev
                dv = kb.dma("g", VS[t0 + sub * 128:t0 + sub * 128 + 128, 0:1024], stgb, evs)
                stgf[10 + sub % 2] = [dv]
            xfree[par] = kb.tok("t")
        kb.fence()
        bk = {}
        ufp = {}
        for w in range(16):
            for hp in range(8):
                par = (w * 8 + hp) % 2
                base = par * 4096
                QT = ctx.BIG[:, base:base + 256]
                KT = ctx.BIG[:, base + 256:base + 768]
                VT = ctx.BIG[:, base + 768:base + 1280]
                PT = ctx.BIG[:, base + 1280:base + 1536]
                OT = ctx.BIG[:, base + 1536:base + 1792]
                TMP = ctx.BIG[:, base + 1792:base + 2048]
                uf = ufp.get(par) or []
                t0q = 256 * w
                lq_ = kb.dma("q", QT, QS[hp * 128:hp * 128 + 128, t0q:t0q + 256], uf)
                loads = [lq_]
                if w > 0:
                    loads.append(kb.dma("q", KT[:, 0:256],
                                 KS[hp * 128:hp * 128 + 128, t0q - 256:t0q], []))
                    loads.append(kb.dma("q", VT[:, 0:256].rearrange("p (sb c) -> p sb c", c=128),
                                 VS[t0q - 256:t0q, hp * 128:hp * 128 + 128].rearrange(
                                     "(sb p) c -> p sb c", p=128), []))
                loads.append(kb.dma("q", KT[:, 256:512],
                             KS[hp * 128:hp * 128 + 128, t0q:t0q + 256], []))
                loads.append(kb.dma("q", VT[:, 256:512].rearrange("p (sb c) -> p sb c", c=128),
                             VS[t0q:t0q + 256, hp * 128:hp * 128 + 128].rearrange(
                                 "(sb p) c -> p sb c", p=128), []))
                kcs = [0, 1, 2, 3] if w > 0 else [2, 3]
                masks = {0: (1, 0), 1: (1, 128), 2: (-1, 0), 3: (-1, -128)}
                for h2 in range(2):
                    hb = h2 * 64
                    nm1 = nmly
                    c1 = bm2_
                    lm = pv = None
                    for ki, kc in enumerate(kcs):
                        psS = ctx.ps[ki % 2]
                        mm = kb.on("t", lambda te, o=psS[:, 0:256],
                                   l=KT[hb:hb + 64, kc * 128:kc * 128 + 128],
                                   r=QT[hb:hb + 64, 0:256]:
                                   te.matmul(o, l, r, start=True, stop=True),
                                   loads + [c1, bk.get(ki % 2)], inc="t")
                        ex = kb.on("s", lambda sE, o=PT, i=psS[:, 0:256], bb=nm1:
                                   sE.activation(o, i, AF.Exp, bias=bb),
                                   [mm] + (bk.get("pt") or []), inc="s")
                        bk[ki % 2] = ex
                        cmul, bas = masks[kc]
                        mk = kb.on("g", lambda g, o=PT, cm=cmul, bb=bas:
                                   g.affine_select(o, o, pattern=[[-cm, 256]],
                                                   compare_op=ALU.is_ge, fill=0.0,
                                                   base=bb, channel_multiplier=cm),
                                   [ex], inc="g")
                        lm = kb.on("t", lambda te, o=ctx.ps[4][0:1, 0:256],
                                   l=ctx.ones_colb[:, 0:1], r=PT,
                                   st=(ki == 0), sp=(ki == len(kcs) - 1):
                                   te.matmul(o, l, r, start=st, stop=sp),
                                   [mk] + ([bk.get(4)] if ki == 0 else []), inc="t")
                        pv = kb.on("t", lambda te, o=ctx.ps[2][hb:hb + 64, 0:256],
                                   l=VT[:, kc * 128 + hb:kc * 128 + hb + 64],
                                   r=PT, tp=(0, hb), st=(ki == 0), sp=(ki == len(kcs) - 1):
                                   te.matmul(o, l, r, start=st, stop=sp, tile_position=tp),
                                   [mk] + ([bk.get(2)] if ki == 0 else []), inc="t")
                        bk["pt"] = [lm, pv]
                        bk[("nm", par)] = [ex]
                    lr = ctx.FW2[0:1, 0:256]
                    a1 = kb.on("v", lambda v, o=lr, i=ctx.ps[4][0:1, 0:256]:
                               v.tensor_scalar_add(o, i, 1e-30),
                               [lm, bk.get("lr")], inc="v")
                    a2 = kb.on("v", lambda v, o=lr, i=lr: v.reciprocal(o, i), [a1], inc="v")
                    bk[4] = a2
                    bc = kb.on("t", lambda te, o=ctx.ps[6][hb:hb + 64, 0:256],
                               l=ctx.ones_rowf[0:1, 0:64], r=lr, tp=(0, hb):
                               te.matmul(o, l, r, start=True, stop=True, tile_position=tp),
                               [a2, bk.get(6)], inc="t")
                    bk["lr"] = bc
                    cp = kb.on("s", lambda sE, o=TMP[hb:hb + 64, 0:256],
                               i=ctx.ps[6][hb:hb + 64, 0:256]:
                               sE.activation(o, i, AF.Identity),
                               [bc, bk.get(("tmp", par, h2))], inc="s")
                    bk[6] = cp
                    mA = kb.on("v", lambda v, o=OT[hb:hb + 64, 0:256],
                               a=ctx.ps[2][hb:hb + 64, 0:256], bb=TMP[hb:hb + 64, 0:256]:
                               v.tensor_mul(o, a, bb),
                               [cp, pv, bk.get(("ot", par))], inc="v")
                    bk[2] = mA
                    bk[("tmp", par, h2)] = mA
                otok = kb.dma("g", AO[hp * 128:hp * 128 + 128, t0q:t0q + 256], OT,
                              [kb.tok("v")])
                bk[("ot", par)] = otok
                ufp[par] = [kb.tok("t"), kb.tok("s"), kb.tok("v"), otok]
        kb.fence()
        ra = ResAdd(ctx, X, XB)

        def xloadA(ci, t0, tw, par):
            o = 8192 + par * 4096 + ci * 512
            return AO[ci * 128:ci * 128 + 128, t0:t0 + tw], ctx.BIG[:, o:o + tw], []

        mmB(ctx, "l0fo" + sfx, io["L0_ffn_Wo" + sfx], D, D, xloadA, ra.evac,
            woff=0, xoff=8192, xbufs=2)
        kb.fence()
        lngr = ctx.CF[0:1, 0:1024]
        lnbc = ctx.CF[:, 2244:2252]
        g1 = kb.dma("q", lngr, io["L0_ffn_lng_r"][ly:ly + 1, 0:1024])
        g2 = kb.dma("q", lnbc, io["L0_ffn_lnb_c"][:, ly * 8:ly * 8 + 8])
        sA = sB = None
        fwf, stoks = {}, []
        for ti in range(NTT):
            t0 = ti * TT
            for dc in range(NCH):
                d0 = dc * 128
                sl = dc % 4
                s_tile = ctx.FW[:, sl * 512:sl * 512 + TT]
                dd = (fwf.get(sl) or [])
                if dc >= 4 and ti == 0:
                    z1 = kb.on("g", lambda g, o=s_tile[:, 0:1]: g.memset(o, 0.0), dd, inc="g")
                    l1 = kb.dma("q", s_tile[:, 1:TT], X[d0:d0 + 128, 0:TT - 1], dd + [z1])
                else:
                    off = t0 - 1 if dc >= 4 else t0
                    l1 = kb.dma("q", s_tile, X[d0:d0 + 128, off:off + TT], dd)
                t3tok = kb.on("s", lambda sE, o=ctx.FW[:, 2048 + sl * 512:2048 + sl * 512 + TT],
                              i=s_tile: sE.activation(o, i, AF.Square), [l1], inc="s")
                if dc == 0:
                    sA = ctx.spool.take()
                    sB = ctx.spool.take()
                first = (dc == 0)
                ts1 = kb.on("t", lambda te, o=sA[0][0:1, 0:TT], l=ctx.ones_colf[:, 0:1],
                            r=s_tile, st=first, sp=(dc == NCH - 1):
                            te.matmul(o, l, r, start=st, stop=sp),
                            [l1] + ([sA[1]] if first else []), inc="t")
                ts2 = kb.on("t", lambda te, o=sB[0][0:1, 0:TT], l=ctx.ones_colf[:, 0:1],
                            r=ctx.FW[:, 2048 + sl * 512:2048 + sl * 512 + TT],
                            st=first, sp=(dc == NCH - 1):
                            te.matmul(o, l, r, start=st, stop=sp),
                            [t3tok] + ([sB[1]] if first else []), inc="t")
                fwf[sl] = [ts1, ts2]
                if dc == NCH - 1:
                    c1_ = kb.on("v", lambda v, o=ctx.STATS[0:1, t0:t0 + TT],
                                i=sA[0][0:1, 0:TT]: v.tensor_copy(o, i), [ts1, ts2], inc="v")
                    c2_ = kb.on("v", lambda v, o=ctx.STATS[0:1, 4096 + t0:4096 + t0 + TT],
                                i=sB[0][0:1, 0:TT]: v.tensor_copy(o, i), [c1_], inc="v")
                    sA[1] = c1_
                    sB[1] = c2_
                    stoks.append(c2_)
        kb.fence()
        ST = ctx.STATS
        mrow = ctx.FW[0:1, 0:T]
        t1_ = kb.on("v", lambda v: v.tensor_scalar_mul(ST[0:1, 0:T], ST[0:1, 0:T], 1.0 / D),
                    stoks, inc="v")
        t2_ = kb.on("v", lambda v: v.tensor_scalar_mul(ST[0:1, 4096:4096 + T],
                    ST[0:1, 4096:4096 + T], 1.0 / D), [t1_], inc="v")
        t3_ = kb.on("v", lambda v: v.tensor_mul(mrow, ST[0:1, 0:T], ST[0:1, 0:T]),
                    [t2_], inc="v")
        t4_ = kb.on("v", lambda v: v.tensor_sub(mrow, ST[0:1, 4096:4096 + T], mrow),
                    [t3_], inc="v")
        t5_ = kb.on("s", lambda sE: sE.activation(mrow, mrow, AF.Sqrt,
                    bias=ctx.ceps[0:1, 0:1]), [t4_], inc="s")
        t6_ = kb.on("v", lambda v: v.reciprocal(mrow, mrow), [t5_], inc="v")
        t7_ = kb.on("v", lambda v: v.tensor_mul(ST[0:1, 0:T], ST[0:1, 0:T], mrow),
                    [t6_], inc="v")
        kb.fence()
        xnf = {}
        for dc in range(NCH):
            d0 = dc * 128
            for ti in range(NTT):
                t0 = ti * TT
                sl = dc % 4
                slot = ctx.spool.take()
                sps = slot[0]
                mm1 = kb.on("t", lambda te, o=sps[:, 0:TT],
                            l=lngr[0:1, dc * 128:dc * 128 + 128], r=mrow[0:1, t0:t0 + TT]:
                            te.matmul(o, l, r, start=True, stop=True),
                            [slot[1], g1, t6_], inc="t")
                pslot = ctx.pool.take()
                pps = pslot[0]
                mm2 = kb.on("t", lambda te, o=pps[:, 0:TT],
                            l=lngr[0:1, dc * 128:dc * 128 + 128], r=ST[0:1, t0:t0 + TT]:
                            te.matmul(o, l, r, start=True, stop=True),
                            [pslot[1], t7_], inc="t")
                xt2 = ctx.FW2[:, sl * 512:sl * 512 + TT]
                dd = [xnf.get(sl)]
                if dc >= 4 and ti == 0:
                    z1 = kb.on("g", lambda g, o=xt2[:, 0:1]: g.memset(o, 0.0), dd, inc="g")
                    lt = kb.dma("q", xt2[:, 1:TT], X[d0:d0 + 128, 0:TT - 1], dd + [z1])
                else:
                    off = t0 - 1 if dc >= 4 else t0
                    lt = kb.dma("q", xt2, X[d0:d0 + 128, off:off + TT], dd)
                v1_ = kb.on("v", lambda v, o=xt2, a=xt2, b=sps[:, 0:TT]:
                            v.tensor_mul(o, a, b), [mm1, lt], inc="v")
                xn = ctx.BIG[:, dc * 4096 + t0:dc * 4096 + t0 + TT]
                v2_ = kb.on("v", lambda v, o=xn, a=xt2, sc=lnbc[:, dc:dc + 1],
                            b=pps[:, 0:TT]:
                            v.scalar_tensor_tensor(o, a, sc, b, ALU.add, ALU.subtract),
                            [mm2, v1_, g2, xnf.get(("b", dc, ti))], inc="v")
                slot[1] = v2_
                pslot[1] = v2_
                xnf[sl] = v2_
        kb.fence()
        xsrc = lambda ci, t0, tw: (ctx.BIG[:, ci * 4096 + t0:ci * 4096 + t0 + tw], [])
        b1a = ctx.CF[:, 2252:2252 + 22]
        b1g = ctx.CF[:, 2274:2274 + 22]
        ba1 = kb.dma("q", b1a, io["L0_ffn_b1a_c"][:, ly * 22:ly * 22 + 22])
        ba2 = kb.dma("q", b1g, io["L0_ffn_b1g_c"][:, ly * 22:ly * 22 + 22])
        hfree, stash_free = {}, {}

        def evacW1(d0, dw, t0, tw, slot, deps):
            ps = slot[0]
            dc = d0 // 128
            hc = dc // 2
            ti2 = t0 // TT
            stash = ctx.BIG[0:dw, 40960 + ti2 * 512:40960 + ti2 * 512 + tw]
            if dc % 2 == 0:
                tok = kb.on("s", lambda sE, o=stash, i=ps[0:dw, 0:tw],
                            bb=b1a[0:dw, hc:hc + 1]:
                            sE.activation(o, i, AF.Identity, bias=bb),
                            list(deps) + [ba1] + (stash_free.get(ti2) or []), inc="s")
                slot[1] = tok
                stash_free[ti2] = [tok]
            else:
                par2 = (hc * NTT + ti2) % 2
                hb_ = ctx.BIG[0:dw, 45056 + par2 * 512:45056 + par2 * 512 + tw]
                g1_ = kb.on("s", lambda sE, o=hb_, i=ps[0:dw, 0:tw],
                            bb=b1g[0:dw, hc:hc + 1]:
                            sE.activation(o, i, AF.Gelu, bias=bb),
                            list(deps) + [ba2] + [hfree.get(par2)], inc="s")
                tok = kb.on("v", lambda v, o=hb_, a=hb_, b=stash:
                            v.tensor_mul(o, a, b),
                            [g1_] + (stash_free.get(ti2) or []), inc="v")
                slot[1] = tok
                stash_free[ti2] = (stash_free.get(ti2) or []) + [tok]
                htok = kb.dma("g", H[hc * 128:hc * 128 + dw, t0:t0 + tw], hb_, [tok])
                hfree[par2] = htok

        mmA(ctx, "l0w1" + sfx, io["L0_ffn_W1p" + sfx], D, 5632, xsrc, evacW1,
            slab_off=32768, gw=512)
        kb.fence()
        b2c = ctx.CF[:, 2296:2304]
        bb2 = kb.dma("q", b2c, io["L0_ffn_b2_c"][:, ly * 8:ly * 8 + 8])
        kb.fence()
        ra2 = ResAdd(ctx, X, XB,
                     bias_col=lambda d0, dw: b2c[0:dw, d0 // 128:d0 // 128 + 1])

        def xloadH(ci, t0, tw, par):
            o = 22528 + ci * 512
            return H[ci * 128:ci * 128 + 128, t0:t0 + tw], ctx.BIG[:, o:o + tw], []

        mmB(ctx, "l0w2" + sfx, io["L0_ffn_W2p" + sfx], 2816, D, xloadH, ra2.evac,
            woff=0, xoff=22528)
        kb.fence()
    if L0FFN_OUTER:
        postln_sum(ctx, X, X0, XB)


def emit_L2_attn(ctx, io, X, XB, H):
    nc, kb = ctx.nc, ctx.kb
    QF, KF, VF, AO = io["_QS"], io["_KS"], io["_VS2"], io["_AO"]
    KG, VG = io["_KG"], io["_VG"]
    ind16 = ctx.CF[:, 2112:2112 + 128]
    selc = ctx.CF[0:16, 1024:1024 + 1024]
    li = kb.dma("q", ind16, io["_ind16"])
    lc = kb.dma("q", selc, io["_selc"])
    selcb = ctx.CAR[0:16, 0:1024]
    lcb = kb.on("v", lambda v: v.tensor_copy(selcb, selc), [lc], inc="v")
    ind16b = ctx.CAR[:, 1024:1152]
    lib = kb.on("v", lambda v: v.tensor_copy(ind16b, ctx.CF[:, 2112:2112 + 128]),
                [li], inc="v")
    cmc = ctx.CF[0:16, 2240:2248]
    lcm = kb.dma("q", cmc, io["_l2cm"])
    bq = ctx.CF[:, 2248:2272]
    lbq = kb.dma("q", bq, io["L2_attn_qkv_b_c"])
    lbq = kb.on("v", lambda v: v.tensor_scalar_mul(ctx.CF[:, 2248:2256],
                ctx.CF[:, 2248:2256], 0.125), [lbq], inc="v")
    kb.fence()
    wq = load_w_group(ctx, io["L2_attn_qkv_wT"], D, 0, 3072, 0)
    xoff, stg = 24576, 33280
    xfree, stgf = {}, {}
    for ti in range(NTT):
        t0 = ti * TT
        par = ti % 2
        xo = xoff + par * 4096
        xt = kb.dma("q", ctx.BIG[:, xo:xo + 4096].rearrange("p (c t) -> p c t", t=TT),
                    XB[:, 0:NCH * XBW].rearrange("p (c t) -> p c t", t=XBW)
                    [:, :, XBO + t0:XBO + t0 + TT], [xfree.get(par)])
        xa = lambda ci, xo=xo: ctx.BIG[:, xo + ci * TT:xo + ci * TT + TT]
        for dc in range(24):
            slot = ctx.pool.take()
            ps = slot[0]
            for ci in range(8):
                mdeps = (wq + [xt, slot[1]]) if ci == 0 else ()
                tok = kb.on("t", lambda te, o=ps[:, 0:TT],
                            l=slab_lhsT(ctx, 0, 3072, ci, dc * 128, 128, 128),
                            r=xa(ci), st=(ci == 0), sp=(ci == 7):
                            te.matmul(o, l, r, start=st, stop=sp),
                            mdeps, inc="t" if ci == 7 else None)
            stgb = ctx.BIG[:, stg + (dc % 2) * 512:stg + (dc % 2) * 512 + TT]
            dst = QF if dc < 8 else (KF if dc < 16 else VF)
            e1 = kb.on("s", lambda sE, o=stgb, i=ps[:, 0:TT], bb=bq[:, dc:dc + 1],
                       sc=(0.125 if dc < 8 else 1.0):
                       sE.activation(o, i, AF.Identity, bias=bb, scale=sc),
                       [tok, lbq] + (stgf.get(dc % 2) or []), inc="s")
            d1 = kb.dma("g", dst[(dc % 8) * 128:(dc % 8) * 128 + 128, t0:t0 + TT],
                        stgb, [e1])
            stgf[dc % 2] = [d1]
            slot[1] = e1
        xfree[par] = kb.tok("t")
    kb.fence()
    cc1 = kb.on("g", lambda g: g.collective_compute(
        "AllGather", ALU.bypass, replica_groups=[list(range(8))],
        ins=[KF[:, :]], outs=[KG[:, :]]), [], inc="cc1")
    kb.on("g", lambda g: g.wait_ge.__self__ if False else g.engine_nop(), [("cc1", 1)])
    cc2 = kb.on("g", lambda g: g.collective_compute(
        "AllGather", ALU.bypass, replica_groups=[list(range(8))],
        ins=[VF[:, :]], outs=[VG[:, :]]), [], inc="cc2")
    kb.fence()
    if L2_DUMP:
        for rr in range(8):
            kb.dma("g", H[2048 + rr * 128:2048 + rr * 128 + 128, 0:4096],
                   KG[rr * 1024:rr * 1024 + 128, 0:4096])
        kb.fence()
    if L2_DELAY:
        for _ in range(100):
            kb.on("v", lambda v: v.tensor_copy(ctx.BIG[:, 0:4096], ctx.BIG[:, 4096:8192]),
                  [], inc="v")
        kb.fence()
    bk = {}
    for si in range(8):
        s0 = si * TT
        lq = kb.dma("q", ctx.BIG[:, 0:4096].rearrange("p (c t) -> p c t", t=TT),
                    QF[:, s0:s0 + TT].rearrange("(c p) t -> p c t", p=128),
                    [bk.get("qfree")])
        macc = ctx.FW2[0:16, 0:512]
        lrow = ctx.FW2[0:16, 512:1024]
        lrow2 = ctx.FW2[0:16, 1536:2048]
        EB = lambda tp: ctx.BIG[0:16, 20480 + tp * 512:20480 + tp * 512 + 512]
        EB2 = lambda tp: ctx.BIG[0:16, 25600 + tp * 512:25600 + tp * 512 + 512]
        prod = ctx.BIG[:, 24576:24576 + 512]
        sc_toks = []
        for tp in range(8):
            par = tp % 2
            ko = 4096 + par * 4096
            lk = kb.dma("q", ctx.BIG[:, ko:ko + 4096].rearrange("p (c t) -> p c t", t=TT),
                        KG[tp * 1024:(tp + 1) * 1024, s0:s0 + TT].rearrange(
                            "(c p) t -> p c t", p=128), [bk.get(("kfree", par))])
            psS = ctx.ps[tp]
            last = None
            for ci in range(8):
                pm = kb.on("v", lambda v, o=prod, a=ctx.BIG[:, ci * TT:ci * TT + TT],
                           b=ctx.BIG[:, ko + ci * TT:ko + ci * TT + TT]:
                           v.tensor_mul(o, a, b),
                           [lq, lk, bk.get("prod")] + ([bk.get(("sb", tp))] if ci == 0 else []),
                           inc="v")
                mm = kb.on("t", lambda te, o=psS[0:16, 0:TT],
                           l=ind16b[:, ci * 16:ci * 16 + 16], r=prod,
                           st=(ci == 0), sp=(ci == 7):
                           te.matmul(o, l, r, start=st, stop=sp),
                           [pm, lib] + ([bk.get(("bankv", tp))] if ci == 0 else []),
                           inc="t")
                bk["prod"] = mm
                last = mm
            bk[("kfree", par)] = kb.tok("v")
            sc_toks.append(last)
            if L2_SERIAL:
                kb.fence()
        SC = ctx.STATS[0:16, 0:4096]
        cps = []
        for tp in range(8):
            cp_ = kb.on("s", lambda sE, o=SC[0:16, tp * 512:tp * 512 + 512],
                        i=ctx.ps[tp][0:16, 0:TT]: sE.activation(o, i, AF.Identity),
                        [sc_toks[tp], bk.get("scf")], inc="s")
            cps.append(cp_)
        bk["scf"] = cps[-1]
        if L2_DUMP:
            kb.dma("g", H[si * 16:si * 16 + 16, 0:4096], SC, [cps[-1]])
        mview = SC.rearrange("h (tp s) -> h s tp", tp=8)
        c0 = kb.on("v", lambda v, o=macc, i=mview: v.reduce_max(o, i, axis=AX),
                   [cps[-1], bk.get("macc")], inc="v")
        esubs = []
        for tp in range(8):
            sb_ = kb.on("v", lambda v, o=ctx.FW2[0:16, 1024:1024 + 512],
                        a=SC[0:16, tp * 512:tp * 512 + 512], b=macc:
                        v.tensor_sub(o, a, b), [c0, bk.get("esub")], inc="v")
            ee = kb.on("s", lambda sE, o=EB(tp), i=ctx.FW2[0:16, 1024:1024 + 512]:
                       sE.activation(o, i, AF.Exp), [sb_] + (bk.get(("eb", tp)) or []),
                       inc="s")
            bk["esub"] = ee
            em_ = kb.on("v", lambda v, o=EB(tp), a=EB(tp), sc=cmc[0:16, tp:tp + 1]:
                        v.tensor_scalar_mul(o, a, sc), [ee, lcm], inc="v")
            esubs.append(em_)
        eview = ctx.BIG[0:16, 20480:20480 + 4096].rearrange("h (tp s) -> h s tp", tp=8)
        lsum = kb.on("v", lambda v, o=lrow, i=eview: v.reduce_sum(o, i, axis=AX),
                     esubs + [bk.get("lrow")], inc="v")
        r1 = kb.on("v", lambda v, o=lrow2, i=lrow: v.tensor_scalar_add(o, i, 1e-30),
                   [lsum, bk.get("lrow2")], inc="v")
        r2 = kb.on("v", lambda v, o=lrow2, i=lrow2: v.reciprocal(o, i), [r1], inc="v")
        if L2_SERIAL:
            kb.fence()
        efin = []
        for tp in range(8):
            ef = kb.on("v", lambda v, o=EB2(tp), a=EB(tp), b=lrow2: v.tensor_mul(o, a, b),
                       [r2] + (bk.get(("eb2", tp)) or []), inc="v")
            efin.append(ef)
        bk["macc"] = r2
        bk["lrow"] = r2
        bk["lrow2"] = kb.tok("v")
        if L2_DUMP:
            kb.dma("g", H[128 + si * 16:128 + si * 16 + 16, 0:512], macc, [c0])
            kb.dma("g", H[256 + si * 16:256 + si * 16 + 16, 0:512], lrow2, [r2])
            for tp_ in range(8):
                kb.dma("g", H[384 + si * 16:384 + si * 16 + 16, tp_ * 512:tp_ * 512 + 512],
                       EB2(tp_), [efin[tp_]])
        acc_init = [None] * 8
        for tp in range(8):
            par = tp % 2
            vo = 12288 + par * 4096
            lv = kb.dma("q", ctx.BIG[:, vo:vo + 4096].rearrange("p (c t) -> p c t", t=TT),
                        VG[tp * 1024:(tp + 1) * 1024, s0:s0 + TT].rearrange(
                            "(c p) t -> p c t", p=128), [bk.get(("vfree", par))])
            for ci in range(8):
                bslot = ctx.ps[tp]
                bm = kb.on("t", lambda te, o=bslot[:, 0:TT],
                           l=selcb[0:16, ci * 128:ci * 128 + 128], r=EB2(tp):
                           te.matmul(o, l, r, start=True, stop=True),
                           [efin[tp], lcb, bk.get(("pvb", tp))], inc="t")
                bk[("eb", tp)] = [bm]
                bk[("eb2", tp)] = [bm]
                acc = ctx.FW[:, ci * 512:ci * 512 + 512]
                if tp == 0:
                    vt = kb.on("v", lambda v, o=acc, a=ctx.BIG[:, vo + ci * TT:vo + ci * TT + TT],
                               b=bslot[:, 0:TT]: v.tensor_mul(o, a, b),
                               [bm, lv, bk.get(("accf", ci))], inc="v")
                else:
                    tmpb = ctx.BIG[:, 24576 + 512:24576 + 1024]
                    vt1 = kb.on("v", lambda v, o=tmpb, a=ctx.BIG[:, vo + ci * TT:vo + ci * TT + TT],
                                b=bslot[:, 0:TT]: v.tensor_mul(o, a, b),
                                [bm, lv, bk.get("tmpb")], inc="v")
                    vt = kb.on("v", lambda v, o=acc, a=acc, b=tmpb: v.tensor_add(o, a, b),
                               [vt1, acc_init[ci]], inc="v")
                    bk["tmpb"] = vt
                bk[("pvb", tp)] = vt
                acc_init[ci] = vt
            bk[("vfree", par)] = kb.tok("v")
            bk[("bankv", tp)] = kb.tok("v")
            if L2_SERIAL:
                kb.fence()
        aots = []
        for ci in range(8):
            aots.append(kb.dma("g", AO[ci * 128:ci * 128 + 128, s0:s0 + TT],
                               ctx.FW[:, ci * 512:ci * 512 + 512], [acc_init[ci]]))
            bk[("accf", ci)] = aots[-1]
        bk["qfree"] = kb.tok("v")
        if L2_SERIAL:
            kb.fence()
    kb.fence()
    bo = ctx.CF[:, 2272:2280]
    lbo = kb.dma("q", bo, io["L2_attn_out_b_c"])
    kb.fence()
    pl = PostLN(ctx, X, XB, bias_col=lambda d0, dw: bo[0:dw, d0 // 128:d0 // 128 + 1])

    def xload(ci, t0, tw, par):
        o = 8192 + par * 4096 + ci * 512
        return AO[ci * 128:ci * 128 + 128, t0:t0 + tw], ctx.BIG[:, o:o + tw], []

    mmB(ctx, "l2o", io["L2_attn_out_wT"], D, D, xload, pl.evac, woff=0, xoff=8192,
        xbufs=2)
    kb.fence()
    pl.apply()


def emit_L2_ffn(ctx, io, X, XB, H):
    nc, kb = ctx.nc, ctx.kb
    GX, Yacc = io["_QS"], io["_Yacc"]
    CNT, CNTG, KGH, GRH = io["_CNT"], io["_CNTG"], io["_KGH"], io["_GRH"]
    UT = ctx.CAR[:, 0:128]
    z0 = kb.on("g", lambda g: g.memset(UT, 1.0), inc="g")
    z1 = kb.on("g", lambda g: g.affine_select(UT, UT, pattern=[[1, 128]],
               compare_op=ALU.is_ge, fill=0.0, base=0, channel_multiplier=-1),
               [z0], inc="g")
    kb.fence()
    wrf = ctx.CF[:, 2240:2304]
    wr = [kb.dma("q", ctx.CF[:, 2240 + ci * 8:2240 + ci * 8 + 8],
                 io["L2_ffn_Wr"][ci * 128:ci * 128 + 128, 0:8]) for ci in range(8)]
    brr = ctx.FW2[0:1, 1576:1584]
    lb = kb.dma("q", brr, io["L2_ffn_br"].rearrange("(a n) -> a n", a=1))
    OH = ctx.FW[:, 0:256]
    RK = ctx.FW[:, 256:512]
    GT = ctx.FW[:, 512:544]
    carry = ctx.FW2[0:1, 1568:1576]
    zc = kb.on("g", lambda g: g.memset(carry, 0.0), inc="g")
    cslot = ctx.ps[6]
    crow_tok = None
    kb.fence()
    xfree = {}
    for tile in range(32):
        t0 = tile * 128
        par = tile % 2
        xts = []
        for ci in range(8):
            xdst = ctx.FW2[:, par * 512 + (ci % 4) * 128:par * 512 + (ci % 4) * 128 + 128]
            if ci >= 4:
                xdst = ctx.FW[:, 1152 + par * 512 + (ci % 4) * 128:
                              1152 + par * 512 + (ci % 4) * 128 + 128]
            xts.append((xdst, kb.dma("q", xdst, X[ci * 128:ci * 128 + 128, t0:t0 + 128],
                                     [xfree.get((par, ci))])))
        slot = ctx.pool.take()
        ps = slot[0]
        ib = kb.on("t", lambda te, o=ps[:, 0:8], l=ctx.ones_rowf[0:1, 0:128],
                   r=brr: te.matmul(o, l, r, start=True, stop=False),
                   [slot[1], lb], inc=None)
        for ci in range(8):
            tok = kb.on("t", lambda te, o=ps[:, 0:8],
                        l=xts[ci][0],
                        r=wrf[0:128, ci * 8:ci * 8 + 8],
                        sp=(ci == 7): te.matmul(o, l, r, start=False, stop=sp),
                        [xts[ci][1]] + (wr if ci == 0 else []),
                        inc="t" if ci == 7 else None)
        lg = ctx.FW[:, 544 + tile * 8:544 + tile * 8 + 8]
        mx = ctx.FW[:, 800 + tile:800 + tile + 1]
        nmx = ctx.FW[:, 832 + tile:832 + tile + 1]
        ex = ctx.FW[:, 864 + tile * 8:864 + tile * 8 + 8]
        sm = ctx.FW[:, 1120 + tile:1120 + tile + 1]
        e0 = kb.on("v", lambda v, o=lg, i=ps[:, 0:8]: v.tensor_copy(o, i),
                   [tok], inc="v")
        slot[1] = e0
        e1 = kb.on("v", lambda v, o=mx, i=lg: v.reduce_max(o, i, axis=AX), [e0], inc="v")
        e1b = kb.on("v", lambda v, o=nmx, i=mx: v.tensor_scalar_mul(o, i, -1.0),
                    [e1], inc="v")
        e2 = kb.on("s", lambda sE, o=ex, i=lg, bb=nmx: sE.activation(
            o, i, AF.Exp, bias=bb), [e1b], inc="s")
        e3 = kb.on("v", lambda v, o=sm, i=ex: v.reduce_sum(o, i, axis=AX), [e2], inc="v")
        e4 = kb.on("v", lambda v, o=GT[:, tile:tile + 1], i=sm: v.reciprocal(o, i),
                   [e3], inc="v")
        oh_t = OH[:, tile * 8:tile * 8 + 8]
        e5 = kb.on("v", lambda v, o=oh_t, i=lg, sc=mx: v.tensor_scalar(
            o, i, sc, None, ALU.is_ge), [e4], inc="v")
        rslot = ctx.pool.take()
        rp = rslot[0]
        ohb = ctx.BIG[:, 10240 + par * 8:10240 + par * 8 + 8]
        e6 = kb.on("v", lambda v, o=ohb, i=oh_t: v.tensor_copy(o, i),
                   [e5, bk_get(ctx, ("ohb", par))], inc="v")
        i1 = kb.on("t", lambda te, o=rp[:, 0:8], l=ctx.ones_rowf[0:1, 0:128], r=carry,
                   st=True: te.matmul(o, l, r, start=True, stop=False),
                   [rslot[1], e6, crow_tok, zc], inc=None)
        i2 = kb.on("t", lambda te, o=rp[:, 0:8], l=UT, r=ohb:
                   te.matmul(o, l, r, start=False, stop=True), [z1], inc="t")
        e7 = kb.on("v", lambda v, o=RK[:, tile * 8:tile * 8 + 8], i=rp[:, 0:8]:
                   v.tensor_copy(o, i), [i2, bk_get(ctx, "rk")], inc="v")
        rslot[1] = e7
        bk_set(ctx, "rk", e7)
        bk_set(ctx, ("ohb", par), i2)
        c1 = kb.on("t", lambda te, o=cslot[0:1, 0:8], l=ctx.ones_colb[:, 0:1], r=ohb,
                   st=(tile == 0), sp=(tile == 31):
                   te.matmul(o, l, r, start=(tile == 0), stop=(tile == 31)),
                   [e6] + ([crow_tok] if tile == 0 else []), inc="t")
        crow_tok = kb.on("s", lambda sE, o=carry, i=cslot[0:1, 0:8]:
                         sE.activation(o, i, AF.Identity), [c1], inc="s")
        for ci in range(8):
            xfree[(par, ci)] = tok
    d1 = kb.dma("q", CNT[0:1, 0:8], carry, [crow_tok])
    cc3 = kb.on("g", lambda g: g.collective_compute(
        "AllGather", ALU.bypass, replica_groups=[list(range(8))],
        ins=[CNT[:, :]], outs=[CNTG[:, :]]), [d1], inc="cc3")
    kb.fence()
    pfx = ctx.FW2[0:8, 1552:1553]
    gcnt = ctx.FW2[0:8, 1560:1568]
    lg1 = kb.dma("q", pfx, io["_l2pfx"])
    lg2 = kb.dma("q", gcnt, CNTG[:, :], [("cc3", 1)])
    bslot = ctx.spool.take()
    b1 = kb.on("t", lambda te, o=bslot[0][0:8, 0:1], l=gcnt, r=pfx:
               te.matmul(o, l, r, start=True, stop=True), [lg1, lg2, bslot[1]], inc="t")
    baser = ctx.FW2[0:1, 1536:1544]
    basec = ctx.FW2[0:8, 1544:1545]
    b2 = kb.on("s", lambda sE, o=basec, i=bslot[0][0:8, 0:1]:
               sE.activation(o, i, AF.Identity), [b1], inc="s")
    bslot[1] = b2
    d2 = kb.dma("q", GRH[0:8, 0:1], basec, [b2])
    d3 = kb.dma("q", baser, GRH[0:8, 0:1].rearrange("(a b) c -> a (b c)", a=1), [d2])
    kb.fence()
    bkd = {}
    for tile in range(32):
        t0 = tile * 128
        slot = ctx.spool.take()
        bp = slot[0]
        i1 = kb.on("t", lambda te, o=bp[:, 0:8], l=ctx.ones_rowf[0:1, 0:128], r=baser:
                   te.matmul(o, l, r, start=True, stop=True), [slot[1], d3], inc="t")
        pos = ctx.FW2[:, 0:8]
        kp = ctx.FW2[:, 8:16]
        kgt = ctx.FW2[:, 16:24]
        p1 = kb.on("v", lambda v, o=pos, a=RK[:, tile * 8:tile * 8 + 8], b=bp[:, 0:8]:
                   v.tensor_add(o, a, b), [i1, bkd.get("pos")], inc="v")
        p2 = kb.on("v", lambda v, o=kp, i=pos: v.tensor_single_scalar(o, i, 4096.0, ALU.is_lt),
                   [p1], inc="v")
        p3 = kb.on("v", lambda v, o=kgt, a=OH[:, tile * 8:tile * 8 + 8],
                   sc=GT[:, tile:tile + 1], b=kp:
                   v.scalar_tensor_tensor(o, a, sc, b, ALU.mult, ALU.mult), [p2], inc="v")
        slot[1] = p3
        bkd["pos"] = p3
        tsl2 = ctx.pool.take()
        tr = kb.on("t", lambda te, o=tsl2[0][0:8, 0:128], i=kgt, idn=ctx.identf[:, :]:
                   te.transpose(o, i, idn), [p3, tsl2[1]], inc="t")
        g3 = kb.on("s", lambda sE, o=ctx.FW2[0:8, 1664:1664 + 128], i=tsl2[0][0:8, 0:128]:
                   sE.activation(o, i, AF.Identity), [tr, bkd.get("kghs")], inc="s")
        g4 = kb.dma("q", KGH[0:8, t0:t0 + 128], ctx.FW2[0:8, 1664:1664 + 128], [g3])
        tsl2[1] = g3
        bkd["kghs"] = g4
    kb.fence()
    tg = ctx.pool.take()
    tr1 = kb.on("t", lambda te, o=tg[0][0:32, 0:128], i=GT, idn=ctx.identf[:, :]:
                te.transpose(o, i, idn), [kb.tok("v"), tg[1]], inc="t")
    tg2 = kb.on("s", lambda sE, o=ctx.FW2[0:32, 1792:1792 + 128], i=tg[0][0:32, 0:128]:
                sE.activation(o, i, AF.Identity), [tr1], inc="s")
    tg[1] = tg2
    dg1 = kb.dma("q", io["_GTH"][:, :], ctx.FW2[0:32, 1792:1792 + 128], [tg2])
    grow = ctx.STATS[0:1, 4096:4096 + 4096]
    dg3 = kb.dma("q", grow, io["_GTH"][0:32, 0:128].rearrange("(a p) c -> a (p c)", a=1),
                 [dg1])
    kb.fence()
    gxf = {}
    for ti in range(NTT):
        t0 = ti * TT
        gsl = ctx.spool.take()
        gb = kb.on("t", lambda te, o=gsl[0][:, 0:TT], l=ctx.ones_rowf[0:1, 0:128],
                   r=grow[0:1, t0:t0 + TT]: te.matmul(o, l, r, start=True, stop=True),
                   [dg3, gsl[1]], inc="t")
        lastv = None
        for ci in range(NCH):
            xt = kb.dma("q", ctx.BIG[:, 8192 + ci * 512:8192 + ci * 512 + TT],
                        XB[:, ci * XBW + XBO + t0:ci * XBW + XBO + t0 + TT],
                        [gxf.get(ci)])
            v1 = kb.on("v", lambda v, o=ctx.BIG[:, 8192 + ci * 512:8192 + ci * 512 + TT],
                       a=ctx.BIG[:, 8192 + ci * 512:8192 + ci * 512 + TT], b=gsl[0][:, 0:TT]:
                       v.tensor_mul(o, a, b), [xt, gb], inc="v")
            d_ = kb.dma("g", GX[ci * 128:ci * 128 + 128, t0:t0 + TT],
                        ctx.BIG[:, 8192 + ci * 512:8192 + ci * 512 + TT], [v1])
            gxf[ci] = d_
            lastv = v1
        gsl[1] = lastv
    kb.fence()
    b1c = ctx.CF[:, 2240:2240 + 64]
    lb1 = kb.dma("q", b1c, io["L2_ffn_b1_c"])
    b2r = ctx.CAR[0:1, 4096:4096 + 8192]
    lb2r = kb.on("g", lambda g: g.dma_start(ctx.CAR[0:1, 4096:4096 + 8192],
                 io["L2_ffn_b2r"][0:1, :]), [], inc="dg", inc_amt=16)
    kgrow_f = ctx.STATS[0:1, 0:4096]
    kgrow_b = ctx.CAR[0:1, 12288:12288 + 4096]
    kb.fence()
    hfree = {}
    for e in range(8):
        we1 = load_w_group(ctx, io[f"L2_ffn_W1_{e}"], D, 0, 1024, 0,
                           deps=[bk_get(ctx, "wfree")])
        we2 = load_w_group(ctx, io[f"L2_ffn_W2_{e}"], D, 0, 1024, 8192,
                           deps=[bk_get(ctx, "wfree")])
        lkg = kb.dma("q", kgrow_f, KGH[e:e + 1, 0:4096], [bk_get(ctx, "kgrow")])
        lkgb = kb.dma("g", kgrow_b, KGH[e:e + 1, 0:4096], [bk_get(ctx, "kgrow")])
        for ti in range(NTT):
            t0 = ti * TT
            par = ti % 2
            xo = 16384 + par * 4096
            xt = kb.dma("q", ctx.BIG[:, xo:xo + 4096].rearrange("p (c t) -> p c t", t=TT),
                        GX[:, t0:t0 + TT].rearrange("(c p) t -> p c t", p=128),
                        [hfree.get(("x", par))])
            kslot = ctx.spool.take()
            kgb_ = kb.on("t", lambda te, o=kslot[0][:, 0:TT], l=ctx.ones_rowf[0:1, 0:128],
                         r=kgrow_f[0:1, t0:t0 + TT]: te.matmul(o, l, r, start=True, stop=True),
                         [lkg, kslot[1]], inc="t")
            hts = []
            for dc in range(8):
                slot = ctx.pool.take()
                ps = slot[0]
                for ci in range(8):
                    mdeps = (we1 + [xt, slot[1]]) if ci == 0 else ()
                    tok = kb.on("t", lambda te, o=ps[:, 0:TT],
                                l=slab_lhsT(ctx, 0, 1024, ci, dc * 128, 128, 128),
                                r=ctx.BIG[:, xo + ci * TT:xo + ci * TT + TT],
                                st=(ci == 0), sp=(ci == 7):
                                te.matmul(o, l, r, start=st, stop=sp),
                                mdeps, inc="t" if ci == 7 else None)
                hstg = ctx.BIG[:, 24576 + dc * 512:24576 + dc * 512 + TT]
                h1 = kb.on("s", lambda sE, o=hstg, i=ps[:, 0:TT],
                           bb=b1c[:, e * 8 + dc:e * 8 + dc + 1]:
                           sE.activation(o, i, AF.Relu, bias=bb),
                           [tok, lb1] + (hfree.get(dc) or []), inc="s")
                h2 = kb.on("v", lambda v, o=hstg, a=hstg, b=kslot[0][:, 0:TT]:
                           v.tensor_mul(o, a, b), [h1, kgb_], inc="v")
                slot[1] = h2
                hts.append(h2)
                hfree[dc] = [h2]
            kslot[1] = hts[-1]
            for dc in range(8):
                slot = ctx.pool.take()
                ps = slot[0]
                ib = kb.on("t", lambda te, o=ps[:, 0:TT],
                           l=b2r[0:1, e * 1024 + dc * 128:e * 1024 + dc * 128 + 128],
                           r=kgrow_b[0:1, t0:t0 + TT]:
                           te.matmul(o, l, r, start=True, stop=False),
                           [slot[1], lkgb], inc=None)
                for ci in range(8):
                    tok = kb.on("t", lambda te, o=ps[:, 0:TT],
                                l=slab_lhsT(ctx, 8192, 1024, ci, dc * 128, 128, 128),
                                r=ctx.BIG[:, 24576 + ci * 512:24576 + ci * 512 + TT],
                                sp=(ci == 7): te.matmul(o, l, r, start=False, stop=sp),
                                [hts[ci]] if ci == 0 else (), inc="t" if ci == 7 else None)
                ystg = ctx.FW[:, dc * 512:dc * 512 + TT]
                y1 = kb.on("s", lambda sE, o=ystg, i=ps[:, 0:TT]:
                           sE.activation(o, i, AF.Identity),
                           [tok] + (hfree.get(("y", dc)) or []), inc="s")
                slot[1] = y1
                if e == 0:
                    yd = kb.dma("g", Yacc[dc * 128:dc * 128 + 128, t0:t0 + TT], ystg, [y1])
                else:
                    yd = kb.on("g", lambda g, o=Yacc[dc * 128:dc * 128 + 128, t0:t0 + TT],
                               i=ystg: g.dma_start(o, i, accum_op=ALU.add),
                               [y1], inc="dg", inc_amt=16)
                hfree[("y", dc)] = [yd]
            hfree[("x", par)] = kb.tok("t")
        bk_set(ctx, "wfree", kb.tok("t"))
        bk_set(ctx, "kgrow", kb.tok("t"))
    kb.fence()
    postln_sum(ctx, X, Yacc, XB)


def bk_get(ctx, k):
    return getattr(ctx, "_bkd", {}).get(k)


def bk_set(ctx, k, v):
    if not hasattr(ctx, "_bkd"):
        ctx._bkd = {}
    ctx._bkd[k] = v


def col_layout(v):
    n = v.shape[0]
    nc_ = cdiv(n, 128)
    out = np.zeros((128, nc_), np.float32)
    for c in range(nc_):
        sz = min(128, n - c * 128)
        out[0:sz, c] = v[c * 128:c * 128 + sz]
    return np.ascontiguousarray(out)


def build_program(block_list, shapes):
    nc = bass.Bass("TRN2", target_bir_lowering=False, debug=False)
    es = ExitStack()
    io = {}

    def din(name, shape, dt=F32):
        if name not in io:
            io[name] = nc.dram_tensor(name, list(shape), dt, kind="ExternalInput").ap()
        return io[name]

    din("x_fm", (D, T))
    xout = nc.dram_tensor("xout", [D, T], F32, kind="ExternalOutput").ap()
    X = nc.dram_tensor("X", [D, T], F32).ap()
    XB = nc.dram_tensor("XB", [128, NCH * XBW], BF16).ap()
    H = nc.dram_tensor("Hscr", [4096, T], BF16).ap()
    io["_QS"] = nc.dram_tensor("QSscr", [D, T], BF16).ap()
    io["_KS"] = nc.dram_tensor("KSscr", [D, T], BF16).ap()
    io["_VS"] = nc.dram_tensor("VSscr", [T, D], BF16).ap()
    io["_AO"] = nc.dram_tensor("AOscr", [D, T], BF16).ap()
    io["_Q2S"] = nc.dram_tensor("Q2Sscr", [D, T], BF16).ap()
    io["_K2S"] = nc.dram_tensor("K2Sscr", [D, T], BF16).ap()
    io["_QRS"] = nc.dram_tensor("QRSscr", [D, T], BF16).ap()
    io["_KRS"] = nc.dram_tensor("KRSscr", [D, T], BF16).ap()
    io["_X0"] = nc.dram_tensor("X0scr", [D, T], F32).ap()
    io["_VS2"] = nc.dram_tensor("VS2scr", [D, T], BF16).ap()
    io["_KG"] = nc.dram_tensor("KGscr", [8 * D, T], BF16).ap()
    io["_VG"] = nc.dram_tensor("VGscr", [8 * D, T], BF16).ap()
    io["_CNT"] = nc.dram_tensor("CNTscr", [1, 8], F32).ap()
    io["_CNTG"] = nc.dram_tensor("CNTGscr", [8, 8], F32).ap()
    io["_KGH"] = nc.dram_tensor("KGHscr", [8, T], F32).ap()
    io["_GRH"] = nc.dram_tensor("GRHscr", [8, 1], F32).ap()
    io["_GTH"] = nc.dram_tensor("GTHscr", [32, 128], F32).ap()
    io["_Yacc"] = nc.dram_tensor("Yaccscr", [D, T], F32).ap()

    for bname in block_list:
        for n, shp, dt in INPUT_DEFS[bname]:
            din(n, shp, dt)

    kb = KB()
    ctx = Ctx(nc, kb, es)
    CTX_HACK.clear()
    CTX_HACK.append(ctx)
    ctx.setup()

    zrow = ctx.CF[:, 3064:3072]
    kb.on("g", lambda g: g.memset(zrow, 0.0), inc="g")
    kb.dma("q", X[:, :], io["x_fm"][:, :])
    for c in range(NCH):
        kb.dma("g", XB[:, c * XBW + XBO:c * XBW + XBO + T],
               io["x_fm"][c * 128:(c + 1) * 128, 0:T])
        kb.dma("g", XB[:, c * XBW:c * XBW + XBO], zrow[:, 0:4])
    kb.fence()

    for bname in block_list:
        EMITTERS[bname](ctx, io, X, XB, H)
        kb.fence()

    kb.dma("q", xout[:, :], X[:, :])
    if DEBUG_CM and "L2_attn" in block_list:
        dbgcm = nc.dram_tensor("dbgCM", [16, 8], F32, kind="ExternalOutput").ap()
        kb.dma("q", dbgcm, CTX_HACK[0].CF[0:16, 2240:2248])
    srcs = {"H": (H, BF16), "X": (X, F32), "XB": (XB, BF16),
            "QS": (io.get("_QS"), BF16), "KS": (io.get("_KS"), BF16),
            "VS": (io.get("_VS"), BF16), "AO": (io.get("_AO"), BF16),
            "VS2": (io.get("_VS2"), BF16), "Yacc": (io.get("_Yacc"), F32),
            "KG": (io.get("_KG"), BF16), "VG": (io.get("_VG"), BF16),
            "KGH": (io.get("_KGH"), F32), "CNTG": (io.get("_CNTG"), F32),
            "GTH": (io.get("_GTH"), F32), "GX": (io.get("_QS"), BF16)}
    for nm, src, shp in DEBUG_DUMP:
        dbg = nc.dram_tensor("dbg" + nm, list(shp), srcs[src][1], kind="ExternalOutput").ap()
        kb.dma("q", dbg, srcs[src][0][:, :])
    kb.fence()
    kb.on("v", lambda v: v.engine_nop())
    kb.emit(nc, es)
    es.close()
    return nc


DEBUG_DUMP = []
L0FFN_LAYERS = 6
L0FFN_OUTER = True
L2_SERIAL = True
DEBUG_CM = False
L2_DUMP = False
L2_DELAY = False
CTX_HACK = []

INPUT_DEFS = {
    "L2_attn": [
        ("L2_attn_qkv_wT", (1024, 3072), F32),
        ("L2_attn_qkv_b_c", (128, 24), F32),
        ("L2_attn_out_wT", (1024, 1024), F32),
        ("L2_attn_out_b_c", (128, 8), F32),
        ("_l2cm", (16, 8), F32),
        ("_ind16", (128, 128), F32),
        ("_selc", (16, 1024), F32),
    ],
    "L2_ffn": [
        ("L2_ffn_Wr", (1024, 8), F32),
        ("L2_ffn_br", (8,), F32),
        ("L2_ffn_b1_c", (128, 64), F32),
        ("L2_ffn_b2r", (1, 8192), F32),
        ("_l2pfx", (8, 1), F32),
    ] + [(f"L2_ffn_W1_{e}", (1024, 1024), F32) for e in range(8)]
      + [(f"L2_ffn_W2_{e}", (1024, 1024), F32) for e in range(8)],
    "L0_ffn": (
        [(f"L0_ffn_Wqkv_{i}", (1024, 3072), F32) for i in range(6)]
        + [(f"L0_ffn_Wo_{i}", (1024, 1024), F32) for i in range(6)]
        + [(f"L0_ffn_W1p_{i}", (1024, 5632), F32) for i in range(6)]
        + [(f"L0_ffn_W2p_{i}", (2816, 1024), F32) for i in range(6)]
        + [("L0_ffn_b1a_c", (128, 132), F32), ("L0_ffn_b1g_c", (128, 132), F32),
           ("L0_ffn_b2_c", (128, 48), F32), ("L0_ffn_lng_r", (6, 1024), F32),
           ("L0_ffn_lnb_c", (128, 48), F32), ("L0_ffn_qs_c", (6, 128), F32),
           ("L0_ffn_ks_c", (6, 128), F32),
           ("L0_ffn_qs2_c", (6, 128), F32), ("L0_ffn_ks2_c", (6, 128), F32),
           ("L0_ffn_qs_r", (6, 64), F32), ("L0_ffn_ks_r", (6, 64), F32),
           ("_xC1s", (128, 4096), F32), ("_xC2s", (128, 4096), F32),
           ("_xC1i", (128, 4096), F32), ("_xC2i", (128, 4096), F32),
           ("_rhP", (128, 128), F32), ("_selc", (16, 1024), F32),
           ("_ind16", (128, 128), F32), ("_sel16", (16, 2048), F32)]
    ),
    "L1_attn": [
        ("L1_attn_W_q", (1024, 1024), F32),
        ("L1_attn_W_k", (1024, 1024), F32),
        ("L1_attn_W_v", (1024, 1024), F32),
        ("L1_attn_W_o", (1024, 1024), F32),
        ("L1_attn_W_down_kv", (1024, 256), F32),
        ("L1_attn_W_up_k", (256, 1024), F32),
        ("L1_attn_W_up_v", (256, 1024), F32),
        ("_l1fqm", (1, 4096), F32),
        ("_l1fqp", (1, 4096), F32),
        ("_l1fkm", (1, 4096), F32),
        ("_l1fkp", (1, 4096), F32),
        ("_l1D", (128, 2048), F32),
        ("_ind16", (128, 128), F32),
        ("_sel16", (16, 2048), F32),
    ],
    "L0_attn": [
        ("L0_attn_Wq1", (1024, 1024), F32),
        ("L0_attn_Wq2", (1024, 1024), F32),
        ("L0_attn_Wk1", (1024, 128), F32),
        ("L0_attn_Wk2", (1024, 128), F32),
        ("L0_attn_Wv", (1024, 128), F32),
        ("L0_attn_Wout", (1024, 1024), F32),
        ("L0_attn_bout_c", (128, 8), F32),
        ("L0_attn_lq1", (64,), F32),
        ("L0_attn_lk1", (64,), F32),
        ("L0_attn_lq2", (64,), F32),
        ("L0_attn_lk2", (64,), F32),
        ("_r0T1q", (128, 4096), F32),
        ("_r0T2q", (128, 4096), F32),
        ("_r0T1k", (128, 4096), F32),
        ("_r0T2k", (128, 4096), F32),
        ("_r0P1", (128, 128), F32),
        ("_r0P2", (128, 128), F32),
        ("_selkv", (2, 16), F32),
        ("_ind16", (128, 128), F32),
        ("_sel16", (16, 2048), F32),
    ],
    "L3_attn": [
        ("L3_attn_W_qkv", (1024, 3072), F32),
        ("L3_attn_b_qkv", (3072,), F32),
        ("L3_attn_b_qkv_c", (128, 24), F32),
        ("L3_attn_W_out", (1024, 1024), F32),
        ("L3_attn_b_out_c", (128, 8), F32),
        ("_ind16", (128, 128), F32),
        ("_sel16", (16, 2048), F32),
    ],
    "L3_ffn": [
        ("L3_ffn_wgu", (1024, 6144), I32),
        ("L3_ffn_wd", (3072, 1024), I32),
    ],
    "L1_ffn": [
        ("L1_ffn_W1", (1024, 4096), F32),
        ("L1_ffn_W2", (4096, 1024), F32),
        ("L1_ffn_quantum_phases_c", (128, 32), F32),
        ("L1_ffn_quantum_weights_real_c", (128, 32), F32),
        ("L1_ffn_quantum_weights_imag_c", (128, 32), F32),
        ("L1_ffn_b1_c", (128, 32), F32),
        ("L1_ffn_b2_c", (128, 8), F32),
    ],
}

EMITTERS = {
    "L2_attn": emit_L2_attn,
    "L2_ffn": emit_L2_ffn,
    "L0_ffn": emit_L0_ffn,
    "L1_attn": emit_L1_attn,
    "L0_attn": emit_L0_attn,
    "L1_ffn": emit_L1_ffn,
    "L3_ffn": emit_L3_ffn,
    "L3_attn": emit_L3_attn,
}


def prep_inputs_L1_ffn(inp):
    return {
        "L1_ffn_W1": np.ascontiguousarray(inp["L1_ffn_W1"], np.float32),
        "L1_ffn_W2": np.ascontiguousarray(inp["L1_ffn_W2"], np.float32),
        "L1_ffn_quantum_phases_c": col_layout(inp["L1_ffn_quantum_phases"]),
        "L1_ffn_quantum_weights_real_c": col_layout(inp["L1_ffn_quantum_weights_real"]),
        "L1_ffn_quantum_weights_imag_c": col_layout(inp["L1_ffn_quantum_weights_imag"]),
        "L1_ffn_b1_c": col_layout(inp["L1_ffn_b1"]),
        "L1_ffn_b2_c": col_layout(inp["L1_ffn_b2"]),
    }


def prep_inputs_L3_ffn(inp):
    wg = np.ascontiguousarray(inp["L3_ffn_w_gate"].T)
    wu = np.ascontiguousarray(inp["L3_ffn_w_up"].T)
    wgu = np.empty((1024, 6144), np.int32)
    for c in range(24):
        wgu[:, c * 256:c * 256 + 128] = wg[:, c * 128:(c + 1) * 128]
        wgu[:, c * 256 + 128:c * 256 + 256] = wu[:, c * 128:(c + 1) * 128]
    return {
        "L3_ffn_wgu": wgu,
        "L3_ffn_wd": np.ascontiguousarray(inp["L3_ffn_w_down"].T),
    }


def consts_attn():
    ind16 = np.zeros((128, 128), np.float32)
    sel16 = np.zeros((16, 2048), np.float32)
    for c in range(8):
        for p in range(128):
            ind16[p, c * 16 + 2 * c + p // 64] = 1.0
    for j in range(16):
        sel16[j, j * 128:(j + 1) * 128] = 1.0
    return {"_ind16": ind16, "_sel16": sel16}


def prep_inputs_L3_attn(inp):
    d = {
        "L3_attn_W_qkv": np.ascontiguousarray(inp["L3_attn_W_qkv"], np.float32),
        "L3_attn_b_qkv": np.ascontiguousarray(inp["L3_attn_b_qkv"], np.float32),
        "L3_attn_b_qkv_c": col_layout(inp["L3_attn_b_qkv"]),
        "L3_attn_W_out": np.ascontiguousarray(inp["L3_attn_W_out"], np.float32),
        "L3_attn_b_out_c": col_layout(inp["L3_attn_b_out"]),
    }
    d.update(consts_attn())
    return d


def prep_inputs_L0_attn(inp):
    pos = np.arange(4096, dtype=np.float64)[:, None]
    div = np.exp(np.arange(0, 64, 2, dtype=np.float64) * (-np.log(10000.0) / 64))
    sin = np.sin(pos * div).astype(np.float32)
    cos = np.cos(pos * div).astype(np.float32)
    T1 = np.zeros((128, 4096), np.float32)
    T2 = np.zeros((128, 4096), np.float32)
    for h in range(2):
        b = h * 64
        T1[b:b + 32] = cos.T
        T1[b + 32:b + 64] = sin.T
        T2[b:b + 32] = -sin.T
        T2[b + 32:b + 64] = cos.T
    P1 = np.zeros((128, 128), np.float32)
    P2 = np.zeros((128, 128), np.float32)
    for h in range(2):
        b = h * 64
        for j in range(32):
            P1[b + 2 * j, b + j] = 1.0
            P1[b + 2 * j, b + 32 + j] = 1.0
            P2[b + 2 * j + 1, b + j] = 1.0
            P2[b + 2 * j + 1, b + 32 + j] = 1.0
    selkv = np.zeros((2, 16), np.float32)
    for hh in range(16):
        selkv[hh % 2, hh] = 1.0
    d = {
        "L0_attn_Wq1": np.ascontiguousarray(inp["L0_attn_Wq1"], np.float32),
        "L0_attn_Wq2": np.ascontiguousarray(inp["L0_attn_Wq2"], np.float32),
        "L0_attn_Wk1": np.ascontiguousarray(inp["L0_attn_Wk1"], np.float32),
        "L0_attn_Wk2": np.ascontiguousarray(inp["L0_attn_Wk2"], np.float32),
        "L0_attn_Wv": np.ascontiguousarray(inp["L0_attn_Wv"], np.float32),
        "L0_attn_Wout": np.ascontiguousarray(inp["L0_attn_Wout"], np.float32),
        "L0_attn_bout_c": col_layout(inp["L0_attn_bout"]),
        "L0_attn_lq1": np.ascontiguousarray(inp["L0_attn_lq1"], np.float32),
        "L0_attn_lk1": np.ascontiguousarray(inp["L0_attn_lk1"], np.float32),
        "L0_attn_lq2": np.ascontiguousarray(inp["L0_attn_lq2"], np.float32),
        "L0_attn_lk2": np.ascontiguousarray(inp["L0_attn_lk2"], np.float32),
        "_r0T1q": T1 * 0.125, "_r0T2q": T2 * 0.125, "_r0T1k": T1, "_r0T2k": T2,
        "_r0P1": P1, "_r0P2": P2, "_selkv": selkv,
    }
    d.update(consts_attn())
    return d


def prep_inputs_L1_attn(inp):
    lam = 0.01
    i = np.arange(4096, dtype=np.float64)
    fqm = (np.exp(-lam * i) / 32.0).astype(np.float32)[None, :]
    fqp = (np.exp(lam * i) / 32.0).astype(np.float32)[None, :]
    fkm = np.exp(-lam * i).astype(np.float32)[None, :]
    fkp = np.exp(lam * i).astype(np.float32)[None, :]
    Dt = np.zeros((128, 2048), np.float32)
    kl = np.arange(128)[:, None]
    ql = np.arange(512)[None, :]
    for di in range(4):
        Dt[:, di * 512:(di + 1) * 512] = np.exp(-lam * np.abs(di * 128 + kl - ql)) / 32.0
    d = {
        "_l1fqm": fqm, "_l1fqp": fqp, "_l1fkm": fkm, "_l1fkp": fkp, "_l1D": Dt,
    }
    for n in ["W_q", "W_k", "W_v", "W_o", "W_down_kv", "W_up_k", "W_up_v"]:
        d["L1_attn_" + n] = np.ascontiguousarray(inp["L1_attn_" + n], np.float32)
    d.update(consts_attn())
    return d


def prep_inputs_L0_ffn(inp):
    d = {}
    W1 = np.asarray(inp["L0_ffn_W1"], np.float32)
    W2 = np.asarray(inp["L0_ffn_W2"], np.float32)
    b1 = np.asarray(inp["L0_ffn_b1"], np.float32)
    for i in range(6):
        d[f"L0_ffn_Wqkv_{i}"] = np.ascontiguousarray(inp["L0_ffn_Wqkv"][i], np.float32)
        d[f"L0_ffn_Wo_{i}"] = np.ascontiguousarray(inp["L0_ffn_Wo"][i], np.float32)
        a = np.zeros((1024, 2816), np.float32)
        g = np.zeros((1024, 2816), np.float32)
        a[:, :2730] = W1[i][:, :2730]
        g[:, :2730] = W1[i][:, 2730:]
        w1p = np.zeros((1024, 5632), np.float32)
        for c in range(22):
            w1p[:, c * 256:c * 256 + 128] = a[:, c * 128:(c + 1) * 128]
            w1p[:, c * 256 + 128:c * 256 + 256] = g[:, c * 128:(c + 1) * 128]
        d[f"L0_ffn_W1p_{i}"] = w1p
        w2p = np.zeros((2816, 1024), np.float32)
        w2p[:2730] = W2[i]
        d[f"L0_ffn_W2p_{i}"] = w2p
    b1a = np.zeros((128, 132), np.float32)
    b1g = np.zeros((128, 132), np.float32)
    for i in range(6):
        av = np.zeros(2816, np.float32)
        gv = np.zeros(2816, np.float32)
        av[:2730] = b1[i][:2730]
        gv[:2730] = b1[i][2730:]
        b1a[:, i * 22:(i + 1) * 22] = col_layout(av)
        b1g[:, i * 22:(i + 1) * 22] = col_layout(gv)
    d["L0_ffn_b1a_c"] = b1a
    d["L0_ffn_b1g_c"] = b1g
    d["L0_ffn_b2_c"] = np.concatenate(
        [col_layout(np.asarray(inp["L0_ffn_b2"][i], np.float32)) for i in range(6)], 1)
    d["L0_ffn_lng_r"] = np.ascontiguousarray(inp["L0_ffn_ln_g"], np.float32)
    d["L0_ffn_lnb_c"] = np.concatenate(
        [col_layout(np.asarray(inp["L0_ffn_ln_b"][i], np.float32)) for i in range(6)], 1)
    qs = np.asarray(inp["L0_ffn_q_scale"], np.float32)
    ks = np.asarray(inp["L0_ffn_k_scale"], np.float32)
    d["L0_ffn_qs_c"] = np.ascontiguousarray(np.tile(qs, (1, 2)))
    d["L0_ffn_ks_c"] = np.ascontiguousarray(np.tile(ks, (1, 2)))
    qs2 = np.roll(qs, 32, axis=1)
    ks2 = np.roll(ks, 32, axis=1)
    d["L0_ffn_qs2_c"] = np.ascontiguousarray(np.tile(qs2, (1, 2)))
    d["L0_ffn_qs_r"] = np.ascontiguousarray(qs)
    d["L0_ffn_ks_r"] = np.ascontiguousarray(ks)
    d["L0_ffn_ks2_c"] = np.ascontiguousarray(np.tile(ks2, (1, 2)))
    n, DH, W = 4096, 64, 256
    half = np.arange(0, DH, 2, dtype=np.float64)
    inv_freq = 1.0 / 10000.0 ** (half / DH)
    t = np.arange(n, dtype=np.float64)
    freqs = t[:, None] * inv_freq[None, :]
    freqs = np.concatenate([freqs, freqs], -1)
    base_scale = (half + 0.4 * DH) / (1.4 * DH)
    power = (t - n // 2) / (W // 2)
    scale = base_scale[None, :] ** power[:, None]
    scale = np.concatenate([scale, scale], -1)
    C1s = (np.cos(freqs) * scale * 8.0).astype(np.float32).T
    C2s = (np.sin(freqs) * scale * 8.0).astype(np.float32).T
    C1i = (np.cos(freqs) / scale).astype(np.float32).T
    C2i = (np.sin(freqs) / scale).astype(np.float32).T
    for nm, tab in [("_xC1s", C1s), ("_xC2s", C2s), ("_xC1i", C1i), ("_xC2i", C2i)]:
        d[nm] = np.ascontiguousarray(np.tile(tab, (2, 1)))
    rh = np.zeros((128, 128), np.float32)
    for hh in range(2):
        b_ = hh * 64
        for j in range(32):
            rh[b_ + 32 + j, b_ + j] = -1.0
            rh[b_ + j, b_ + 32 + j] = 1.0
    d["_rhP"] = rh
    selc = np.zeros((16, 1024), np.float32)
    for c in range(8):
        for p in range(128):
            selc[2 * c + p // 64, c * 128 + p] = 1.0
    d["_selc"] = selc
    d.update(consts_attn())
    return d


def prep_inputs_L2_attn(inp):
    d = {
        "L2_attn_qkv_wT": np.ascontiguousarray(np.asarray(inp["L2_attn_qkv_w"], np.float32).T),
        "L2_attn_qkv_b_c": col_layout(np.asarray(inp["L2_attn_qkv_b"], np.float32)),
        "L2_attn_out_wT": np.ascontiguousarray(np.asarray(inp["L2_attn_out_w"], np.float32).T),
        "L2_attn_out_b_c": col_layout(np.asarray(inp["L2_attn_out_b"], np.float32)),
    }
    d.update(consts_attn())
    selc = np.zeros((16, 1024), np.float32)
    for c in range(8):
        for p in range(128):
            selc[2 * c + p // 64, c * 128 + p] = 1.0
    d["_selc"] = selc
    return d


def prep_inputs_L2_ffn(inp):
    d = {
        "L2_ffn_Wr": np.ascontiguousarray(np.asarray(inp["L2_ffn_Wr"], np.float32)),
        "L2_ffn_br": np.ascontiguousarray(np.asarray(inp["L2_ffn_br"], np.float32)),
        "L2_ffn_b1_c": np.concatenate(
            [col_layout(np.asarray(inp["L2_ffn_b1"][e], np.float32)) for e in range(8)], 1),
        "L2_ffn_b2r": np.ascontiguousarray(
            np.asarray(inp["L2_ffn_b2"], np.float32).reshape(1, 8192)),
    }
    for e in range(8):
        d[f"L2_ffn_W1_{e}"] = np.ascontiguousarray(np.asarray(inp["L2_ffn_W1"][e], np.float32))
        d[f"L2_ffn_W2_{e}"] = np.ascontiguousarray(np.asarray(inp["L2_ffn_W2"][e], np.float32))
    return d


PREP = {"L2_attn": prep_inputs_L2_attn, "L2_ffn": prep_inputs_L2_ffn,
        "L1_ffn": prep_inputs_L1_ffn, "L3_ffn": prep_inputs_L3_ffn,
        "L3_attn": prep_inputs_L3_attn, "L0_attn": prep_inputs_L0_attn,
        "L1_attn": prep_inputs_L1_attn, "L0_ffn": prep_inputs_L0_ffn}


def run_blocks(block_list, x, inp, cores=None):
    B = x.shape[0]
    cores = cores if cores is not None else list(range(B))
    nc = build_program(block_list, None)
    common = {}
    for b in block_list:
        common.update(PREP[b](inp))
    in_maps = []
    for c in range(len(cores)):
        m = dict(common)
        m["x_fm"] = np.ascontiguousarray(x[c].T)
        if "L2_attn" in block_list:
            cm = np.zeros((16, 8), np.float32)
            cm[:, :c + 1] = 1.0
            m["_l2cm"] = cm
        if "L2_ffn" in block_list:
            pf = np.zeros((8, 1), np.float32)
            pf[:c] = 1.0
            m["_l2pfx"] = pf
        in_maps.append(m)
    res = run_bass_kernel_spmd(nc, in_maps, core_ids=list(range(len(cores))))
    return np.stack([r["xout"].T for r in res.results])


ALL_BLOCKS = ["L0_attn", "L0_ffn", "L1_attn", "L1_ffn",
              "L2_attn", "L2_ffn", "L3_attn", "L3_ffn"]


def kernel(**inputs):
    x = np.asarray(inputs["L0_attn_x"], np.float32)
    return run_blocks(ALL_BLOCKS, x, inputs)


# version 03 (final)
# speedup vs baseline: 1.0139x; 1.0139x over previous
import numpy as np
import ml_dtypes
from contextlib import ExitStack
import concourse.bass as bass
import concourse.mybir as mybir
from concourse.bass_utils import run_bass_kernel_spmd

F32 = mybir.dt.float32
BF16 = mybir.dt.bfloat16
I32 = mybir.dt.int32
AF = mybir.ActivationFunctionType
ALU = mybir.AluOpType
AX = mybir.AxisListType.X
BF = ml_dtypes.bfloat16

D = 1024
T = 4096
NCH = 8
XBW = 4104
XBO = 4
TT = 512
NTT = T // TT

ENGINES = ("t", "s", "v", "g", "q")


def cdiv(a, b):
    return (a + b - 1) // b


def chunks_of(n, c=128):
    return [(i, min(c, n - i)) for i in range(0, n, c)]


class KB:

    def __init__(self):
        self.ops = {e: [] for e in ENGINES}
        self.cnt = {}
        self.pend = {e: {} for e in ENGINES}

    def _sem(self, s):
        if s not in self.cnt:
            self.cnt[s] = 0

    def on(self, e, fn, deps=(), inc=None, inc_amt=1):
        waits = dict(self.pend[e])
        self.pend[e] = {}
        for d in deps:
            if d is None:
                continue
            s, v = d
            if v > waits.get(s, -1):
                waits[s] = v
        tok = None
        incs = []
        if inc is not None:
            self._sem(inc)
            self.cnt[inc] += inc_amt
            incs.append((inc, inc_amt))
            tok = (inc, self.cnt[inc])
        self.ops[e].append((fn, sorted(waits.items()), incs))
        return tok

    def tok(self, s):
        self._sem(s)
        return (s, self.cnt[s])

    def dma(self, e, out_ap, in_ap, deps=()):
        return self.on(e, lambda eng, o=out_ap, i=in_ap: eng.dma_start(o, i),
                       deps, inc="d" + e, inc_amt=16)

    def fence(self, engines=ENGINES):
        snap = list(self.cnt.items())
        for e in engines:
            for s, c in snap:
                if c > self.pend[e].get(s, -1):
                    self.pend[e][s] = c

    def emit(self, nc, es):
        sems = {s: es.enter_context(nc.semaphore(s)) for s in self.cnt}
        block = es.enter_context(nc.Block())

        def runner(ename):
            def go(eng):
                for fn, waits, incs in self.ops[ename]:
                    for s, v in waits:
                        eng.wait_ge(sems[s], v)
                    ins = fn(eng)
                    for s, amt in incs:
                        ins.then_inc(sems[s], amt)
            return go
        block.tensor(runner("t"))
        block.scalar(runner("s"))
        block.vector(runner("v"))
        block.gpsimd(runner("g"))
        block.sync(runner("q"))


class Pool:
    def __init__(self, tensors):
        self.slots = [[tns, None] for tns in tensors]
        self.i = 0

    def take(self):
        slot = self.slots[self.i % len(self.slots)]
        self.i += 1
        return slot


class Ctx:
    def __init__(self, nc, kb, es):
        self.nc, self.kb, self.es = nc, kb, es
        sb = lambda name, shape, dt: es.enter_context(nc.sbuf_tensor(name, shape, dt))
        self.BIG = sb("BIG", [128, 49152], BF16)
        self.FW = sb("FW", [128, 4096], F32)
        self.FW2 = sb("FW2", [128, 2048], F32)
        self.CAR = sb("CAR", [128, 16384], BF16)
        self.CF = sb("CF", [128, 3072], F32)
        self.STATS = sb("STATS", [16, 8192], F32)
        self.ITMP = sb("ITMP", [128, 32], I32)
        self.ones_colf = sb("ones_colf", [128, 1], F32)
        self.ones_rowf = sb("ones_rowf", [1, 128], F32)
        self.ones_row = sb("ones_row", [1, 128], BF16)
        self.ones_colb = sb("ones_colb", [128, 1], BF16)
        self.ident = sb("ident", [128, 128], BF16)
        self.identf = sb("identf", [128, 128], F32)
        self.ps = [es.enter_context(nc.psum_tensor(f"ps{i}", [128, 512], F32))
                   for i in range(8)]
        self.pool = Pool(self.ps[:5])
        self.spool = Pool(self.ps[5:7])
        self.opool = Pool(self.ps[7:8])
        kb._sem("init")
        self.wg_users = {}

    def setup(self):
        kb = self.kb
        last = [None]

        def g(fn):
            last[0] = kb.on("g", fn, deps=[last[0]], inc="init")
        g(lambda gg: gg.memset(self.ones_colf[:, :], 1.0))
        g(lambda gg: gg.memset(self.ones_rowf[:, :], 1.0))
        g(lambda gg: gg.memset(self.ones_row[:, :], 1.0))
        g(lambda gg: gg.memset(self.ones_colb[:, :], 1.0))
        g(lambda gg: gg.memset(self.ident[:, :], 1.0))
        g(lambda gg: gg.affine_select(self.ident[:, :], self.ident[:, :],
          pattern=[[1, 128]], compare_op=ALU.is_equal, fill=0.0, base=0,
          channel_multiplier=-1))
        g(lambda gg: gg.memset(self.identf[:, :], 1.0))
        g(lambda gg: gg.affine_select(self.identf[:, :], self.identf[:, :],
          pattern=[[1, 128]], compare_op=ALU.is_equal, fill=0.0, base=0,
          channel_multiplier=-1))
        g(lambda gg: gg.memset(self.CF[:, 3040:3041], np.pi / 2))
        g(lambda gg: gg.memset(self.CF[:, 3041:3042], 1e-5))
        self.cpi2 = self.CF[:, 3040:3041]
        self.ceps = self.CF[:, 3041:3042]
        self.init_tok = ("init", kb.cnt["init"])
        self.kb.fence()


def load_w_group(ctx, W_ap, din, g0, gw, slab_off, deps=()):
    kb = ctx.kb
    chs = chunks_of(din)
    toks = list(deps)
    nfull = sum(1 for _, sz in chs if sz == 128)
    out_toks = []
    if nfull:
        src = W_ap[0:nfull * 128, g0:g0 + gw].rearrange("(n p) w -> p n w", p=128)
        dst = ctx.BIG[:, slab_off:slab_off + nfull * gw].rearrange("p (n w) -> p n w", w=gw)
        out_toks.append(kb.dma("g", dst, src, deps))
    for ci, (st, sz) in enumerate(chs):
        if sz == 128:
            continue
        src = W_ap[st:st + sz, g0:g0 + gw]
        dst = ctx.BIG[0:sz, slab_off + ci * gw:slab_off + ci * gw + gw]
        out_toks.append(kb.dma("g", dst, src, deps))
    return out_toks


def slab_lhsT(ctx, slab_off, gw, ci, c0, cw, psz):
    return ctx.BIG[0:psz, slab_off + ci * gw + c0:slab_off + ci * gw + c0 + cw]


def mmA(ctx, tag, W_ap, din, dout, xsrc, evac, slab_off, gw=512, tt=TT,
        ntok=T, extra_deps=()):
    kb = ctx.kb
    chs = chunks_of(din)
    slab_sz = len(chs) * gw
    for g in range(cdiv(dout, gw)):
        g0, gww = g * gw, min(gw, dout - g * gw)
        par = g % 2
        off = slab_off + par * slab_sz
        deps = list(extra_deps)
        pu = ctx.wg_users.get((tag, par))
        if pu is not None:
            deps.append(pu)
        wtoks = load_w_group(ctx, W_ap, din, g0, gww, off, deps)
        last_mm = None
        for dc0, dcw in chunks_of(gww):
            for ti in range(cdiv(ntok, tt)):
                t0 = ti * tt
                slot = ctx.pool.take()
                ps = slot[0]
                for ci, (cst, csz) in enumerate(chs):
                    xa, xd = xsrc(ci, t0, tt)
                    mdeps = (wtoks + [slot[1]] + list(xd)) if ci == 0 else ()
                    fn = (lambda te, o=ps[0:dcw, 0:tt],
                          l=slab_lhsT(ctx, off, gww, ci, dc0, dcw, csz), r=xa,
                          st=(ci == 0), sp=(ci == len(chs) - 1):
                          te.matmul(o, l, r, start=st, stop=sp))
                    tok = kb.on("t", fn, deps=mdeps,
                                inc="t" if ci == len(chs) - 1 else None)
                    if ci == len(chs) - 1:
                        last_mm = tok
                evac(g0 + dc0, dcw, t0, tt, slot, [last_mm])
        ctx.wg_users[(tag, par)] = last_mm


def mmB(ctx, tag, W_ap, din, dout, xload, evac, woff, xoff, tt=TT, ntok=T,
        extra_deps=(), wdeps_extra=(), xbufs=1):
    kb = ctx.kb
    chs = chunks_of(din)
    wtoks = load_w_group(ctx, W_ap, din, 0, dout, woff, list(wdeps_extra))
    xfree = {}
    for ti in range(cdiv(ntok, tt)):
        t0 = ti * tt
        par = ti % xbufs
        xtoks = []
        xaps = []
        for ci, (cst, csz) in enumerate(chs):
            src, dst, deps = xload(ci, t0, tt, par)
            if src is not None:
                xtoks.append(kb.dma("q", dst, src,
                                    list(deps) + ([xfree[par]] if par in xfree else [])))
            else:
                xtoks += list(deps)
            xaps.append((dst, csz))
        last_mm = None
        for dc0, dcw in chunks_of(dout):
            slot = ctx.pool.take()
            ps = slot[0]
            for ci, (cst, csz) in enumerate(chs):
                mdeps = (wtoks + xtoks + [slot[1]] + list(extra_deps)) if ci == 0 else ()
                fn = (lambda te, o=ps[0:dcw, 0:tt],
                      l=slab_lhsT(ctx, woff, dout, ci, dc0, dcw, csz),
                      r=xaps[ci][0],
                      st=(ci == 0), sp=(ci == len(chs) - 1):
                      te.matmul(o, l, r, start=st, stop=sp))
                tok = kb.on("t", fn, deps=mdeps,
                            inc="t" if ci == len(chs) - 1 else None)
                if ci == len(chs) - 1:
                    last_mm = tok
            evac(dc0, dcw, t0, tt, slot, [last_mm])
        xfree[par] = last_mm


class PostLN:

    def __init__(self, ctx, X, XB, bias_col=None, scale=1.0):
        self.ctx, self.X, self.XB = ctx, X, XB
        self.bias_col = bias_col
        self.scale = scale
        self.fw_free = {}
        self.x_free = {}
        self.stat_done = []

    def evac(self, dc0, dcw, t0, tw, slot, deps):
        ctx, kb = self.ctx, self.ctx.kb
        ps = slot[0]
        dc = dc0 // 128
        s_tile = ctx.FW[0:dcw, (dc % 4) * 512:(dc % 4) * 512 + tw]
        sq_tile = ctx.FW[0:dcw, 2048 + (dc % 4) * 512:2048 + (dc % 4) * 512 + tw]
        xt = ctx.FW2[0:dcw, (dc % 4) * 512:(dc % 4) * 512 + tw]
        ltok = kb.dma("q", xt, self.X[dc0:dc0 + dcw, t0:t0 + tw],
                      deps=[self.x_free.get(dc % 4)])
        b = 0.0 if self.bias_col is None else self.bias_col(dc0, dcw)
        t1 = kb.on("s", lambda s, o=s_tile, i=ps[0:dcw, 0:tw], bb=b, sc=self.scale:
                   s.activation(o, i, AF.Identity, bias=bb, scale=sc),
                   list(deps) + (self.fw_free.get(dc % 4) or []), inc="s")
        t2 = kb.on("v", lambda v, o=s_tile, a=s_tile, x=xt: v.tensor_add(o, a, x),
                   [t1, ltok], inc="v")
        self.x_free[dc % 4] = t2
        slot[1] = t2
        st1 = kb.dma("q", self.X[dc0:dc0 + dcw, t0:t0 + tw], s_tile, [t2])
        t3 = kb.on("s", lambda s, o=sq_tile, i=s_tile: s.activation(o, i, AF.Square),
                   [t2], inc="s")
        if dc == 0:
            self.sA = ctx.spool.take()
            self.sB = ctx.spool.take()
        first = (dc == 0)
        dA = [self.sA[1]] if first else []
        dB = [self.sB[1]] if first else []
        ts1 = kb.on("t", lambda te, o=self.sA[0][0:1, 0:tw], l=ctx.ones_colf[0:dcw, 0:1],
                    r=s_tile: te.matmul(o, l, r, start=first, stop=(dc == NCH - 1)),
                    [t2] + dA, inc="t")
        ts2 = kb.on("t", lambda te, o=self.sB[0][0:1, 0:tw], l=ctx.ones_colf[0:dcw, 0:1],
                    r=sq_tile: te.matmul(o, l, r, start=first, stop=(dc == NCH - 1)),
                    [t3] + dB, inc="t")
        self.fw_free[dc % 4] = [st1, ts2]
        if dc == NCH - 1:
            ST = self.ctx.STATS
            c1 = kb.on("v", lambda v, o=ST[0:1, t0:t0 + tw], i=self.sA[0][0:1, 0:tw]:
                       v.tensor_copy(o, i), [ts1, ts2], inc="v")
            c2 = kb.on("v", lambda v, o=ST[0:1, 4096 + t0:4096 + t0 + tw],
                       i=self.sB[0][0:1, 0:tw]: v.tensor_copy(o, i), [c1], inc="v")
            self.sA[1] = c1
            self.sB[1] = c2
            self.stat_done.append(c2)

    def apply(self, gain_row=None, bias_col=None, xb_extra=None):
        ctx, kb = self.ctx, self.ctx.kb
        ST = ctx.STATS
        deps = list(self.stat_done)
        t1 = kb.on("v", lambda v: v.tensor_scalar_mul(ST[0:1, 0:T], ST[0:1, 0:T], 1.0 / D),
                   deps, inc="v")
        t2 = kb.on("v", lambda v: v.tensor_scalar_mul(ST[0:1, 4096:4096 + T],
                   ST[0:1, 4096:4096 + T], 1.0 / D), [t1], inc="v")
        mrow = ctx.FW[0:1, 0:T]
        t3 = kb.on("v", lambda v: v.tensor_mul(mrow, ST[0:1, 0:T], ST[0:1, 0:T]),
                   [t2], inc="v")
        t4 = kb.on("v", lambda v: v.tensor_sub(mrow, ST[0:1, 4096:4096 + T], mrow),
                   [t3], inc="v")
        t5 = kb.on("s", lambda s: s.activation(mrow, mrow, AF.Sqrt, bias=ctx.ceps[0:1, 0:1]),
                   [t4], inc="s")
        t6 = kb.on("v", lambda v: v.reciprocal(mrow, mrow), [t5], inc="v")
        t7 = kb.on("v", lambda v: v.tensor_mul(ST[0:1, 0:T], ST[0:1, 0:T], mrow),
                   [t6], inc="v")
        kb.fence()
        xfree = {}
        for dc in range(NCH):
            d0 = dc * 128
            grow = (ctx.ones_rowf[0:1, 0:128] if gain_row is None
                    else gain_row(d0, 128))
            for ti in range(NTT):
                t0 = ti * TT
                slot = ctx.spool.take()
                sps = slot[0]
                m1 = kb.on("t", lambda te, o=sps[:, 0:TT], l=grow,
                           r=mrow[0:1, t0:t0 + TT]: te.matmul(o, l, r, start=True, stop=True),
                           [slot[1]], inc="t")
                pslot = ctx.pool.take()
                pps = pslot[0]
                m2 = kb.on("t", lambda te, o=pps[:, 0:TT], l=grow,
                           r=ST[0:1, t0:t0 + TT]: te.matmul(o, l, r, start=True, stop=True),
                           [pslot[1]], inc="t")
                xt = ctx.FW2[:, (dc % 4) * 512:(dc % 4) * 512 + TT]
                lt = kb.dma("q", xt, self.X[d0:d0 + 128, t0:t0 + TT],
                            deps=[xfree.get(dc % 4)])
                v1 = kb.on("v", lambda v, o=xt, a=xt, b=sps[:, 0:TT]:
                           v.tensor_mul(o, a, b), [m1, lt], inc="v")
                bcol = 0.0 if bias_col is None else bias_col(d0, 128)
                v2 = kb.on("v", lambda v, o=xt, a=xt, sc=bcol, b=pps[:, 0:TT]:
                           v.scalar_tensor_tensor(o, a, sc, b, ALU.add, ALU.subtract),
                           [m2, v1], inc="v")
                slot[1] = v2
                pslot[1] = v2
                st1 = kb.dma("q", self.X[d0:d0 + 128, t0:t0 + TT], xt, [v2])
                st2 = kb.dma("g", self.XB[:, dc * XBW + XBO + t0:dc * XBW + XBO + t0 + TT],
                             xt, [v2])
                if xb_extra is not None:
                    xb_extra(dc, t0, xt, [v2])
                xfree[dc % 4] = kb.tok("dg")
        kb.fence()


XRES_SZ = NCH * XBW


def xres_load(ctx, xoff, XB, deps=()):
    kb = ctx.kb
    return kb.dma("q", ctx.BIG[:, xoff:xoff + NCH * XBW], XB[:, 0:NCH * XBW], deps)


def xres_src(ctx, xoff, shift_hi=False):
    def src(ci, t0, tw):
        off = xoff + ci * XBW + XBO + t0
        if shift_hi and ci >= 4:
            off -= 1
        return ctx.BIG[:, off:off + tw], []
    return src


def emit_L1_ffn(ctx, io, X, XB, H):
    nc, kb = ctx.nc, ctx.kb
    DF = 4096
    thc = ctx.CF[:, 2048:2048 + 32]
    wrc = ctx.CF[:, 2080:2080 + 32]
    wic = ctx.CF[:, 2112:2112 + 32]
    P1 = ctx.CF[:, 2144:2144 + 32]
    P2 = ctx.CF[:, 2176:2176 + 32]
    b1c = ctx.CF[:, 2208:2208 + 32]
    b2c = ctx.CF[:, 2240:2240 + 8]
    tmp = ctx.CF[:, 2248:2248 + 32]
    l1 = kb.dma("q", thc, io["L1_ffn_quantum_phases_c"])
    l2 = kb.dma("q", wrc, io["L1_ffn_quantum_weights_real_c"])
    l3 = kb.dma("q", wic, io["L1_ffn_quantum_weights_imag_c"])
    l4 = kb.dma("q", b1c, io["L1_ffn_b1_c"])
    l5 = kb.dma("q", b2c, io["L1_ffn_b2_c"])
    pi2 = 2 * np.pi
    iti = ctx.ITMP[:, 0:32]
    r0 = kb.on("v", lambda v: v.tensor_scalar_mul(tmp, thc, 1.0 / pi2), [l1], inc="v")
    ra = kb.on("v", lambda v: v.tensor_copy(iti, tmp), [r0], inc="v")
    rb = kb.on("v", lambda v: v.tensor_copy(tmp, iti), [ra], inc="v")
    r1 = kb.on("v", lambda v: v.scalar_tensor_tensor(thc, tmp, -pi2, thc, ALU.mult, ALU.add),
               [rb], inc="v")
    r2a = kb.on("v", lambda v: v.tensor_single_scalar(tmp, thc, np.pi, ALU.is_gt), [r1], inc="v")
    r2b = kb.on("v", lambda v: v.scalar_tensor_tensor(thc, tmp, -pi2, thc, ALU.mult, ALU.add),
                [r2a], inc="v")
    r2c = kb.on("v", lambda v: v.tensor_single_scalar(tmp, thc, -np.pi, ALU.is_lt), [r2b], inc="v")
    r3 = kb.on("v", lambda v: v.scalar_tensor_tensor(thc, tmp, pi2, thc, ALU.mult, ALU.add),
               [r2c], inc="v")
    snc = ctx.CF[:, 2280:2280 + 32]
    csc = ctx.CF[:, 2312:2312 + 32]
    r4 = kb.on("s", lambda s: s.activation(snc, thc, AF.Sin), [r3], inc="s")
    r5 = kb.on("s", lambda s: s.activation(csc, thc, AF.Sin, bias=ctx.cpi2), [r3], inc="s")
    r6 = kb.on("v", lambda v: v.tensor_mul(tmp, csc, wrc), [r5, l2], inc="v")
    r7 = kb.on("v", lambda v: v.scalar_tensor_tensor(P1, snc, 1.0, wic, ALU.mult, ALU.mult),
               [r4, l3], inc="v")
    r8 = kb.on("v", lambda v: v.tensor_add(tmp, tmp, P1), [r6, r7], inc="v")
    r9 = kb.on("v", lambda v: v.scalar_tensor_tensor(P1, tmp, 0.1, wic, ALU.mult, ALU.mult),
               [r8], inc="v")
    rA = kb.on("v", lambda v: v.tensor_mul(P2, P1, csc), [r9], inc="v")
    rB = kb.on("v", lambda v: v.tensor_mul(P1, P1, snc), [rA], inc="v")
    kb.fence()

    xoff = 0
    xtok = xres_load(ctx, xoff, XB)
    xsrc = xres_src(ctx, xoff)
    hfree = {}
    b1s = ctx.CF[:, 2344:2344 + 32]
    b1sc = ctx.CF[:, 2376:2376 + 32]
    u1 = kb.on("v", lambda v: v.tensor_scalar_mul(b1s, b1c, 0.1), [l4], inc="v")
    kb.on("v", lambda v: v.tensor_scalar_add(b1sc, b1s, np.pi / 2), [u1], inc="v")
    kb.fence()

    def evac1(d0, dw, t0, tw, slot, deps):
        ps = slot[0]
        hc = d0 // 128
        par = ctx._l1h_par = (getattr(ctx, "_l1h_par", 0) + 1) % 2
        boff = XRES_SZ + par * 2048
        cosb = ctx.BIG[0:dw, boff:boff + tw]
        sinb = ctx.BIG[0:dw, boff + 512:boff + 512 + tw]
        qb = ctx.BIG[0:dw, boff + 1024:boff + 1024 + tw]
        dd = list(deps) + [hfree.get(par)]
        t1 = kb.on("s", lambda s, o=cosb, i=ps[0:dw, 0:tw], bc=b1sc[0:dw, hc:hc + 1]:
                   s.activation(o, i, AF.Sin, bias=bc, scale=0.1), dd, inc="s")
        t2 = kb.on("s", lambda s, o=sinb, i=ps[0:dw, 0:tw], bc=b1s[0:dw, hc:hc + 1]:
                   s.activation(o, i, AF.Sin, bias=bc, scale=0.1), [t1], inc="s")
        t3 = kb.on("v", lambda v, o=qb, a=cosb, sc=P1[0:dw, hc:hc + 1], b=ps[0:dw, 0:tw]:
                   v.scalar_tensor_tensor(o, a, sc, b, ALU.mult, ALU.add), [t2], inc="v")
        t4 = kb.on("v", lambda v, o=qb, a=sinb, sc=P2[0:dw, hc:hc + 1], b=qb:
                   v.scalar_tensor_tensor(o, a, sc, b, ALU.mult, ALU.add), [t3], inc="v")
        slot[1] = t4
        t5 = kb.on("g", lambda g, o=qb, a=qb, bc=b1c[0:dw, hc:hc + 1]:
                   g.tensor_scalar(o, a, bc, 0.0, ALU.add, ALU.max), [t4], inc="g")
        htok = kb.dma("g", H[d0:d0 + dw, t0:t0 + tw], qb, [t5])
        hfree[par] = htok

    mmA(ctx, "l1w1", io["L1_ffn_W1"], D, DF, xsrc, evac1,
        slab_off=XRES_SZ + 4096, gw=512, extra_deps=[xtok])
    kb.fence()

    pl = PostLN(ctx, X, XB, bias_col=lambda d0, dw: b2c[0:dw, d0 // 128:d0 // 128 + 1])

    def xload(ci, t0, tw, par):
        dst = ctx.BIG[:, 32768 + ci * 512:32768 + ci * 512 + tw]
        return H[ci * 128:ci * 128 + 128, t0:t0 + tw], dst, []

    mmB(ctx, "l1w2", io["L1_ffn_W2"], DF, D, xload, pl.evac, woff=0, xoff=32768)
    kb.fence()
    pl.apply()


def norms_finish(ctx, deps, blkw):
    kb = ctx.kb
    nb = T // blkw
    ST, FW = ctx.STATS, ctx.FW
    t1 = kb.on("s", lambda s: s.activation(ST[0:16, 0:8192], ST[0:16, 0:8192], AF.Sqrt),
               deps, inc="s")
    qb = FW[0:16, 3072:3072 + nb]
    kbx = FW[0:16, 3072 + nb:3072 + 2 * nb]
    t2 = kb.on("v", lambda v: v.reduce_max(qb, ST[0:16, 0:4096].rearrange(
        "h (b w) -> h b w", w=blkw), axis=AX), [t1], inc="v")
    t3 = kb.on("v", lambda v: v.reduce_max(kbx, ST[0:16, 4096:8192].rearrange(
        "h (b w) -> h b w", w=blkw), axis=AX), [t2], inc="v")
    return qb, kbx, [t3]


def negm_col(ctx, sel_ap, mcol_ap, dst_col, bank, deps, extra_scale=-1.0):
    kb = ctx.kb
    m1 = kb.on("t", lambda te, o=bank[0][0:128, 0:1], l=sel_ap, r=mcol_ap:
               te.matmul(o, l, r, start=True, stop=True), list(deps) + [bank[1]], inc="t")
    c1 = kb.on("s", lambda sE, o=dst_col, i=bank[0][0:128, 0:1], sc=extra_scale:
               sE.activation(o, i, AF.Identity, scale=sc), [m1], inc="s")
    bank[1] = c1
    return c1


def emit_L3_ffn(ctx, io, X, XB, H):
    nc, kb = ctx.nc, ctx.kb
    s127 = 1.0 / 127.0
    xoff = 0
    xtok = xres_load(ctx, xoff, XB)
    xsrc = xres_src(ctx, xoff)
    stash_off = XRES_SZ
    hb_off = XRES_SZ + 4096
    slab_off = XRES_SZ + 4096 + 1024
    hfree = {}
    stash_free = {}

    def evac(d0, dw, t0, tw, slot, deps):
        ps = slot[0]
        dc = d0 // 128
        hc = dc // 2
        ti = t0 // TT
        stash = ctx.BIG[0:dw, stash_off + ti * 512:stash_off + ti * 512 + tw]
        if dc % 2 == 0:
            tok = kb.on("s", lambda sE, o=stash, i=ps[0:dw, 0:tw]:
                        sE.activation(o, i, AF.Silu, scale=s127),
                        list(deps) + (stash_free.get(ti) or []), inc="s")
            slot[1] = tok
            stash_free[ti] = [tok]
        else:
            par = (hc * NTT + ti) % 2
            hb = ctx.BIG[0:dw, hb_off + par * 512:hb_off + par * 512 + tw]
            tok = kb.on("v", lambda v, o=hb, a=ps[0:dw, 0:tw], b=stash:
                        v.tensor_mul(o, a, b),
                        list(deps) + [hfree.get(par)] + (stash_free.get(ti) or []),
                        inc="v")
            slot[1] = tok
            stash_free[ti] = (stash_free.get(ti) or []) + [tok]
            htok = kb.dma("g", H[hc * 128:hc * 128 + dw, t0:t0 + tw], hb, [tok])
            hfree[par] = htok

    mmA(ctx, "l3gu", io["L3_ffn_wgu"], D, 6144, xsrc, evac,
        slab_off=slab_off, gw=512, extra_deps=[xtok])
    kb.fence()

    b0 = ctx.CF[:, 2240:2248]
    z = kb.on("g", lambda g: g.memset(ctx.CF[:, 2240:2248], 0.0), inc="g")
    kb.fence()
    pl = PostLN(ctx, X, XB, scale=s127 * s127)

    def xload(ci, t0, tw, par):
        dst = ctx.BIG[:, 24576 + ci * 512:24576 + ci * 512 + tw]
        return H[ci * 128:ci * 128 + 128, t0:t0 + tw], dst, []

    mmB(ctx, "l3dn", io["L3_ffn_wd"], 3072, D, xload, pl.evac, woff=0, xoff=24576)
    kb.fence()
    pl.apply()


def emit_L3_attn(ctx, io, X, XB, H):
    nc, kb = ctx.nc, ctx.kb
    QS, KS, VS, AO = io["_QS"], io["_KS"], io["_VS"], io["_AO"]
    bqkv = ctx.CF[:, 2048:2048 + 24]
    lb = kb.dma("q", bqkv, io["L3_attn_b_qkv_c"])
    bq8 = ctx.CF[:, 2072:2072 + 8]
    u0 = kb.on("v", lambda v: v.tensor_scalar_mul(bq8, ctx.CF[:, 2048:2056], 0.125),
               [lb], inc="v")
    ind16 = ctx.CF[:, 2112:2112 + 128]
    sel = ctx.FW[0:16, 1024:1024 + 2048]
    li = kb.dma("q", ind16, io["_ind16"])
    ls = kb.dma("q", sel, io["_sel16"])
    bvf = ctx.CF[0:1, 2040:2048]
    bvrow_f = ctx.CF[0:1, 1024:2048]
    lv = kb.dma("q", bvrow_f, io["L3_attn_b_qkv"].rearrange("(a n) -> a n", a=1)[0:1, 2048:3072])
    bvrow = ctx.CAR[0:1, 0:1024]
    uv = kb.on("v", lambda v: v.tensor_copy(bvrow, bvrow_f), [lv], inc="v")
    kb.fence()
    wtoks = load_w_group(ctx, io["L3_attn_W_qkv"], D, 0, 3072, 0)
    xoff, stg = 24576, 33280
    xfree = {}
    stg_free = {}
    for ti in range(NTT):
        t0 = ti * TT
        par = ti % 2
        xo = xoff + par * 4096
        src = XB[:, 0:NCH * XBW].rearrange("p (c t) -> p c t", t=XBW)[:, :, XBO + t0:XBO + t0 + TT]
        dst = ctx.BIG[:, xo:xo + 4096].rearrange("p (c t) -> p c t", t=TT)
        xt = kb.dma("q", dst, src, [xfree.get(par)])
        xa = lambda ci, xo=xo: ctx.BIG[:, xo + ci * TT:xo + ci * TT + TT]
        nq = ctx.spool.take()
        nk = ctx.spool.take()
        for dc in range(16):
            slot = ctx.pool.take()
            ps = slot[0]
            for ci in range(8):
                mdeps = (wtoks + [xt, slot[1]]) if ci == 0 else ()
                tok = kb.on("t", lambda te, o=ps[:, 0:TT],
                            l=slab_lhsT(ctx, 0, 3072, ci, dc * 128, 128, 128),
                            r=xa(ci), st=(ci == 0), sp=(ci == 7):
                            te.matmul(o, l, r, start=st, stop=sp),
                            mdeps, inc="t" if ci == 7 else None)
            isq = dc < 8
            c = dc % 8
            stgb = ctx.BIG[:, stg + (dc % 2) * 512:stg + (dc % 2) * 512 + TT]
            sq = ctx.FW[:, (dc % 2) * 512:(dc % 2) * 512 + TT]
            bias = bq8[:, c:c + 1] if isq else bqkv[:, 8 + c:9 + c]
            e1 = kb.on("s", lambda sE, o=stgb, i=ps[:, 0:TT], bb=bias,
                       sc=(0.125 if isq else 1.0):
                       sE.activation(o, i, AF.Identity, bias=bb, scale=sc),
                       [tok, u0] + (stg_free.get(dc % 2) or []), inc="s")
            d1 = kb.dma("g", (QS if isq else KS)[c * 128:c * 128 + 128, t0:t0 + TT],
                        stgb, [e1])
            e2 = kb.on("s", lambda sE, o=sq, i=stgb: sE.activation(o, i, AF.Square),
                       [e1], inc="s")
            npsum = nq if isq else nk
            first = (c == 0)
            n1 = kb.on("t", lambda te, o=npsum[0][0:16, 0:TT],
                       l=ind16[:, c * 16:c * 16 + 16], r=sq,
                       st=first, sp=(c == 7): te.matmul(o, l, r, start=st, stop=sp),
                       [e2, li] + ([npsum[1]] if first else []), inc="t")
            stg_free[dc % 2] = [d1, e2]
            slot[1] = e1
            if dc == 7:
                cq = kb.on("v", lambda v, o=ctx.STATS[0:16, t0:t0 + TT],
                           i=nq[0][0:16, 0:TT]: v.tensor_copy(o, i), [n1], inc="v")
                nq[1] = cq
            if dc == 15:
                ck = kb.on("v", lambda v, o=ctx.STATS[0:16, 4096 + t0:4096 + t0 + TT],
                           i=nk[0][0:16, 0:TT]: v.tensor_copy(o, i), [n1], inc="v")
                nk[1] = ck
        for sub in range(4):
            for sl in range(2):
                slot = ctx.pool.take()
                ps = slot[0]
                kb.on("t", lambda te, o=ps[:, 0:TT], l=ctx.ones_row[0:1, 0:128],
                      r=bvrow[0:1, sl * 512:sl * 512 + 512]:
                      te.matmul(o, l, r, start=True, stop=False),
                      [slot[1], uv], inc=None)
                for ci in range(8):
                    tok = kb.on("t", lambda te, o=ps[:, 0:TT],
                                l=ctx.BIG[:, xo + ci * TT + sub * 128:
                                          xo + ci * TT + sub * 128 + 128],
                                r=slab_lhsT(ctx, 0, 3072, ci, 2048 + sl * 512, TT, 128),
                                sp=(ci == 7): te.matmul(o, l, r, start=False, stop=sp),
                                (), inc="t" if ci == 7 else None)
                stgb = ctx.BIG[:, stg + 1024 + (sub % 2) * 512:stg + 1024 + (sub % 2) * 512 + TT]
                e1 = kb.on("s", lambda sE, o=stgb, i=ps[:, 0:TT]:
                           sE.activation(o, i, AF.Identity),
                           [tok] + (stg_free.get(10 + sub % 2) or []), inc="s")
                d1 = kb.dma("g", VS[t0 + sub * 128:t0 + sub * 128 + 128,
                                    sl * 512:sl * 512 + 512], stgb, [e1])
                stg_free[10 + sub % 2] = [d1]
                slot[1] = e1
        xfree[par] = kb.tok("t")
    kb.fence()
    qb, kbx, ntoks = norms_finish(ctx, [], 512)
    m1 = ctx.FW[0:16, 3104:3112]
    m2 = ctx.FW[0:16, 3112:3120]
    t1 = kb.on("v", lambda v: v.tensor_mul(m1, qb, kbx), ntoks, inc="v")
    t2 = kb.on("v", lambda v: v.tensor_mul(m2[0:16, 0:7], qb[0:16, 0:7], kbx[0:16, 1:8]),
               [t1], inc="v")
    t3 = kb.on("v", lambda v: v.tensor_mul(m2[0:16, 7:8], qb[0:16, 7:8], kbx[0:16, 0:1]),
               [t2], inc="v")
    kb.fence()
    bk = {i: None for i in range(8)}
    ufp = {}
    for b in range(8):
        for hp in range(8):
            par = (b * 8 + hp) % 2
            base = par * 8192
            QT = ctx.BIG[:, base:base + 512]
            KT = ctx.BIG[:, base + 512:base + 1536]
            VT = ctx.BIG[:, base + 1536:base + 2560]
            PT = ctx.BIG[:, base + 2560:base + 3072]
            OT = ctx.BIG[:, base + 3072:base + 3584]
            TMP = ctx.BIG[:, base + 3584:base + 4096]
            TMP2 = ctx.BIG[:, base + 4096:base + 4608]
            uf = ufp.get(par) or []
            bn = (b + 1) % 8
            lq = kb.dma("q", QT, QS[hp * 128:hp * 128 + 128, b * 512:b * 512 + 512], uf)
            lk1 = kb.dma("q", KT[:, 0:512], KS[hp * 128:hp * 128 + 128, b * 512:b * 512 + 512], [])
            lk2 = kb.dma("q", KT[:, 512:1024], KS[hp * 128:hp * 128 + 128, bn * 512:bn * 512 + 512], [])
            lv1 = kb.dma("q", VT[:, 0:512].rearrange("p (sb c) -> p sb c", c=128),
                         VS[b * 512:b * 512 + 512, hp * 128:hp * 128 + 128].rearrange(
                             "(sb p) c -> p sb c", p=128), [])
            lv2 = kb.dma("q", VT[:, 512:1024].rearrange("p (sb c) -> p sb c", c=128),
                         VS[bn * 512:bn * 512 + 512, hp * 128:hp * 128 + 128].rearrange(
                             "(sb p) c -> p sb c", p=128), [])
            loads = [lq, lk1, lk2, lv1, lv2]
            nm1 = ctx.CF[:, 2900 + par * 2:2901 + par * 2]
            nm2 = ctx.CF[:, 2901 + par * 2:2902 + par * 2]
            for h2 in range(2):
                hb = h2 * 64
                selh = sel[0:16, (2 * hp + h2) * 128:(2 * hp + h2) * 128 + 128]
                bank6 = [ctx.ps[6], bk[6]]
                c1 = negm_col(ctx, selh, m1[0:16, b:b + 1], nm1, bank6,
                              [t3, ls] + uf + (bk.get(("nm", par)) or []))
                c2 = negm_col(ctx, selh, m2[0:16, b:b + 1], nm2, bank6, [])
                bk[6] = bank6[1]
                fin_deps = []
                for sidx in range(2):
                    psL = ctx.ps[4 + sidx]
                    psO = ctx.ps[2 + sidx]
                    lm = pv = None
                    for kc in range(4):
                        psS = ctx.ps[kc % 2]
                        mm = kb.on("t", lambda te, o=psS[:, 0:512],
                                   l=KT[hb:hb + 64, sidx * 512 + kc * 128:sidx * 512 + kc * 128 + 128],
                                   r=QT[hb:hb + 64, 0:512], tp=(hb, 0):
                                   te.matmul(o, l, r, start=True, stop=True, tile_position=tp),
                                   loads + [c1, c2, bk[kc % 2]], inc="t")
                        ex = kb.on("s", lambda sE, o=PT[:, 0:512], i=psS[:, 0:512],
                                   bb=(nm1 if sidx == 0 else nm2):
                                   sE.activation(o, i, AF.Exp, bias=bb),
                                   [mm] + (bk.get("pt") or []), inc="s")
                        bk[kc % 2] = ex
                        lm = kb.on("t", lambda te, o=psL[0:1, 0:512], l=ctx.ones_colb[:, 0:1],
                                   r=PT[:, 0:512], st=(kc == 0), sp=(kc == 3):
                                   te.matmul(o, l, r, start=st, stop=sp),
                                   [ex] + ([bk[4 + sidx]] if kc == 0 else []), inc="t")
                        pv = kb.on("t", lambda te, o=psO[hb:hb + 64, 0:512],
                                   l=VT[:, sidx * 512 + kc * 128 + hb:sidx * 512 + kc * 128 + hb + 64],
                                   r=PT[:, 0:512], tp=(0, hb),
                                   st=(kc == 0), sp=(kc == 3):
                                   te.matmul(o, l, r, start=st, stop=sp, tile_position=tp),
                                   [ex] + ([bk[2 + sidx]] if kc == 0 else []), inc="t")
                        bk["pt"] = [lm, pv]
                        bk[("nm", par)] = [ex]
                    fin_deps.append((lm, pv))
                for sidx in range(2):
                    lmf, pvf = fin_deps[sidx]
                    lrow = ctx.FW2[0:1, sidx * 512:sidx * 512 + 512]
                    a1 = kb.on("v", lambda v, o=lrow, i=ctx.ps[4 + sidx][0:1, 0:512]:
                               v.tensor_scalar_add(o, i, 1e-30),
                               [lmf, bk.get(("lr", sidx))], inc="v")
                    a2 = kb.on("v", lambda v, o=lrow, i=lrow: v.reciprocal(o, i),
                               [a1], inc="v")
                    bk[4 + sidx] = a2
                    bc = kb.on("t", lambda te, o=ctx.ps[6][hb:hb + 64, 0:512],
                               l=ctx.ones_rowf[0:1, 0:64], r=lrow, tp=(0, hb):
                               te.matmul(o, l, r, start=True, stop=True, tile_position=tp),
                               [a2, bk.get(6)], inc="t")
                    bk[("lr", sidx)] = bc
                    dstR = (TMP if sidx == 0 else TMP2)
                    cp = kb.on("s", lambda sE, o=dstR[hb:hb + 64, 0:512],
                               i=ctx.ps[6][hb:hb + 64, 0:512]:
                               sE.activation(o, i, AF.Identity),
                               [bc, bk.get(("tmp", par, sidx))], inc="s")
                    bk[6] = cp
                    if sidx == 0:
                        mA = kb.on("v", lambda v, o=OT[hb:hb + 64, 0:512],
                                   a=ctx.ps[2][hb:hb + 64, 0:512], bb=TMP[hb:hb + 64, 0:512]:
                                   v.tensor_mul(o, a, bb),
                                   [cp, pvf, bk.get(("ot", par))], inc="v")
                        bk[2] = mA
                        bk[("tmp", par, 0)] = mA
                    else:
                        mB = kb.on("v", lambda v, o=TMP2[hb:hb + 64, 0:512],
                                   a=ctx.ps[3][hb:hb + 64, 0:512], bb=TMP2[hb:hb + 64, 0:512]:
                                   v.tensor_mul(o, a, bb), [cp, pvf], inc="v")
                        bk[3] = mB
                        bk[("tmp", par, 1)] = mB
                        kb.on("v", lambda v, o=OT[hb:hb + 64, 0:512],
                              a=OT[hb:hb + 64, 0:512], bb=TMP2[hb:hb + 64, 0:512]:
                              v.tensor_add(o, a, bb), [mB], inc="v")
            otok = kb.dma("g", AO[hp * 128:hp * 128 + 128, b * 512:b * 512 + 512], OT,
                          [kb.tok("v")])
            bk[("ot", par)] = otok
            ufp[par] = [kb.tok("t"), kb.tok("s"), kb.tok("v"), otok]
    kb.fence()
    bout = ctx.CF[:, 2048:2056]
    lb2 = kb.dma("q", bout, io["L3_attn_b_out_c"])
    kb.fence()
    pl = PostLN(ctx, X, XB, bias_col=lambda d0, dw: bout[0:dw, d0 // 128:d0 // 128 + 1])

    def xload(ci, t0, tw, par):
        o = 8192 + par * 4096 + ci * 512
        dst = ctx.BIG[:, o:o + tw]
        return AO[ci * 128:ci * 128 + 128, t0:t0 + tw], dst, []

    mmB(ctx, "l3o", io["L3_attn_W_out"], D, D, xload, pl.evac, woff=0, xoff=8192,
        xbufs=2)
    kb.fence()
    pl.apply()


def emit_L0_attn(ctx, io, X, XB, H):
    nc, kb = ctx.nc, ctx.kb
    QS, KS, VS, AO = io["_QS"], io["_KS"], io["_VS"], io["_AO"]
    Q2S, K2S = io["_Q2S"], io["_K2S"]
    ind16 = ctx.CF[:, 2112:2112 + 128]
    sel = ctx.FW[0:16, 1024:1024 + 2048]
    li = kb.dma("q", ind16, io["_ind16"])
    ls = kb.dma("q", sel, io["_sel16"])
    t_t1q = kb.dma("g", ctx.CAR[:, 0:4096], io["_r0T1q"])
    t_t2q = kb.dma("g", ctx.CAR[:, 4096:8192], io["_r0T2q"])
    t_t1k = kb.dma("g", ctx.CAR[:, 8192:12288], io["_r0T1k"])
    t_t2k = kb.dma("g", ctx.CAR[:, 12288:16384], io["_r0T2k"])
    p1 = kb.dma("g", ctx.ident[:, :] if False else ctx.CAR[:, 0:0], io["_r0P1"]) \
        if False else None
    P1b = ctx.BIG[:, 48640:48768]
    P2b = ctx.BIG[:, 48768:48896]
    p1 = kb.dma("g", P1b, io["_r0P1"])
    p2 = kb.dma("g", P2b, io["_r0P2"])
    selkv = ctx.CF[0:2, 2240:2240 + 16]
    lkv = kb.dma("q", selkv, io["_selkv"])
    lrow = ctx.CF[0:1, 2260:2260 + 64]
    lrow2 = ctx.CF[0:1, 2324:2324 + 64]
    nlam = ctx.CF[:, 2390:2391]
    ldeps = [kb.dma("q", lrow, io["L0_attn_lq1"].rearrange("(a n) -> a n", a=1)),
             kb.dma("q", lrow2, io["L0_attn_lk1"].rearrange("(a n) -> a n", a=1))]
    v1 = kb.on("v", lambda v: v.tensor_mul(lrow, lrow, lrow2), ldeps, inc="v")
    v2 = kb.on("v", lambda v: v.reduce_sum(ctx.CF[0:1, 2388:2389], lrow, axis=AX),
               [v1], inc="v")
    ldeps2 = [kb.dma("q", lrow, io["L0_attn_lq2"].rearrange("(a n) -> a n", a=1), [v2]),
              kb.dma("q", lrow2, io["L0_attn_lk2"].rearrange("(a n) -> a n", a=1))]
    v3 = kb.on("v", lambda v: v.tensor_mul(lrow, lrow, lrow2), ldeps2, inc="v")
    v4 = kb.on("v", lambda v: v.reduce_sum(ctx.CF[0:1, 2389:2390], lrow, axis=AX),
               [v3], inc="v")
    e1 = kb.on("s", lambda sE: sE.activation(ctx.CF[0:1, 2388:2389],
               ctx.CF[0:1, 2388:2389], AF.Exp), [v2], inc="s")
    e2 = kb.on("s", lambda sE: sE.activation(ctx.CF[0:1, 2389:2390],
               ctx.CF[0:1, 2389:2390], AF.Exp), [v4, e1], inc="s")
    v5 = kb.on("v", lambda v: v.tensor_sub(ctx.CF[0:1, 2388:2389],
               ctx.CF[0:1, 2388:2389], ctx.CF[0:1, 2389:2390]), [e2], inc="v")
    v6 = kb.on("v", lambda v: v.tensor_scalar(ctx.CF[0:1, 2388:2389],
               ctx.CF[0:1, 2388:2389], 0.8, 0.9, ALU.add, ALU.min), [v5], inc="v")
    v7 = kb.on("v", lambda v: v.tensor_scalar(ctx.CF[0:1, 2388:2389],
               ctx.CF[0:1, 2388:2389], 0.1, -1.0, ALU.max, ALU.mult), [v6], inc="v")
    bl = kb.on("t", lambda te: te.matmul(ctx.ps[6][:, 0:1], ctx.ones_rowf[0:1, 0:128],
               ctx.CF[0:1, 2388:2389], start=True, stop=True), [v7], inc="t")
    bl2 = kb.on("s", lambda sE: sE.activation(nlam, ctx.ps[6][:, 0:1], AF.Identity),
                [bl], inc="s")
    bout_c = ctx.CF[:, 2392:2400]
    lb2 = kb.dma("q", bout_c, io["L0_attn_bout_c"])
    kb.fence()
    w1 = load_w_group(ctx, io["L0_attn_Wq1"], D, 0, 1024, 0)
    w2 = load_w_group(ctx, io["L0_attn_Wq2"], D, 0, 1024, 8192)
    w3 = load_w_group(ctx, io["L0_attn_Wk1"], D, 0, 128, 16384)
    w4 = load_w_group(ctx, io["L0_attn_Wk2"], D, 0, 128, 17408)
    w5 = load_w_group(ctx, io["L0_attn_Wv"], D, 0, 128, 18432)
    wtoks = w1 + w2 + w3 + w4 + w5
    xoff, stg = 19456, 29184
    xfree = {}
    stg_free = {}
    mxfree = {}
    for ti in range(NTT):
        t0 = ti * TT
        par = ti % 2
        xo = xoff + par * 4096
        src = XB[:, 0:NCH * XBW].rearrange("p (c t) -> p c t", t=XBW)[:, :, XBO + t0:XBO + t0 + TT]
        dst = ctx.BIG[:, xo:xo + 4096].rearrange("p (c t) -> p c t", t=TT)
        xt = kb.dma("q", dst, src, [xfree.get(par)])
        xa = lambda ci, xo=xo: ctx.BIG[:, xo + ci * TT:xo + ci * TT + TT]
        nrm = {}
        for sysi, (woff, dsts, tq1, tq2, nchunks) in enumerate([
                (0, QS, 0, 4096, 8), (8192, Q2S, 0, 4096, 8),
                (16384, KS, 8192, 12288, 1), (17408, K2S, 8192, 12288, 1)]):
            isq = sysi < 2
            nq = ctx.spool.take() if isq else ctx.pool.take()
            for dc in range(nchunks):
                slot = ctx.pool.take()
                ps = slot[0]
                for ci in range(8):
                    mdeps = (wtoks + [xt, slot[1]]) if ci == 0 else ()
                    tok = kb.on("t", lambda te, o=ps[:, 0:TT],
                                l=slab_lhsT(ctx, woff, nchunks * 128, ci, dc * 128, 128, 128),
                                r=xa(ci), st=(ci == 0), sp=(ci == 7):
                                te.matmul(o, l, r, start=st, stop=sp),
                                mdeps, inc="t" if ci == 7 else None)
                sA = ctx.BIG[:, stg + 0 * 512:stg + 0 * 512 + 512]
                sB = ctx.BIG[:, stg + 1024:stg + 1024 + 512]
                sC = ctx.BIG[:, stg + 1536:stg + 1536 + 512]
                e0 = kb.on("s", lambda sE, o=sA, i=ps[:, 0:TT]:
                           sE.activation(o, i, AF.Identity),
                           [tok] + (stg_free.get(0) or []), inc="s")
                slot[1] = e0
                pslot = ctx.pool.take()
                pslot2 = ctx.pool.take()
                m1 = kb.on("t", lambda te, o=pslot[0][:, 0:TT], l=P1b, r=sA:
                           te.matmul(o, l, r, start=True, stop=True),
                           [e0, p1, p2, pslot[1]], inc="t")
                m2 = kb.on("t", lambda te, o=pslot2[0][:, 0:TT], l=P2b, r=sA:
                           te.matmul(o, l, r, start=True, stop=True),
                           [pslot2[1]], inc="t")
                tq = (tq1 if isq or True else 0)
                T1 = ctx.CAR[:, (0 if isq else 8192) + t0:(0 if isq else 8192) + t0 + TT]
                T2 = ctx.CAR[:, (4096 if isq else 12288) + t0:(4096 if isq else 12288) + t0 + TT]
                d1 = kb.on("v", lambda v, o=sB, a=pslot[0][:, 0:TT], b=T1:
                           v.tensor_mul(o, a, b),
                           [m1, t_t1q, t_t1k] + (stg_free.get(1) or []), inc="v")
                d2 = kb.on("v", lambda v, o=sC, a=pslot2[0][:, 0:TT], b=T2:
                           v.tensor_mul(o, a, b),
                           [m2, t_t2q, t_t2k] + (stg_free.get(2) or []), inc="v")
                d3 = kb.on("v", lambda v, o=sB, a=sB, b=sC: v.tensor_add(o, a, b),
                           [d1, d2], inc="v")
                pslot[1] = d1
                pslot2[1] = d2
                dd = kb.dma("g", dsts[dc * 128:dc * 128 + 128, t0:t0 + TT], sB, [d3])
                sq = ctx.FW[:, 512:1024]
                e2_ = kb.on("s", lambda sE, o=sq, i=sB: sE.activation(o, i, AF.Square),
                            [d3, kb.tok("s")] + (stg_free.get(3) or []), inc="s")
                first = (dc == 0)
                n1 = kb.on("t", lambda te, o=nq[0][0:16, 0:TT],
                           l=ind16[:, (dc if isq else 0) * 16:(dc if isq else 0) * 16 + 16],
                           r=sq, st=first, sp=(dc == nchunks - 1):
                           te.matmul(o, l, r, start=st, stop=sp),
                           [e2_, li] + ([nq[1]] if first else []), inc="t")
                stg_free[0] = [m1, m2]
                stg_free[1] = [d3]
                stg_free[2] = [d3]
                stg_free[3] = [n1]
            mxo = 3184 + sysi * 32 + ti * 4
            rows = 16 if isq else 2
            rd = kb.on("v", lambda v, o=ctx.FW[0:rows, mxo:mxo + 4],
                       i=nq[0][0:rows, 0:TT].rearrange("h (b w) -> h b w", w=128),
                       deps=None: v.reduce_max(o, i, axis=AX),
                       [n1, mxfree.get(mxo)], inc="v")
            nq[1] = rd
            mxfree[mxo] = rd
        for sub in range(4):
            slot = ctx.pool.take()
            ps = slot[0]
            for ci in range(8):
                mdeps = (wtoks + [xt, slot[1]]) if ci == 0 else ()
                tok = kb.on("t", lambda te, o=ps[:, 0:128],
                            l=ctx.BIG[:, xo + ci * TT + sub * 128:xo + ci * TT + sub * 128 + 128],
                            r=slab_lhsT(ctx, 18432, 128, ci, 0, 128, 128),
                            st=(ci == 0), sp=(ci == 7):
                            te.matmul(o, l, r, start=st, stop=sp),
                            mdeps, inc="t" if ci == 7 else None)
            stgb = ctx.BIG[:, stg + 2048 + (sub % 2) * 512:stg + 2048 + (sub % 2) * 512 + 128]
            ev = kb.on("s", lambda sE, o=stgb, i=ps[:, 0:128]:
                       sE.activation(o, i, AF.Identity),
                       [tok] + (stg_free.get(10 + sub % 2) or []), inc="s")
            dv = kb.dma("g", VS[t0 + sub * 128:t0 + sub * 128 + 128, 0:128], stgb, [ev])
            stg_free[10 + sub % 2] = [dv]
            slot[1] = ev
        xfree[par] = kb.tok("t")
    kb.fence()
    sq1 = kb.on("s", lambda sE: sE.activation(ctx.FW[0:16, 3184:3184 + 128],
                ctx.FW[0:16, 3184:3184 + 128], AF.Sqrt), [], inc="s")
    mw = {}
    for sysi in range(4):
        rows = 16 if sysi < 2 else 2
        base = 3184 + sysi * 32
        o = 3312 + sysi * 16
        tokm = kb.on("v", lambda v, b=base, oo=o, r=rows:
                     v.tensor_max(ctx.FW[0:r, oo:oo + 16], ctx.FW[0:r, b:b + 16],
                                  ctx.FW[0:r, b + 1:b + 17]), [sq1], inc="v")
        mw[sysi] = tokm
    mb = {}
    for sysn, (qi, ki) in enumerate([(0, 2), (1, 3)]):
        pslot = ctx.spool.take()
        t1 = kb.on("t", lambda te, o=pslot[0][0:16, 0:16], l=selkv,
                   r=ctx.FW[0:2, 3312 + ki * 16:3312 + ki * 16 + 16]:
                   te.matmul(o, l, r, start=True, stop=True),
                   [mw[ki], lkv, pslot[1]], inc="t")
        dst = ctx.FW[0:16, 3376 + sysn * 16:3376 + sysn * 16 + 16]
        t2 = kb.on("v", lambda v, o=dst, a=ctx.FW[0:16, 3312 + qi * 16:3312 + qi * 16 + 16],
                   b=pslot[0][0:16, 0:16]: v.tensor_mul(o, a, b), [t1, mw[qi]], inc="v")
        pslot[1] = t2
        mb[sysn] = (dst, t2)
    kb.fence()
    bk = {}
    ufp = {}
    for w in range(16):
        for hp in range(8):
            par = (w * 8 + hp) % 2
            base = par * 8192
            QT1 = ctx.BIG[:, base:base + 256]
            QT2 = ctx.BIG[:, base + 256:base + 512]
            KT1 = ctx.BIG[:, base + 512:base + 768]
            KT2 = ctx.BIG[:, base + 768:base + 1024]
            VT = ctx.BIG[:, base + 1024:base + 1280]
            PT = ctx.BIG[:, base + 1280:base + 1536]
            OT = ctx.BIG[:, base + 1536:base + 1792]
            TMP = ctx.BIG[:, base + 1792:base + 2048]
            TMP2 = ctx.BIG[:, base + 2048:base + 2304]
            uf = ufp.get(par) or []
            c0 = 128 * w
            lq1 = kb.dma("q", QT1, QS[hp * 128:hp * 128 + 128, c0:c0 + 256], uf)
            lq2 = kb.dma("q", QT2, Q2S[hp * 128:hp * 128 + 128, c0:c0 + 256], [])
            lk1 = kb.dma("q", KT1, KS[0:128, c0:c0 + 256], [])
            lk2 = kb.dma("q", KT2, K2S[0:128, c0:c0 + 256], [])
            lv = kb.dma("q", VT.rearrange("p (sb c) -> p sb c", c=128),
                        VS[c0:c0 + 256, 0:128].rearrange("(sb p) c -> p sb c", p=128), [])
            loads = [lq1, lq2, lk1, lk2, lv]
            for h2 in range(2):
                hb = h2 * 64
                kvb = ((2 * hp + h2) % 2) * 64
                selh = sel[0:16, (2 * hp + h2) * 128:(2 * hp + h2) * 128 + 128]
                nm1 = ctx.CF[:, 2900 + par * 2:2901 + par * 2]
                nm2 = ctx.CF[:, 2901 + par * 2:2902 + par * 2]
                bank6 = [ctx.ps[6], bk.get(6)]
                c1 = negm_col(ctx, selh, mb[0][0][0:16, w:w + 1], nm1, bank6,
                              [mb[0][1], ls] + uf + (bk.get(("nm", par)) or []))
                c2 = negm_col(ctx, selh, mb[1][0][0:16, w:w + 1], nm2, bank6,
                              [mb[1][1]])
                bk[6] = bank6[1]
                fin = []
                for sidx in range(2):
                    QT = QT1 if sidx == 0 else QT2
                    KT = KT1 if sidx == 0 else KT2
                    psL = ctx.ps[4 + sidx]
                    psO = ctx.ps[2 + sidx]
                    lm = pv = None
                    for kc in range(2):
                        psS = ctx.ps[kc % 2]
                        mm = kb.on("t", lambda te, o=psS[:, 0:256],
                                   l=KT[kvb:kvb + 64, kc * 128:kc * 128 + 128],
                                   r=QT[hb:hb + 64, 0:256], tp=(0, 0):
                                   te.matmul(o, l, r, start=True, stop=True),
                                   loads + [c1, c2, bk.get(kc % 2)], inc="t")
                        ex = kb.on("s", lambda sE, o=PT[:, 0:256], i=psS[:, 0:256],
                                   bb=(nm1 if sidx == 0 else nm2):
                                   sE.activation(o, i, AF.Exp, bias=bb),
                                   [mm] + (bk.get("pt") or []), inc="s")
                        bk[kc % 2] = ex
                        lm = kb.on("t", lambda te, o=psL[0:1, 0:256],
                                   l=ctx.ones_colb[:, 0:1], r=PT[:, 0:256],
                                   st=(kc == 0), sp=(kc == 1):
                                   te.matmul(o, l, r, start=st, stop=sp),
                                   [ex] + ([bk.get(4 + sidx)] if kc == 0 else []), inc="t")
                        pv = kb.on("t", lambda te, o=psO[hb:hb + 64, 0:256],
                                   l=VT[:, kc * 128 + kvb:kc * 128 + kvb + 64],
                                   r=PT[:, 0:256], tp=(0, hb),
                                   st=(kc == 0), sp=(kc == 1):
                                   te.matmul(o, l, r, start=st, stop=sp, tile_position=tp),
                                   [ex] + ([bk.get(2 + sidx)] if kc == 0 else []), inc="t")
                        bk["pt"] = [lm, pv]
                        bk[("nm", par)] = [ex]
                    fin.append((lm, pv))
                for sidx in range(2):
                    lmf, pvf = fin[sidx]
                    lr = ctx.FW2[0:1, sidx * 512:sidx * 512 + 256]
                    a1 = kb.on("v", lambda v, o=lr, i=ctx.ps[4 + sidx][0:1, 0:256]:
                               v.tensor_scalar_add(o, i, 1e-30),
                               [lmf, bk.get(("lr", sidx))], inc="v")
                    a2 = kb.on("v", lambda v, o=lr, i=lr: v.reciprocal(o, i), [a1], inc="v")
                    bk[4 + sidx] = a2
                    bc = kb.on("t", lambda te, o=ctx.ps[6][hb:hb + 64, 0:256],
                               l=ctx.ones_rowf[0:1, 0:64], r=lr, tp=(0, hb):
                               te.matmul(o, l, r, start=True, stop=True, tile_position=tp),
                               [a2, bk.get(6)], inc="t")
                    bk[("lr", sidx)] = bc
                    dstR = (TMP if sidx == 0 else TMP2)
                    cp = kb.on("s", lambda sE, o=dstR[hb:hb + 64, 0:256],
                               i=ctx.ps[6][hb:hb + 64, 0:256]:
                               sE.activation(o, i, AF.Identity),
                               [bc, bk.get(("tmp", par, sidx))], inc="s")
                    bk[6] = cp
                    if sidx == 0:
                        mA = kb.on("v", lambda v, o=OT[hb:hb + 64, 0:256],
                                   a=ctx.ps[2][hb:hb + 64, 0:256], bb=TMP[hb:hb + 64, 0:256]:
                                   v.tensor_mul(o, a, bb),
                                   [cp, pvf, bk.get(("ot", par))], inc="v")
                        bk[2] = mA
                        bk[("tmp", par, 0)] = mA
                    else:
                        mB = kb.on("v", lambda v, o=TMP2[hb:hb + 64, 0:256],
                                   a=ctx.ps[3][hb:hb + 64, 0:256], bb=TMP2[hb:hb + 64, 0:256]:
                                   v.tensor_mul(o, a, bb), [cp, pvf], inc="v")
                        bk[3] = mB
                        bk[("tmp", par, 1)] = mB
                        kb.on("v", lambda v, o=OT[hb:hb + 64, 0:256],
                              a=TMP2[hb:hb + 64, 0:256], sc=nlam[hb:hb + 64, 0:1],
                              b=OT[hb:hb + 64, 0:256]:
                              v.scalar_tensor_tensor(o, a, sc, b, ALU.mult, ALU.add),
                              [mB, bl2], inc="v")
            otok = kb.dma("g", AO[hp * 128:hp * 128 + 128, 256 * w:256 * w + 256], OT,
                          [kb.tok("v")])
            bk[("ot", par)] = otok
            ufp[par] = [kb.tok("t"), kb.tok("s"), kb.tok("v"), otok]
    kb.fence()
    pl = PostLN(ctx, X, XB, bias_col=lambda d0, dw: bout_c[0:dw, d0 // 128:d0 // 128 + 1])

    def xload(ci, t0, tw, par):
        o = 8192 + par * 4096 + ci * 512
        dst = ctx.BIG[:, o:o + tw]
        return AO[ci * 128:ci * 128 + 128, t0:t0 + tw], dst, []

    mmB(ctx, "l0o", io["L0_attn_Wout"], D, D, xload, pl.evac, woff=0, xoff=8192,
        xbufs=2)
    kb.fence()
    pl.apply()


def emit_L1_attn(ctx, io, X, XB, H):
    nc, kb = ctx.nc, ctx.kb
    QmS, QpS, QRS = io["_QS"], io["_Q2S"], io["_QRS"]
    KpS, KmS, KRS = io["_KS"], io["_K2S"], io["_KRS"]
    VS, AO = io["_VS"], io["_AO"]
    ind16 = ctx.CF[:, 2112:2112 + 128]
    sel = ctx.FW[0:16, 1024:1024 + 2048]
    li = kb.dma("q", ind16, io["_ind16"])
    ls = kb.dma("q", sel, io["_sel16"])
    dt_ = kb.dma("q", ctx.CF[:, 0:2048], io["_l1D"])
    kb.fence()

    wq = load_w_group(ctx, io["L1_attn_W_q"], D, 0, 1024, 0)
    xfree, stg_free, nfree = {}, {}, {}
    fb_free = {}
    for ti in range(NTT):
        t0 = ti * TT
        par = ti % 2
        xo = 8192 + par * 4096
        src = XB[:, 0:NCH * XBW].rearrange("p (c t) -> p c t", t=XBW)[:, :, XBO + t0:XBO + t0 + TT]
        dst = ctx.BIG[:, xo:xo + 4096].rearrange("p (c t) -> p c t", t=TT)
        xt = kb.dma("q", dst, src, [xfree.get(par)])
        xa = lambda ci, xo=xo: ctx.BIG[:, xo + ci * TT:xo + ci * TT + TT]
        frm = ctx.spool.take()
        fr1 = kb.dma("q", ctx.FW[0:1, 0:TT], io["_l1fqm"][0:1, t0:t0 + TT],
                     [fb_free.get("m")])
        b1 = kb.on("t", lambda te, o=frm[0][:, 0:TT], l=ctx.ones_rowf[0:1, 0:128],
                   r=ctx.FW[0:1, 0:TT]: te.matmul(o, l, r, start=True, stop=True),
                   [fr1, frm[1]], inc="t")
        frp = ctx.spool.take()
        fr2 = kb.dma("q", ctx.FW[0:1, 512:512 + TT], io["_l1fqp"][0:1, t0:t0 + TT],
                     [fb_free.get("p")])
        b2 = kb.on("t", lambda te, o=frp[0][:, 0:TT], l=ctx.ones_rowf[0:1, 0:128],
                   r=ctx.FW[0:1, 512:512 + TT]: te.matmul(o, l, r, start=True, stop=True),
                   [fr2, frp[1]], inc="t")
        fb_free["m"] = b1
        fb_free["p"] = b2
        nslot = ctx.pool.take()
        for dc in range(8):
            slot = ctx.pool.take()
            ps = slot[0]
            for ci in range(8):
                mdeps = (wq + [xt, slot[1]]) if ci == 0 else ()
                tok = kb.on("t", lambda te, o=ps[:, 0:TT],
                            l=slab_lhsT(ctx, 0, 1024, ci, dc * 128, 128, 128),
                            r=xa(ci), st=(ci == 0), sp=(ci == 7):
                            te.matmul(o, l, r, start=st, stop=sp),
                            mdeps, inc="t" if ci == 7 else None)
            sR = ctx.BIG[:, 16384 + 0 * 512:16384 + 512]
            sM = ctx.BIG[:, 16896:16896 + 512]
            sP = ctx.BIG[:, 17408:17408 + 512]
            e0 = kb.on("s", lambda sE, o=sR, i=ps[:, 0:TT]:
                       sE.activation(o, i, AF.Identity),
                       [tok] + (stg_free.get(0) or []), inc="s")
            slot[1] = e0
            d0 = kb.dma("g", QRS[dc * 128:dc * 128 + 128, t0:t0 + TT], sR, [e0])
            v1 = kb.on("v", lambda v, o=sM, a=sR, b=frm[0][:, 0:TT]:
                       v.tensor_mul(o, a, b), [e0, b1] + (stg_free.get(1) or []), inc="v")
            d1 = kb.dma("g", QmS[dc * 128:dc * 128 + 128, t0:t0 + TT], sM, [v1])
            v2 = kb.on("v", lambda v, o=sP, a=sR, b=frp[0][:, 0:TT]:
                       v.tensor_mul(o, a, b), [e0, b2] + (stg_free.get(2) or []), inc="v")
            d2 = kb.dma("g", QpS[dc * 128:dc * 128 + 128, t0:t0 + TT], sP, [v2])
            sq = ctx.FW[:, 3584:3584 + 512]
            e2 = kb.on("s", lambda sE, o=sq, i=sR: sE.activation(o, i, AF.Square),
                       [e0] + (stg_free.get(3) or []), inc="s")
            n1 = kb.on("t", lambda te, o=nslot[0][0:16, 0:TT],
                       l=ind16[:, dc * 16:dc * 16 + 16], r=sq,
                       st=(dc == 0), sp=(dc == 7): te.matmul(o, l, r, start=(dc == 0), stop=(dc == 7)),
                       [e2, li] + ([nslot[1]] if dc == 0 else []), inc="t")
            stg_free[0] = [d0, v1, v2, e2]
            stg_free[1] = [d1]
            stg_free[2] = [d2]
            stg_free[3] = [n1]
        rd = kb.on("v", lambda v, o=ctx.FW[0:16, 3200 + ti:3200 + ti + 1],
                   i=nslot[0][0:16, 0:TT]: v.reduce_max(o, i, axis=AX),
                   [n1, nfree.get(ti)], inc="v")
        nslot[1] = rd
        nfree[ti] = rd
        frm[1] = v1
        frp[1] = v2
        xfree[par] = kb.tok("t")
    kb.fence()
    wk = load_w_group(ctx, io["L1_attn_W_k"], D, 0, 1024, 0)
    wv = load_w_group(ctx, io["L1_attn_W_v"], D, 0, 1024, 8192)
    ad = kb.on("v", lambda v: v.tensor_add(ctx.BIG[:, 0:8192], ctx.BIG[:, 0:8192],
               ctx.BIG[:, 8192:16384]), wk + wv, inc="v")
    kb.fence()
    for ti in range(NTT):
        t0 = ti * TT
        par = ti % 2
        xo = 8192 + par * 4096
        src = XB[:, 0:NCH * XBW].rearrange("p (c t) -> p c t", t=XBW)[:, :, XBO + t0:XBO + t0 + TT]
        dst = ctx.BIG[:, xo:xo + 4096].rearrange("p (c t) -> p c t", t=TT)
        xt = kb.dma("q", dst, src, [xfree.get(10 + par)])
        xa = lambda ci, xo=xo: ctx.BIG[:, xo + ci * TT:xo + ci * TT + TT]
        for dc in range(8):
            slot = ctx.pool.take()
            ps = slot[0]
            for ci in range(8):
                mdeps = ([ad, xt, slot[1]]) if ci == 0 else ()
                tok = kb.on("t", lambda te, o=ps[:, 0:TT],
                            l=slab_lhsT(ctx, 0, 1024, ci, dc * 128, 128, 128),
                            r=xa(ci), st=(ci == 0), sp=(ci == 7):
                            te.matmul(o, l, r, start=st, stop=sp),
                            mdeps, inc="t" if ci == 7 else None)
            kvdst = ctx.BIG[:, 16384 + dc * 4096 + t0:16384 + dc * 4096 + t0 + TT]
            e0 = kb.on("s", lambda sE, o=kvdst, i=ps[:, 0:TT]:
                       sE.activation(o, i, AF.Identity), [tok], inc="s")
            slot[1] = e0
        xfree[10 + par] = kb.tok("t")
    kb.fence()
    wdn = load_w_group_car(ctx, io["L1_attn_W_down_kv"], D, 0, 256, 0)
    for ti in range(NTT):
        t0 = ti * TT
        for dc in range(2):
            slot = ctx.pool.take()
            ps = slot[0]
            for ci in range(8):
                mdeps = (wdn + [slot[1]]) if ci == 0 else ()
                tok = kb.on("t", lambda te, o=ps[:, 0:TT],
                            l=ctx.CAR[0:128, ci * 256 + dc * 128:ci * 256 + dc * 128 + 128],
                            r=ctx.BIG[:, 16384 + ci * 4096 + t0:16384 + ci * 4096 + t0 + TT],
                            st=(ci == 0), sp=(ci == 7):
                            te.matmul(o, l, r, start=st, stop=sp),
                            mdeps, inc="t" if ci == 7 else None)
            e0 = kb.on("s", lambda sE, o=ctx.CAR[:, 2048 + dc * 4096 + t0:2048 + dc * 4096 + t0 + TT],
                       i=ps[:, 0:TT]: sE.activation(o, i, AF.Identity), [tok], inc="s")
            slot[1] = e0
    kb.fence()
    wuk = load_w_group(ctx, io["L1_attn_W_up_k"], 256, 0, 1024, 0)
    wuv = load_w_group(ctx, io["L1_attn_W_up_v"], 256, 0, 1024, 2048)
    lat = lambda ci, t0: ctx.CAR[0:128, 2048 + ci * 4096 + t0:2048 + ci * 4096 + t0 + TT]
    for ti in range(NTT):
        t0 = ti * TT
        frm = ctx.spool.take()
        fr1 = kb.dma("q", ctx.FW[0:1, 0:TT], io["_l1fkm"][0:1, t0:t0 + TT],
                     [fb_free.get("m")])
        b1 = kb.on("t", lambda te, o=frm[0][:, 0:TT], l=ctx.ones_rowf[0:1, 0:128],
                   r=ctx.FW[0:1, 0:TT]: te.matmul(o, l, r, start=True, stop=True),
                   [fr1, frm[1]], inc="t")
        frp = ctx.spool.take()
        fr2 = kb.dma("q", ctx.FW[0:1, 512:512 + TT], io["_l1fkp"][0:1, t0:t0 + TT],
                     [fb_free.get("p")])
        b2 = kb.on("t", lambda te, o=frp[0][:, 0:TT], l=ctx.ones_rowf[0:1, 0:128],
                   r=ctx.FW[0:1, 512:512 + TT]: te.matmul(o, l, r, start=True, stop=True),
                   [fr2, frp[1]], inc="t")
        fb_free["m"] = b1
        fb_free["p"] = b2
        nslot = ctx.pool.take()
        for dc in range(8):
            slot = ctx.pool.take()
            ps = slot[0]
            for ci in range(2):
                mdeps = (wuk + [slot[1]]) if ci == 0 else ()
                tok = kb.on("t", lambda te, o=ps[:, 0:TT],
                            l=slab_lhsT(ctx, 0, 1024, ci, dc * 128, 128, 128),
                            r=lat(ci, t0), st=(ci == 0), sp=(ci == 1):
                            te.matmul(o, l, r, start=st, stop=sp),
                            mdeps, inc="t" if ci == 1 else None)
            sR = ctx.BIG[:, 4096:4096 + 512]
            sM = ctx.BIG[:, 4608:4608 + 512]
            sP = ctx.BIG[:, 5120:5120 + 512]
            e0 = kb.on("s", lambda sE, o=sR, i=ps[:, 0:TT]:
                       sE.activation(o, i, AF.Identity),
                       [tok] + (stg_free.get(0) or []), inc="s")
            slot[1] = e0
            d0 = kb.dma("g", KRS[dc * 128:dc * 128 + 128, t0:t0 + TT], sR, [e0])
            v1 = kb.on("v", lambda v, o=sM, a=sR, b=frm[0][:, 0:TT]:
                       v.tensor_mul(o, a, b), [e0, b1] + (stg_free.get(1) or []), inc="v")
            d1 = kb.dma("g", KmS[dc * 128:dc * 128 + 128, t0:t0 + TT], sM, [v1])
            v2 = kb.on("v", lambda v, o=sP, a=sR, b=frp[0][:, 0:TT]:
                       v.tensor_mul(o, a, b), [e0, b2] + (stg_free.get(2) or []), inc="v")
            d2 = kb.dma("g", KpS[dc * 128:dc * 128 + 128, t0:t0 + TT], sP, [v2])
            sq = ctx.FW[:, 3584:3584 + 512]
            e2 = kb.on("s", lambda sE, o=sq, i=sR: sE.activation(o, i, AF.Square),
                       [e0] + (stg_free.get(3) or []), inc="s")
            n1 = kb.on("t", lambda te, o=nslot[0][0:16, 0:TT],
                       l=ind16[:, dc * 16:dc * 16 + 16], r=sq,
                       st=(dc == 0), sp=(dc == 7): te.matmul(o, l, r, start=(dc == 0), stop=(dc == 7)),
                       [e2] + ([nslot[1]] if dc == 0 else []), inc="t")
            stg_free[0] = [d0, v1, v2, e2]
            stg_free[1] = [d1]
            stg_free[2] = [d2]
            stg_free[3] = [n1]
            frm[1] = v1
            frp[1] = v2
        rd = kb.on("v", lambda v, o=ctx.FW[0:16, 3216 + ti:3216 + ti + 1],
                   i=nslot[0][0:16, 0:TT]: v.reduce_max(o, i, axis=AX),
                   [n1, nfree.get(20 + ti)], inc="v")
        nslot[1] = rd
        nfree[20 + ti] = rd
        for sub in range(4):
            stgb = ctx.BIG[:, 5632 + (sub % 2) * 1024:5632 + (sub % 2) * 1024 + 1024]
            evs = []
            for sl in range(2):
                slot = ctx.pool.take()
                ps = slot[0]
                for ci in range(2):
                    mdeps = (wuv + [slot[1]]) if ci == 0 else ()
                    tok = kb.on("t", lambda te, o=ps[:, 0:512],
                                l=ctx.CAR[0:128, 2048 + ci * 4096 + t0 + sub * 128:
                                          2048 + ci * 4096 + t0 + sub * 128 + 128],
                                r=slab_lhsT(ctx, 2048, 1024, ci, sl * 512, 512, 128),
                                st=(ci == 0), sp=(ci == 1):
                                te.matmul(o, l, r, start=st, stop=sp),
                                mdeps, inc="t" if ci == 1 else None)
                ev = kb.on("s", lambda sE, o=stgb[:, sl * 512:sl * 512 + 512], i=ps[:, 0:512]:
                           sE.activation(o, i, AF.Identity),
                           [tok] + (stg_free.get(10 + sub % 2) or []), inc="s")
                evs.append(ev)
                slot[1] = ev
            dv = kb.dma("g", VS[t0 + sub * 128:t0 + sub * 128 + 128, 0:1024], stgb, evs)
            stg_free[10 + sub % 2] = [dv]
    kb.fence()
    s1 = kb.on("s", lambda sE: sE.activation(ctx.FW[0:16, 3200:3208],
               ctx.FW[0:16, 3200:3208], AF.Sqrt), [], inc="s")
    s2 = kb.on("v", lambda v: v.reduce_max(ctx.FW[0:16, 3230:3231],
               ctx.FW[0:16, 3216:3224], axis=AX), [s1], inc="v")
    s3 = kb.on("s", lambda sE: sE.activation(ctx.FW[0:16, 3230:3231],
               ctx.FW[0:16, 3230:3231], AF.Sqrt), [s2], inc="s")
    mtab = ctx.FW[0:16, 3232:3240]
    s4 = kb.on("v", lambda v: v.tensor_scalar(mtab, ctx.FW[0:16, 3200:3208],
               ctx.FW[0:16, 3230:3231], 1.0 / 32.0, ALU.mult, ALU.mult), [s3], inc="v")
    kb.fence()
    bk = {}
    ufk = {}
    for hp in range(8):
        ukf = ufk.get("kv") or []
        lkp = kb.dma("q", ctx.BIG[:, 0:4096], KpS[hp * 128:hp * 128 + 128, 0:T], ukf)
        lkm = kb.dma("q", ctx.BIG[:, 4096:8192], KmS[hp * 128:hp * 128 + 128, 0:T], [])
        lkr = kb.dma("q", ctx.BIG[:, 8192:12288], KRS[hp * 128:hp * 128 + 128, 0:T], [])
        lvt = kb.dma("q", ctx.BIG[:, 12288:16384].rearrange("p (sb c) -> p sb c", c=128),
                     VS[0:T, hp * 128:hp * 128 + 128].rearrange("(sb p) c -> p sb c", p=128),
                     [])
        kloads = [lkp, lkm, lkr, lvt]
        for qt in range(8):
            qt0 = qt * 512
            par = qt % 2
            qb = 16384 + par * 2048
            QM = ctx.BIG[:, qb:qb + 512]
            QP = ctx.BIG[:, qb + 512:qb + 1024]
            QR = ctx.BIG[:, qb + 1024:qb + 1536]
            uf = ufk.get(par) or []
            l1_ = kb.dma("q", QM, QmS[hp * 128:hp * 128 + 128, qt0:qt0 + 512], uf)
            l2_ = kb.dma("q", QP, QpS[hp * 128:hp * 128 + 128, qt0:qt0 + 512], [])
            l3_ = kb.dma("q", QR, QRS[hp * 128:hp * 128 + 128, qt0:qt0 + 512], [])
            qloads = [l1_, l2_, l3_]
            PT = ctx.BIG[:, 20480 + par * 512:20480 + par * 512 + 512]
            OT = ctx.BIG[:, 21504 + par * 512:21504 + par * 512 + 512]
            for h2 in range(2):
                hb = h2 * 64
                selh = sel[0:16, (2 * hp + h2) * 128:(2 * hp + h2) * 128 + 128]
                nm1 = ctx.CF[:, 2900 + par:2901 + par]
                bank6 = [ctx.ps[6], bk.get(6)]
                c1 = negm_col(ctx, selh, mtab[0:16, qt:qt + 1], nm1, bank6,
                              [s4, ls] + uf + (bk.get(("nm", par)) or []))
                bk[6] = bank6[1]
                lm = pv = None
                for kc in range(32):
                    k0 = kc * 128
                    if k0 + 128 <= qt0:
                        reg = "lo"
                    elif k0 > qt0 + 511:
                        reg = "hi"
                    else:
                        reg = "st"
                    psS = ctx.ps[kc % 2]
                    if reg != "st":
                        koff = (0 if reg == "lo" else 4096) + k0
                        mm = kb.on("t", lambda te, o=psS[:, 0:512],
                                   l=ctx.BIG[hb:hb + 64, koff:koff + 128],
                                   r=(QM if reg == "lo" else QP)[hb:hb + 64, 0:512]:
                                   te.matmul(o, l, r, start=True, stop=True),
                                   kloads + qloads + [c1, bk.get(kc % 2)], inc="t")
                        ex = kb.on("s", lambda sE, o=PT, i=psS[:, 0:512], bb=nm1:
                                   sE.activation(o, i, AF.Exp, bias=bb),
                                   [mm] + (bk.get("pt") or []), inc="s")
                    else:
                        mm = kb.on("t", lambda te, o=psS[:, 0:512],
                                   l=ctx.BIG[hb:hb + 64, 8192 + k0:8192 + k0 + 128],
                                   r=QR[hb:hb + 64, 0:512]:
                                   te.matmul(o, l, r, start=True, stop=True),
                                   kloads + qloads + [c1, bk.get(kc % 2)], inc="t")
                        di = (k0 - qt0) // 128
                        fstg = ctx.FW[:, 3584:3584 + 512]
                        sm = kb.on("v", lambda v, o=fstg, a=psS[:, 0:512],
                                   b=ctx.CF[:, di * 512:di * 512 + 512]:
                                   v.tensor_mul(o, a, b),
                                   [mm, dt_] + (bk.get("fstg") or []), inc="v")
                        ex = kb.on("s", lambda sE, o=PT, i=fstg, bb=nm1:
                                   sE.activation(o, i, AF.Exp, bias=bb),
                                   [sm] + (bk.get("pt") or []), inc="s")
                        bk["fstg"] = [ex]
                    bk[kc % 2] = ex
                    lm = kb.on("t", lambda te, o=ctx.ps[4][0:1, 0:512],
                               l=ctx.ones_colb[:, 0:1], r=PT,
                               st=(kc == 0), sp=(kc == 31):
                               te.matmul(o, l, r, start=st, stop=sp),
                               [ex] + ([bk.get(4)] if kc == 0 else []), inc="t")
                    pv = kb.on("t", lambda te, o=ctx.ps[2][hb:hb + 64, 0:512],
                               l=ctx.BIG[:, 12288 + k0 + hb:12288 + k0 + hb + 64],
                               r=PT, tp=(0, hb), st=(kc == 0), sp=(kc == 31):
                               te.matmul(o, l, r, start=st, stop=sp, tile_position=tp),
                               [ex] + ([bk.get(2)] if kc == 0 else []), inc="t")
                    bk["pt"] = [lm, pv]
                    bk[("nm", par)] = [ex]
                lr = ctx.FW2[0:1, 0:512]
                a1 = kb.on("v", lambda v, o=lr, i=ctx.ps[4][0:1, 0:512]:
                           v.tensor_scalar_add(o, i, 1e-30),
                           [lm, bk.get("lr")], inc="v")
                a2 = kb.on("v", lambda v, o=lr, i=lr: v.reciprocal(o, i), [a1], inc="v")
                bk[4] = a2
                bc = kb.on("t", lambda te, o=ctx.ps[6][hb:hb + 64, 0:512],
                           l=ctx.ones_rowf[0:1, 0:64], r=lr, tp=(0, hb):
                           te.matmul(o, l, r, start=True, stop=True, tile_position=tp),
                           [a2, bk.get(6)], inc="t")
                bk["lr"] = bc
                TMPc = ctx.BIG[:, 22528:23040]
                cp = kb.on("s", lambda sE, o=TMPc[hb:hb + 64, 0:512],
                           i=ctx.ps[6][hb:hb + 64, 0:512]:
                           sE.activation(o, i, AF.Identity),
                           [bc, bk.get(("tmp", h2))], inc="s")
                bk[6] = cp
                mA = kb.on("v", lambda v, o=OT[hb:hb + 64, 0:512],
                           a=ctx.ps[2][hb:hb + 64, 0:512], bb=TMPc[hb:hb + 64, 0:512]:
                           v.tensor_mul(o, a, bb),
                           [cp, pv, bk.get(("ot", par))], inc="v")
                bk[2] = mA
                bk[("tmp", h2)] = mA
            otok = kb.dma("g", AO[hp * 128:hp * 128 + 128, qt0:qt0 + 512], OT,
                          [kb.tok("v")])
            bk[("ot", par)] = otok
            ufk[par] = [kb.tok("t"), kb.tok("s"), kb.tok("v"), otok]
        ufk["kv"] = [kb.tok("t"), kb.tok("s"), kb.tok("v")]
    kb.fence()
    pl = PostLN(ctx, X, XB)

    def xload(ci, t0, tw, par):
        o = 8192 + par * 4096 + ci * 512
        dst = ctx.BIG[:, o:o + tw]
        return AO[ci * 128:ci * 128 + 128, t0:t0 + tw], dst, []

    mmB(ctx, "l1o", io["L1_attn_W_o"], D, D, xload, pl.evac, woff=0, xoff=8192,
        xbufs=2)
    kb.fence()
    pl.apply()


def load_w_group_car(ctx, W_ap, din, g0, gw, slab_off, deps=()):
    kb = ctx.kb
    chs = chunks_of(din)
    nfull = sum(1 for _, sz in chs if sz == 128)
    out_toks = []
    if nfull:
        src = W_ap[0:nfull * 128, g0:g0 + gw].rearrange("(n p) w -> p n w", p=128)
        dst = ctx.CAR[:, slab_off:slab_off + nfull * gw].rearrange("p (n w) -> p n w", w=gw)
        out_toks.append(kb.dma("g", dst, src, deps))
    return out_toks


class ResAdd:

    def __init__(self, ctx, X, XB, bias_col=None):
        self.ctx, self.X, self.XB = ctx, X, XB
        self.bias_col = bias_col
        self.fw_free = {}
        self.x_free = {}

    def evac(self, dc0, dcw, t0, tw, slot, deps):
        ctx, kb = self.ctx, self.ctx.kb
        ps = slot[0]
        dc = dc0 // 128
        sl = dc % 4
        s_tile = ctx.FW[0:dcw, sl * 512:sl * 512 + tw]
        xt = ctx.FW2[0:dcw, sl * 512:sl * 512 + tw]
        ltok = kb.dma("q", xt, self.X[dc0:dc0 + dcw, t0:t0 + tw],
                      deps=[self.x_free.get(sl)])
        b = 0.0 if self.bias_col is None else self.bias_col(dc0, dcw)
        t1 = kb.on("s", lambda sE, o=s_tile, i=ps[0:dcw, 0:tw], bb=b:
                   sE.activation(o, i, AF.Identity, bias=bb),
                   list(deps) + (self.fw_free.get(sl) or []), inc="s")
        t2 = kb.on("v", lambda v, o=s_tile, a=s_tile, x=xt: v.tensor_add(o, a, x),
                   [t1, ltok], inc="v")
        self.x_free[sl] = t2
        slot[1] = t2
        d1 = kb.dma("q", self.X[dc0:dc0 + dcw, t0:t0 + tw], s_tile, [t2])
        d2 = kb.dma("g", self.XB[:, dc * XBW + XBO + t0:dc * XBW + XBO + t0 + tw],
                    s_tile, [t2])
        self.fw_free[sl] = [d1, d2]


def postln_sum(ctx, X, X0, XB):
    kb = ctx.kb
    pl = PostLN(ctx, X, XB)
    fw_free = {}
    x_free = {}
    for ti in range(NTT):
        t0 = ti * TT
        sA = sB = None
        for dc in range(NCH):
            d0 = dc * 128
            sl = dc % 4
            s_tile = ctx.FW[:, sl * 512:sl * 512 + TT]
            sq_tile = ctx.FW[:, 2048 + sl * 512:2048 + sl * 512 + TT]
            xt = ctx.FW2[:, sl * 512:sl * 512 + TT]
            l1 = kb.dma("q", s_tile, X[d0:d0 + 128, t0:t0 + TT],
                        (fw_free.get(sl) or []))
            l2 = kb.dma("q", xt, X0[d0:d0 + 128, t0:t0 + TT], [x_free.get(sl)])
            t2 = kb.on("v", lambda v, o=s_tile, a=s_tile, x=xt: v.tensor_add(o, a, x),
                       [l1, l2], inc="v")
            x_free[sl] = t2
            d1 = kb.dma("q", X[d0:d0 + 128, t0:t0 + TT], s_tile, [t2])
            t3 = kb.on("s", lambda sE, o=sq_tile, i=s_tile: sE.activation(o, i, AF.Square),
                       [t2], inc="s")
            if dc == 0:
                sA = ctx.spool.take()
                sB = ctx.spool.take()
            first = (dc == 0)
            ts1 = kb.on("t", lambda te, o=sA[0][0:1, 0:TT], l=ctx.ones_colf[:, 0:1],
                        r=s_tile, st=first, sp=(dc == NCH - 1):
                        te.matmul(o, l, r, start=st, stop=sp),
                        [t2] + ([sA[1]] if first else []), inc="t")
            ts2 = kb.on("t", lambda te, o=sB[0][0:1, 0:TT], l=ctx.ones_colf[:, 0:1],
                        r=sq_tile, st=first, sp=(dc == NCH - 1):
                        te.matmul(o, l, r, start=st, stop=sp),
                        [t3] + ([sB[1]] if first else []), inc="t")
            fw_free[sl] = [d1, ts1, ts2]
            if dc == NCH - 1:
                c1 = kb.on("v", lambda v, o=ctx.STATS[0:1, t0:t0 + TT], i=sA[0][0:1, 0:TT]:
                           v.tensor_copy(o, i), [ts1, ts2], inc="v")
                c2 = kb.on("v", lambda v, o=ctx.STATS[0:1, 4096 + t0:4096 + t0 + TT],
                           i=sB[0][0:1, 0:TT]: v.tensor_copy(o, i), [c1], inc="v")
                sA[1] = c1
                sB[1] = c2
                pl.stat_done.append(c2)
    kb.fence()
    pl.apply()


def emit_L0_ffn(ctx, io, X, XB, H):
    nc, kb = ctx.nc, ctx.kb
    QS, KS, VS, AO = io["_QS"], io["_KS"], io["_VS"], io["_AO"]
    X0 = io["_X0"]
    kb.dma("q", X0[:, :], X[:, :])
    ind16 = ctx.CF[:, 2112:2112 + 128]
    sel = ctx.FW[0:16, 1024:1024 + 2048]
    selc = ctx.CF[0:16, 1024:1024 + 1024]
    li = kb.dma("q", ind16, io["_ind16"])
    ls = kb.dma("q", sel, io["_sel16"])
    lc = kb.dma("q", selc, io["_selc"])
    Prb = ctx.BIG[:, 48896:49024]
    lpr = kb.dma("g", Prb, io["_rhP"])
    kb.fence()
    for ly in range(L0FFN_LAYERS):
        sfx = f"_{ly}"
        qs_c = ctx.CF[:, 2240:2241]
        ks_c = ctx.CF[:, 2241:2242]
        qs2_c = ctx.CF[:, 2242:2243]
        ks2_c = ctx.CF[:, 2243:2244]
        lq = kb.dma("q", qs_c, io["L0_ffn_qs_c"][ly:ly + 1, :].rearrange(
            "a (p one) -> (a p) one", one=1))
        lk = kb.dma("q", ks_c, io["L0_ffn_ks_c"][ly:ly + 1, :].rearrange(
            "a (p one) -> (a p) one", one=1))
        lq2 = kb.dma("q", qs2_c, io["L0_ffn_qs2_c"][ly:ly + 1, :].rearrange(
            "a (p one) -> (a p) one", one=1))
        lk2 = kb.dma("q", ks2_c, io["L0_ffn_ks2_c"][ly:ly + 1, :].rearrange(
            "a (p one) -> (a p) one", one=1))
        tb = []
        for j, (nm, col) in enumerate([("_xC1s", qs_c), ("_xC2s", qs2_c),
                                       ("_xC1i", ks_c), ("_xC2i", ks2_c)]):
            tt_ = kb.dma("g", ctx.CAR[:, j * 4096:(j + 1) * 4096], io[nm],
                         [kb.tok("v")])
            tb.append(kb.on("v", lambda v, o=ctx.CAR[:, j * 4096:(j + 1) * 4096],
                            cc=col: v.tensor_scalar_mul(o, o, cc[0:128, 0:1]),
                            [tt_, lq, lk, lq2, lk2], inc="v"))
        qrow = ctx.CF[0:1, 2304:2368]
        krow = ctx.CF[0:1, 2368:2432]
        lqr = kb.dma("q", qrow, io["L0_ffn_qs_r"][ly:ly + 1, 0:64])
        lkr = kb.dma("q", krow, io["L0_ffn_ks_r"][ly:ly + 1, 0:64])
        mq = ctx.CF[0:1, 2432:2433]
        mk = ctx.CF[0:1, 2433:2434]
        r1_ = kb.on("v", lambda v, o=mq, i=qrow: v.reduce_max(o, i, axis=AX,
                    apply_absolute_value=True), [lqr], inc="v")
        r2_ = kb.on("v", lambda v, o=mk, i=krow: v.reduce_max(o, i, axis=AX,
                    apply_absolute_value=True), [lkr, r1_], inc="v")
        r3_ = kb.on("v", lambda v, o=mq, b=mk: v.scalar_tensor_tensor(o, o, -8.0, b,
                    ALU.mult, ALU.mult), [r2_], inc="v")
        nmly = ctx.CF[:, 2434:2435]
        bslot = ctx.spool.take()
        bm_ = kb.on("t", lambda te, o=bslot[0][:, 0:1], l=ctx.ones_rowf[0:1, 0:128],
                    r=ctx.CF[0:1, 2432:2433]: te.matmul(o, l, r, start=True, stop=True),
                    [r3_, bslot[1]], inc="t")
        bm2_ = kb.on("s", lambda sE, o=nmly, i=bslot[0][:, 0:1]:
                     sE.activation(o, i, AF.Identity), [bm_], inc="s")
        bslot[1] = bm2_
        kb.fence()
        wq = load_w_group(ctx, io["L0_ffn_Wqkv" + sfx], D, 0, 3072, 0)
        xfree, stgf = {}, {}
        for ti in range(NTT):
            t0 = ti * TT
            par = ti % 2
            xo = 24576 + par * 4096
            src = XB[:, 0:NCH * XBW].rearrange("p (c t) -> p c t", t=XBW)
            src = bass.AP(src.tensor, src.offset, list(src.ap))
            d1 = kb.dma("q", ctx.BIG[:, xo:xo + 2048].rearrange("p (c t) -> p c t", t=TT),
                        XB[:, 0:4 * XBW].rearrange("p (c t) -> p c t", t=XBW)
                        [:, :, XBO + t0:XBO + t0 + TT], [xfree.get(par)])
            d2 = kb.dma("q", ctx.BIG[:, xo + 2048:xo + 4096].rearrange("p (c t) -> p c t", t=TT),
                        XB[:, 4 * XBW:8 * XBW].rearrange("p (c t) -> p c t", t=XBW)
                        [:, :, XBO + t0 - 1:XBO + t0 - 1 + TT], [xfree.get(par)])
            xt = [d1, d2]
            xa = lambda ci, xo=xo: ctx.BIG[:, xo + ci * TT:xo + ci * TT + TT]
            for dc in range(16):
                isq = dc < 8
                c = dc % 8
                slot = ctx.pool.take()
                ps = slot[0]
                for ci in range(8):
                    mdeps = (wq + xt + [slot[1]]) if ci == 0 else ()
                    tok = kb.on("t", lambda te, o=ps[:, 0:TT],
                                l=slab_lhsT(ctx, 0, 3072, ci, dc * 128, 128, 128),
                                r=xa(ci), st=(ci == 0), sp=(ci == 7):
                                te.matmul(o, l, r, start=st, stop=sp),
                                mdeps, inc="t" if ci == 7 else None)
                pz = dc % 2
                sbb = 36864 + pz * 2048
                sR = ctx.BIG[:, sbb:sbb + 512]
                s2 = ctx.BIG[:, sbb + 512:sbb + 1024]
                s3 = ctx.BIG[:, sbb + 1024:sbb + 1536]
                s4 = ctx.BIG[:, sbb + 1536:sbb + 2048]
                sq = ctx.FW[:, 3072 + pz * 512:3072 + pz * 512 + 512]
                fr = ctx.FW[0:16, pz * 512:pz * 512 + 512]
                e0 = kb.on("s", lambda sE, o=sR, i=ps[:, 0:TT]:
                           sE.activation(o, i, AF.Identity),
                           [tok] + (stgf.get((0, pz)) or []), inc="s")
                slot[1] = e0
                e1 = kb.on("s", lambda sE, o=sq, i=sR: sE.activation(o, i, AF.Square),
                           [e0] + (stgf.get((1, pz)) or []), inc="s")
                nslot = ctx.pool.take()
                n1 = kb.on("t", lambda te, o=nslot[0][0:16, 0:TT],
                           l=ind16[:, c * 16:c * 16 + 16], r=sq:
                           te.matmul(o, l, r, start=True, stop=True),
                           [e1, li, nslot[1]], inc="t")
                f1 = kb.on("s", lambda sE, o=fr, i=nslot[0][0:16, 0:TT]:
                           sE.activation(o, i, AF.Sqrt),
                           [n1] + (stgf.get((2, pz)) or []), inc="s")
                f2 = kb.on("v", lambda v, o=fr: v.tensor_scalar_max(o, o, 1e-12),
                           [f1], inc="v")
                f3 = kb.on("v", lambda v, o=fr: v.reciprocal(o, o), [f2], inc="v")
                nslot[1] = f1
                fslot = ctx.pool.take()
                fb = kb.on("t", lambda te, o=fslot[0][:, 0:TT],
                           l=selc[0:16, c * 128:c * 128 + 128], r=fr:
                           te.matmul(o, l, r, start=True, stop=True),
                           [f3, lc, fslot[1]], inc="t")
                v0 = kb.on("v", lambda v, o=s2, a=sR, b=fslot[0][:, 0:TT]:
                           v.tensor_mul(o, a, b), [fb, e0] + (stgf.get((3, pz)) or []), inc="v")
                fslot[1] = v0
                rslot = ctx.pool.take()
                rm = kb.on("t", lambda te, o=rslot[0][:, 0:TT], l=Prb, r=s2:
                           te.matmul(o, l, r, start=True, stop=True),
                           [v0, lpr, rslot[1]], inc="t")
                C1 = ctx.CAR[:, (0 if isq else 8192) + t0:(0 if isq else 8192) + t0 + TT]
                C2 = ctx.CAR[:, (4096 if isq else 12288) + t0:(4096 if isq else 12288) + t0 + TT]
                w1_ = kb.on("v", lambda v, o=s3, a=s2, b=C1: v.tensor_mul(o, a, b),
                            [v0, tb[0], tb[2]] + (stgf.get((4, pz)) or []), inc="v")
                w2_ = kb.on("v", lambda v, o=s4, a=rslot[0][:, 0:TT], b=C2:
                            v.tensor_mul(o, a, b),
                            [rm, tb[1], tb[3]] + (stgf.get((5, pz)) or []), inc="v")
                rslot[1] = w2_
                w3_ = kb.on("v", lambda v, o=s3, a=s3, b=s4: v.tensor_add(o, a, b),
                            [w1_, w2_], inc="v")
                dd = kb.dma("g", (QS if isq else KS)[c * 128:c * 128 + 128, t0:t0 + TT],
                            s3, [w3_])
                stgf[(0, pz)] = [e1, v0]
                stgf[(1, pz)] = [n1]
                stgf[(2, pz)] = [fb]
                stgf[(3, pz)] = [rm, w1_]
                stgf[(4, pz)] = [w3_, dd]
                stgf[(5, pz)] = [w3_]
            for sub in range(4):
                stgb = ctx.BIG[:, 34816 + (sub % 2) * 1024:34816 + (sub % 2) * 1024 + 1024]
                evs = []
                for sl in range(2):
                    slot = ctx.pool.take()
                    ps = slot[0]
                    for ci in range(8):
                        mdeps = (wq + xt + [slot[1]]) if ci == 0 else ()
                        tok = kb.on("t", lambda te, o=ps[:, 0:512],
                                    l=ctx.BIG[:, xo + ci * TT + sub * 128:xo + ci * TT + sub * 128 + 128],
                                    r=slab_lhsT(ctx, 0, 3072, ci, 2048 + sl * 512, 512, 128),
                                    st=(ci == 0), sp=(ci == 7):
                                    te.matmul(o, l, r, start=st, stop=sp),
                                    mdeps, inc="t" if ci == 7 else None)
                    ev = kb.on("s", lambda sE, o=stgb[:, sl * 512:sl * 512 + 512],
                               i=ps[:, 0:512]: sE.activation(o, i, AF.Identity),
                               [tok] + (stgf.get(10 + sub % 2) or []), inc="s")
                    evs.append(ev)
                    slot[1] = ev
                dv = kb.dma("g", VS[t0 + sub * 128:t0 + sub * 128 + 128, 0:1024], stgb, evs)
                stgf[10 + sub % 2] = [dv]
            xfree[par] = kb.tok("t")
        kb.fence()
        bk = {}
        ufp = {}
        for w in range(16):
            for hp in range(8):
                par = (w * 8 + hp) % 2
                base = par * 4096
                QT = ctx.BIG[:, base:base + 256]
                KT = ctx.BIG[:, base + 256:base + 768]
                VT = ctx.BIG[:, base + 768:base + 1280]
                PT = ctx.BIG[:, base + 1280:base + 1536]
                OT = ctx.BIG[:, base + 1536:base + 1792]
                TMP = ctx.BIG[:, base + 1792:base + 2048]
                uf = ufp.get(par) or []
                t0q = 256 * w
                lq_ = kb.dma("q", QT, QS[hp * 128:hp * 128 + 128, t0q:t0q + 256], uf)
                loads = [lq_]
                if w > 0:
                    loads.append(kb.dma("q", KT[:, 0:256],
                                 KS[hp * 128:hp * 128 + 128, t0q - 256:t0q], []))
                    loads.append(kb.dma("q", VT[:, 0:256].rearrange("p (sb c) -> p sb c", c=128),
                                 VS[t0q - 256:t0q, hp * 128:hp * 128 + 128].rearrange(
                                     "(sb p) c -> p sb c", p=128), []))
                loads.append(kb.dma("q", KT[:, 256:512],
                             KS[hp * 128:hp * 128 + 128, t0q:t0q + 256], []))
                loads.append(kb.dma("q", VT[:, 256:512].rearrange("p (sb c) -> p sb c", c=128),
                             VS[t0q:t0q + 256, hp * 128:hp * 128 + 128].rearrange(
                                 "(sb p) c -> p sb c", p=128), []))
                kcs = [0, 1, 2, 3] if w > 0 else [2, 3]
                masks = {0: (1, 0), 1: (1, 128), 2: (-1, 0), 3: (-1, -128)}
                for h2 in range(2):
                    hb = h2 * 64
                    nm1 = nmly
                    c1 = bm2_
                    lm = pv = None
                    for ki, kc in enumerate(kcs):
                        psS = ctx.ps[ki % 2]
                        mm = kb.on("t", lambda te, o=psS[:, 0:256],
                                   l=KT[hb:hb + 64, kc * 128:kc * 128 + 128],
                                   r=QT[hb:hb + 64, 0:256]:
                                   te.matmul(o, l, r, start=True, stop=True),
                                   loads + [c1, bk.get(ki % 2)], inc="t")
                        ex = kb.on("s", lambda sE, o=PT, i=psS[:, 0:256], bb=nm1:
                                   sE.activation(o, i, AF.Exp, bias=bb),
                                   [mm] + (bk.get("pt") or []), inc="s")
                        bk[ki % 2] = ex
                        cmul, bas = masks[kc]
                        mk = kb.on("g", lambda g, o=PT, cm=cmul, bb=bas:
                                   g.affine_select(o, o, pattern=[[-cm, 256]],
                                                   compare_op=ALU.is_ge, fill=0.0,
                                                   base=bb, channel_multiplier=cm),
                                   [ex], inc="g")
                        lm = kb.on("t", lambda te, o=ctx.ps[4][0:1, 0:256],
                                   l=ctx.ones_colb[:, 0:1], r=PT,
                                   st=(ki == 0), sp=(ki == len(kcs) - 1):
                                   te.matmul(o, l, r, start=st, stop=sp),
                                   [mk] + ([bk.get(4)] if ki == 0 else []), inc="t")
                        pv = kb.on("t", lambda te, o=ctx.ps[2][hb:hb + 64, 0:256],
                                   l=VT[:, kc * 128 + hb:kc * 128 + hb + 64],
                                   r=PT, tp=(0, hb), st=(ki == 0), sp=(ki == len(kcs) - 1):
                                   te.matmul(o, l, r, start=st, stop=sp, tile_position=tp),
                                   [mk] + ([bk.get(2)] if ki == 0 else []), inc="t")
                        bk["pt"] = [lm, pv]
                        bk[("nm", par)] = [ex]
                    lr = ctx.FW2[0:1, 0:256]
                    a1 = kb.on("v", lambda v, o=lr, i=ctx.ps[4][0:1, 0:256]:
                               v.tensor_scalar_add(o, i, 1e-30),
                               [lm, bk.get("lr")], inc="v")
                    a2 = kb.on("v", lambda v, o=lr, i=lr: v.reciprocal(o, i), [a1], inc="v")
                    bk[4] = a2
                    bc = kb.on("t", lambda te, o=ctx.ps[6][hb:hb + 64, 0:256],
                               l=ctx.ones_rowf[0:1, 0:64], r=lr, tp=(0, hb):
                               te.matmul(o, l, r, start=True, stop=True, tile_position=tp),
                               [a2, bk.get(6)], inc="t")
                    bk["lr"] = bc
                    cp = kb.on("s", lambda sE, o=TMP[hb:hb + 64, 0:256],
                               i=ctx.ps[6][hb:hb + 64, 0:256]:
                               sE.activation(o, i, AF.Identity),
                               [bc, bk.get(("tmp", par, h2))], inc="s")
                    bk[6] = cp
                    mA = kb.on("v", lambda v, o=OT[hb:hb + 64, 0:256],
                               a=ctx.ps[2][hb:hb + 64, 0:256], bb=TMP[hb:hb + 64, 0:256]:
                               v.tensor_mul(o, a, bb),
                               [cp, pv, bk.get(("ot", par))], inc="v")
                    bk[2] = mA
                    bk[("tmp", par, h2)] = mA
                otok = kb.dma("g", AO[hp * 128:hp * 128 + 128, t0q:t0q + 256], OT,
                              [kb.tok("v")])
                bk[("ot", par)] = otok
                ufp[par] = [kb.tok("t"), kb.tok("s"), kb.tok("v"), otok]
        kb.fence()
        ra = ResAdd(ctx, X, XB)

        def xloadA(ci, t0, tw, par):
            o = 8192 + par * 4096 + ci * 512
            return AO[ci * 128:ci * 128 + 128, t0:t0 + tw], ctx.BIG[:, o:o + tw], []

        mmB(ctx, "l0fo" + sfx, io["L0_ffn_Wo" + sfx], D, D, xloadA, ra.evac,
            woff=0, xoff=8192, xbufs=2)
        kb.fence()
        lngr = ctx.CF[0:1, 0:1024]
        lnbc = ctx.CF[:, 2244:2252]
        g1 = kb.dma("q", lngr, io["L0_ffn_lng_r"][ly:ly + 1, 0:1024])
        g2 = kb.dma("q", lnbc, io["L0_ffn_lnb_c"][:, ly * 8:ly * 8 + 8])
        sA = sB = None
        fwf, stoks = {}, []
        for ti in range(NTT):
            t0 = ti * TT
            for dc in range(NCH):
                d0 = dc * 128
                sl = dc % 4
                s_tile = ctx.FW[:, sl * 512:sl * 512 + TT]
                dd = (fwf.get(sl) or [])
                if dc >= 4 and ti == 0:
                    z1 = kb.on("g", lambda g, o=s_tile[:, 0:1]: g.memset(o, 0.0), dd, inc="g")
                    l1 = kb.dma("q", s_tile[:, 1:TT], X[d0:d0 + 128, 0:TT - 1], dd + [z1])
                else:
                    off = t0 - 1 if dc >= 4 else t0
                    l1 = kb.dma("q", s_tile, X[d0:d0 + 128, off:off + TT], dd)
                t3tok = kb.on("s", lambda sE, o=ctx.FW[:, 2048 + sl * 512:2048 + sl * 512 + TT],
                              i=s_tile: sE.activation(o, i, AF.Square), [l1], inc="s")
                if dc == 0:
                    sA = ctx.spool.take()
                    sB = ctx.spool.take()
                first = (dc == 0)
                ts1 = kb.on("t", lambda te, o=sA[0][0:1, 0:TT], l=ctx.ones_colf[:, 0:1],
                            r=s_tile, st=first, sp=(dc == NCH - 1):
                            te.matmul(o, l, r, start=st, stop=sp),
                            [l1] + ([sA[1]] if first else []), inc="t")
                ts2 = kb.on("t", lambda te, o=sB[0][0:1, 0:TT], l=ctx.ones_colf[:, 0:1],
                            r=ctx.FW[:, 2048 + sl * 512:2048 + sl * 512 + TT],
                            st=first, sp=(dc == NCH - 1):
                            te.matmul(o, l, r, start=st, stop=sp),
                            [t3tok] + ([sB[1]] if first else []), inc="t")
                fwf[sl] = [ts1, ts2]
                if dc == NCH - 1:
                    c1_ = kb.on("v", lambda v, o=ctx.STATS[0:1, t0:t0 + TT],
                                i=sA[0][0:1, 0:TT]: v.tensor_copy(o, i), [ts1, ts2], inc="v")
                    c2_ = kb.on("v", lambda v, o=ctx.STATS[0:1, 4096 + t0:4096 + t0 + TT],
                                i=sB[0][0:1, 0:TT]: v.tensor_copy(o, i), [c1_], inc="v")
                    sA[1] = c1_
                    sB[1] = c2_
                    stoks.append(c2_)
        kb.fence()
        ST = ctx.STATS
        mrow = ctx.FW[0:1, 0:T]
        t1_ = kb.on("v", lambda v: v.tensor_scalar_mul(ST[0:1, 0:T], ST[0:1, 0:T], 1.0 / D),
                    stoks, inc="v")
        t2_ = kb.on("v", lambda v: v.tensor_scalar_mul(ST[0:1, 4096:4096 + T],
                    ST[0:1, 4096:4096 + T], 1.0 / D), [t1_], inc="v")
        t3_ = kb.on("v", lambda v: v.tensor_mul(mrow, ST[0:1, 0:T], ST[0:1, 0:T]),
                    [t2_], inc="v")
        t4_ = kb.on("v", lambda v: v.tensor_sub(mrow, ST[0:1, 4096:4096 + T], mrow),
                    [t3_], inc="v")
        t5_ = kb.on("s", lambda sE: sE.activation(mrow, mrow, AF.Sqrt,
                    bias=ctx.ceps[0:1, 0:1]), [t4_], inc="s")
        t6_ = kb.on("v", lambda v: v.reciprocal(mrow, mrow), [t5_], inc="v")
        t7_ = kb.on("v", lambda v: v.tensor_mul(ST[0:1, 0:T], ST[0:1, 0:T], mrow),
                    [t6_], inc="v")
        kb.fence()
        xnf = {}
        for dc in range(NCH):
            d0 = dc * 128
            for ti in range(NTT):
                t0 = ti * TT
                sl = dc % 4
                slot = ctx.spool.take()
                sps = slot[0]
                mm1 = kb.on("t", lambda te, o=sps[:, 0:TT],
                            l=lngr[0:1, dc * 128:dc * 128 + 128], r=mrow[0:1, t0:t0 + TT]:
                            te.matmul(o, l, r, start=True, stop=True),
                            [slot[1], g1, t6_], inc="t")
                pslot = ctx.pool.take()
                pps = pslot[0]
                mm2 = kb.on("t", lambda te, o=pps[:, 0:TT],
                            l=lngr[0:1, dc * 128:dc * 128 + 128], r=ST[0:1, t0:t0 + TT]:
                            te.matmul(o, l, r, start=True, stop=True),
                            [pslot[1], t7_], inc="t")
                xt2 = ctx.FW2[:, sl * 512:sl * 512 + TT]
                dd = [xnf.get(sl)]
                if dc >= 4 and ti == 0:
                    z1 = kb.on("g", lambda g, o=xt2[:, 0:1]: g.memset(o, 0.0), dd, inc="g")
                    lt = kb.dma("q", xt2[:, 1:TT], X[d0:d0 + 128, 0:TT - 1], dd + [z1])
                else:
                    off = t0 - 1 if dc >= 4 else t0
                    lt = kb.dma("q", xt2, X[d0:d0 + 128, off:off + TT], dd)
                v1_ = kb.on("v", lambda v, o=xt2, a=xt2, b=sps[:, 0:TT]:
                            v.tensor_mul(o, a, b), [mm1, lt], inc="v")
                xn = ctx.BIG[:, dc * 4096 + t0:dc * 4096 + t0 + TT]
                v2_ = kb.on("v", lambda v, o=xn, a=xt2, sc=lnbc[:, dc:dc + 1],
                            b=pps[:, 0:TT]:
                            v.scalar_tensor_tensor(o, a, sc, b, ALU.add, ALU.subtract),
                            [mm2, v1_, g2, xnf.get(("b", dc, ti))], inc="v")
                slot[1] = v2_
                pslot[1] = v2_
                xnf[sl] = v2_
        kb.fence()
        xsrc = lambda ci, t0, tw: (ctx.BIG[:, ci * 4096 + t0:ci * 4096 + t0 + tw], [])
        b1a = ctx.CF[:, 2252:2252 + 22]
        b1g = ctx.CF[:, 2274:2274 + 22]
        ba1 = kb.dma("q", b1a, io["L0_ffn_b1a_c"][:, ly * 22:ly * 22 + 22])
        ba2 = kb.dma("q", b1g, io["L0_ffn_b1g_c"][:, ly * 22:ly * 22 + 22])
        hfree, stash_free = {}, {}

        def evacW1(d0, dw, t0, tw, slot, deps):
            ps = slot[0]
            dc = d0 // 128
            hc = dc // 2
            ti2 = t0 // TT
            stash = ctx.BIG[0:dw, 40960 + ti2 * 512:40960 + ti2 * 512 + tw]
            if dc % 2 == 0:
                tok = kb.on("s", lambda sE, o=stash, i=ps[0:dw, 0:tw],
                            bb=b1a[0:dw, hc:hc + 1]:
                            sE.activation(o, i, AF.Identity, bias=bb),
                            list(deps) + [ba1] + (stash_free.get(ti2) or []), inc="s")
                slot[1] = tok
                stash_free[ti2] = [tok]
            else:
                par2 = (hc * NTT + ti2) % 2
                hb_ = ctx.BIG[0:dw, 45056 + par2 * 512:45056 + par2 * 512 + tw]
                g1_ = kb.on("s", lambda sE, o=hb_, i=ps[0:dw, 0:tw],
                            bb=b1g[0:dw, hc:hc + 1]:
                            sE.activation(o, i, AF.Gelu, bias=bb),
                            list(deps) + [ba2] + [hfree.get(par2)], inc="s")
                tok = kb.on("v", lambda v, o=hb_, a=hb_, b=stash:
                            v.tensor_mul(o, a, b),
                            [g1_] + (stash_free.get(ti2) or []), inc="v")
                slot[1] = tok
                stash_free[ti2] = (stash_free.get(ti2) or []) + [tok]
                htok = kb.dma("g", H[hc * 128:hc * 128 + dw, t0:t0 + tw], hb_, [tok])
                hfree[par2] = htok

        mmA(ctx, "l0w1" + sfx, io["L0_ffn_W1p" + sfx], D, 5632, xsrc, evacW1,
            slab_off=32768, gw=512)
        kb.fence()
        b2c = ctx.CF[:, 2296:2304]
        bb2 = kb.dma("q", b2c, io["L0_ffn_b2_c"][:, ly * 8:ly * 8 + 8])
        kb.fence()
        ra2 = ResAdd(ctx, X, XB,
                     bias_col=lambda d0, dw: b2c[0:dw, d0 // 128:d0 // 128 + 1])

        def xloadH(ci, t0, tw, par):
            o = 22528 + ci * 512
            return H[ci * 128:ci * 128 + 128, t0:t0 + tw], ctx.BIG[:, o:o + tw], []

        mmB(ctx, "l0w2" + sfx, io["L0_ffn_W2p" + sfx], 2816, D, xloadH, ra2.evac,
            woff=0, xoff=22528)
        kb.fence()
    if L0FFN_OUTER:
        postln_sum(ctx, X, X0, XB)


def emit_L2_attn(ctx, io, X, XB, H):
    nc, kb = ctx.nc, ctx.kb
    QF, KF, VF, AO = io["_QS"], io["_KS"], io["_VS2"], io["_AO"]
    KG, VG = io["_KG"], io["_VG"]
    ind16 = ctx.CF[:, 2112:2112 + 128]
    selc = ctx.CF[0:16, 1024:1024 + 1024]
    li = kb.dma("q", ind16, io["_ind16"])
    lc = kb.dma("q", selc, io["_selc"])
    selcb = ctx.CAR[0:16, 0:1024]
    lcb = kb.on("v", lambda v: v.tensor_copy(selcb, selc), [lc], inc="v")
    ind16b = ctx.CAR[:, 1024:1152]
    lib = kb.on("v", lambda v: v.tensor_copy(ind16b, ctx.CF[:, 2112:2112 + 128]),
                [li], inc="v")
    cmc = ctx.CF[0:16, 2240:2248]
    lcm = kb.dma("q", cmc, io["_l2cm"])
    bq = ctx.CF[:, 2248:2272]
    lbq = kb.dma("q", bq, io["L2_attn_qkv_b_c"])
    lbq = kb.on("v", lambda v: v.tensor_scalar_mul(ctx.CF[:, 2248:2256],
                ctx.CF[:, 2248:2256], 0.125), [lbq], inc="v")
    kb.fence()
    wq = load_w_group(ctx, io["L2_attn_qkv_wT"], D, 0, 3072, 0)
    xoff, stg = 24576, 33280
    xfree, stgf = {}, {}
    for ti in range(NTT):
        t0 = ti * TT
        par = ti % 2
        xo = xoff + par * 4096
        xt = kb.dma("q", ctx.BIG[:, xo:xo + 4096].rearrange("p (c t) -> p c t", t=TT),
                    XB[:, 0:NCH * XBW].rearrange("p (c t) -> p c t", t=XBW)
                    [:, :, XBO + t0:XBO + t0 + TT], [xfree.get(par)])
        xa = lambda ci, xo=xo: ctx.BIG[:, xo + ci * TT:xo + ci * TT + TT]
        for dc in range(24):
            slot = ctx.pool.take()
            ps = slot[0]
            for ci in range(8):
                mdeps = (wq + [xt, slot[1]]) if ci == 0 else ()
                tok = kb.on("t", lambda te, o=ps[:, 0:TT],
                            l=slab_lhsT(ctx, 0, 3072, ci, dc * 128, 128, 128),
                            r=xa(ci), st=(ci == 0), sp=(ci == 7):
                            te.matmul(o, l, r, start=st, stop=sp),
                            mdeps, inc="t" if ci == 7 else None)
            stgb = ctx.BIG[:, stg + (dc % 2) * 512:stg + (dc % 2) * 512 + TT]
            dst = QF if dc < 8 else (KF if dc < 16 else VF)
            e1 = kb.on("s", lambda sE, o=stgb, i=ps[:, 0:TT], bb=bq[:, dc:dc + 1],
                       sc=(0.125 if dc < 8 else 1.0):
                       sE.activation(o, i, AF.Identity, bias=bb, scale=sc),
                       [tok, lbq] + (stgf.get(dc % 2) or []), inc="s")
            d1 = kb.dma("g", dst[(dc % 8) * 128:(dc % 8) * 128 + 128, t0:t0 + TT],
                        stgb, [e1])
            stgf[dc % 2] = [d1]
            slot[1] = e1
        xfree[par] = kb.tok("t")
    kb.fence()
    cc1 = kb.on("g", lambda g: g.collective_compute(
        "AllGather", ALU.bypass, replica_groups=[list(range(8))],
        ins=[KF[:, :]], outs=[KG[:, :]]), [], inc="cc1")
    kb.on("g", lambda g: g.wait_ge.__self__ if False else g.engine_nop(), [("cc1", 1)])
    cc2 = kb.on("g", lambda g: g.collective_compute(
        "AllGather", ALU.bypass, replica_groups=[list(range(8))],
        ins=[VF[:, :]], outs=[VG[:, :]]), [], inc="cc2")
    kb.fence()
    if L2_DUMP:
        for rr in range(8):
            kb.dma("g", H[2048 + rr * 128:2048 + rr * 128 + 128, 0:4096],
                   KG[rr * 1024:rr * 1024 + 128, 0:4096])
        kb.fence()
    if L2_DELAY:
        for _ in range(100):
            kb.on("v", lambda v: v.tensor_copy(ctx.BIG[:, 0:4096], ctx.BIG[:, 4096:8192]),
                  [], inc="v")
        kb.fence()
    bk = {}
    for si in range(8):
        s0 = si * TT
        lq = kb.dma("q", ctx.BIG[:, 0:4096].rearrange("p (c t) -> p c t", t=TT),
                    QF[:, s0:s0 + TT].rearrange("(c p) t -> p c t", p=128),
                    [bk.get("qfree")])
        macc = ctx.FW2[0:16, 0:512]
        lrow = ctx.FW2[0:16, 512:1024]
        lrow2 = ctx.FW2[0:16, 1536:2048]
        EB = lambda tp: ctx.BIG[0:16, 20480 + tp * 512:20480 + tp * 512 + 512]
        EB2 = lambda tp: ctx.BIG[0:16, 25600 + tp * 512:25600 + tp * 512 + 512]
        prod = ctx.BIG[:, 24576:24576 + 512]
        sc_toks = []
        for tp in range(8):
            par = tp % 2
            ko = 4096 + par * 4096
            lk = kb.dma("q", ctx.BIG[:, ko:ko + 4096].rearrange("p (c t) -> p c t", t=TT),
                        KG[tp * 1024:(tp + 1) * 1024, s0:s0 + TT].rearrange(
                            "(c p) t -> p c t", p=128), [bk.get(("kfree", par))])
            psS = ctx.ps[tp]
            last = None
            for ci in range(8):
                pm = kb.on("v", lambda v, o=prod, a=ctx.BIG[:, ci * TT:ci * TT + TT],
                           b=ctx.BIG[:, ko + ci * TT:ko + ci * TT + TT]:
                           v.tensor_mul(o, a, b),
                           [lq, lk, bk.get("prod")] + ([bk.get(("sb", tp))] if ci == 0 else []),
                           inc="v")
                mm = kb.on("t", lambda te, o=psS[0:16, 0:TT],
                           l=ind16b[:, ci * 16:ci * 16 + 16], r=prod,
                           st=(ci == 0), sp=(ci == 7):
                           te.matmul(o, l, r, start=st, stop=sp),
                           [pm, lib] + ([bk.get(("bankv", tp))] if ci == 0 else []),
                           inc="t")
                bk["prod"] = mm
                last = mm
            bk[("kfree", par)] = kb.tok("v")
            sc_toks.append(last)
            if L2_SERIAL:
                kb.fence()
        SC = ctx.STATS[0:16, 0:4096]
        cps = []
        for tp in range(8):
            cp_ = kb.on("s", lambda sE, o=SC[0:16, tp * 512:tp * 512 + 512],
                        i=ctx.ps[tp][0:16, 0:TT]: sE.activation(o, i, AF.Identity),
                        [sc_toks[tp], bk.get("scf")], inc="s")
            cps.append(cp_)
        bk["scf"] = cps[-1]
        if L2_DUMP:
            kb.dma("g", H[si * 16:si * 16 + 16, 0:4096], SC, [cps[-1]])
        mview = SC.rearrange("h (tp s) -> h s tp", tp=8)
        c0 = kb.on("v", lambda v, o=macc, i=mview: v.reduce_max(o, i, axis=AX),
                   [cps[-1], bk.get("macc")], inc="v")
        esubs = []
        for tp in range(8):
            sb_ = kb.on("v", lambda v, o=ctx.FW2[0:16, 1024:1024 + 512],
                        a=SC[0:16, tp * 512:tp * 512 + 512], b=macc:
                        v.tensor_sub(o, a, b), [c0, bk.get("esub")], inc="v")
            ee = kb.on("s", lambda sE, o=EB(tp), i=ctx.FW2[0:16, 1024:1024 + 512]:
                       sE.activation(o, i, AF.Exp), [sb_] + (bk.get(("eb", tp)) or []),
                       inc="s")
            bk["esub"] = ee
            em_ = kb.on("v", lambda v, o=EB(tp), a=EB(tp), sc=cmc[0:16, tp:tp + 1]:
                        v.tensor_scalar_mul(o, a, sc), [ee, lcm], inc="v")
            esubs.append(em_)
        eview = ctx.BIG[0:16, 20480:20480 + 4096].rearrange("h (tp s) -> h s tp", tp=8)
        lsum = kb.on("v", lambda v, o=lrow, i=eview: v.reduce_sum(o, i, axis=AX),
                     esubs + [bk.get("lrow")], inc="v")
        r1 = kb.on("v", lambda v, o=lrow2, i=lrow: v.tensor_scalar_add(o, i, 1e-30),
                   [lsum, bk.get("lrow2")], inc="v")
        r2 = kb.on("v", lambda v, o=lrow2, i=lrow2: v.reciprocal(o, i), [r1], inc="v")
        if L2_SERIAL:
            kb.fence()
        efin = []
        for tp in range(8):
            ef = kb.on("v", lambda v, o=EB2(tp), a=EB(tp), b=lrow2: v.tensor_mul(o, a, b),
                       [r2] + (bk.get(("eb2", tp)) or []), inc="v")
            efin.append(ef)
        bk["macc"] = r2
        bk["lrow"] = r2
        bk["lrow2"] = kb.tok("v")
        if L2_DUMP:
            kb.dma("g", H[128 + si * 16:128 + si * 16 + 16, 0:512], macc, [c0])
            kb.dma("g", H[256 + si * 16:256 + si * 16 + 16, 0:512], lrow2, [r2])
            for tp_ in range(8):
                kb.dma("g", H[384 + si * 16:384 + si * 16 + 16, tp_ * 512:tp_ * 512 + 512],
                       EB2(tp_), [efin[tp_]])
        acc_init = [None] * 8
        for tp in range(8):
            par = tp % 2
            vo = 12288 + par * 4096
            lv = kb.dma("q", ctx.BIG[:, vo:vo + 4096].rearrange("p (c t) -> p c t", t=TT),
                        VG[tp * 1024:(tp + 1) * 1024, s0:s0 + TT].rearrange(
                            "(c p) t -> p c t", p=128), [bk.get(("vfree", par))])
            for ci in range(8):
                bslot = ctx.ps[tp]
                bm = kb.on("t", lambda te, o=bslot[:, 0:TT],
                           l=selcb[0:16, ci * 128:ci * 128 + 128], r=EB2(tp):
                           te.matmul(o, l, r, start=True, stop=True),
                           [efin[tp], lcb, bk.get(("pvb", tp))], inc="t")
                bk[("eb", tp)] = [bm]
                bk[("eb2", tp)] = [bm]
                acc = ctx.FW[:, ci * 512:ci * 512 + 512]
                if tp == 0:
                    vt = kb.on("v", lambda v, o=acc, a=ctx.BIG[:, vo + ci * TT:vo + ci * TT + TT],
                               b=bslot[:, 0:TT]: v.tensor_mul(o, a, b),
                               [bm, lv, bk.get(("accf", ci))], inc="v")
                else:
                    tmpb = ctx.BIG[:, 24576 + 512:24576 + 1024]
                    vt1 = kb.on("v", lambda v, o=tmpb, a=ctx.BIG[:, vo + ci * TT:vo + ci * TT + TT],
                                b=bslot[:, 0:TT]: v.tensor_mul(o, a, b),
                                [bm, lv, bk.get("tmpb")], inc="v")
                    vt = kb.on("v", lambda v, o=acc, a=acc, b=tmpb: v.tensor_add(o, a, b),
                               [vt1, acc_init[ci]], inc="v")
                    bk["tmpb"] = vt
                bk[("pvb", tp)] = vt
                acc_init[ci] = vt
            bk[("vfree", par)] = kb.tok("v")
            bk[("bankv", tp)] = kb.tok("v")
            if L2_SERIAL:
                kb.fence()
        aots = []
        for ci in range(8):
            aots.append(kb.dma("g", AO[ci * 128:ci * 128 + 128, s0:s0 + TT],
                               ctx.FW[:, ci * 512:ci * 512 + 512], [acc_init[ci]]))
            bk[("accf", ci)] = aots[-1]
        bk["qfree"] = kb.tok("v")
        if L2_SERIAL:
            kb.fence()
    kb.fence()
    bo = ctx.CF[:, 2272:2280]
    lbo = kb.dma("q", bo, io["L2_attn_out_b_c"])
    kb.fence()
    pl = PostLN(ctx, X, XB, bias_col=lambda d0, dw: bo[0:dw, d0 // 128:d0 // 128 + 1])

    def xload(ci, t0, tw, par):
        o = 8192 + par * 4096 + ci * 512
        return AO[ci * 128:ci * 128 + 128, t0:t0 + tw], ctx.BIG[:, o:o + tw], []

    mmB(ctx, "l2o", io["L2_attn_out_wT"], D, D, xload, pl.evac, woff=0, xoff=8192,
        xbufs=2)
    kb.fence()
    pl.apply()


def emit_L2_ffn(ctx, io, X, XB, H):
    nc, kb = ctx.nc, ctx.kb
    GX, Yacc = io["_QS"], io["_Yacc"]
    CNT, CNTG, KGH, GRH = io["_CNT"], io["_CNTG"], io["_KGH"], io["_GRH"]
    UT = ctx.CAR[:, 0:128]
    z0 = kb.on("g", lambda g: g.memset(UT, 1.0), inc="g")
    z1 = kb.on("g", lambda g: g.affine_select(UT, UT, pattern=[[1, 128]],
               compare_op=ALU.is_ge, fill=0.0, base=0, channel_multiplier=-1),
               [z0], inc="g")
    kb.fence()
    wrf = ctx.CF[:, 2240:2304]
    wr = [kb.dma("q", ctx.CF[:, 2240 + ci * 8:2240 + ci * 8 + 8],
                 io["L2_ffn_Wr"][ci * 128:ci * 128 + 128, 0:8]) for ci in range(8)]
    brr = ctx.FW2[0:1, 1576:1584]
    lb = kb.dma("q", brr, io["L2_ffn_br"].rearrange("(a n) -> a n", a=1))
    OH = ctx.FW[:, 0:256]
    RK = ctx.FW[:, 256:512]
    GT = ctx.FW[:, 512:544]
    carry = ctx.FW2[0:1, 1568:1576]
    zc = kb.on("g", lambda g: g.memset(carry, 0.0), inc="g")
    cslot = ctx.ps[6]
    crow_tok = None
    kb.fence()
    xfree = {}
    for tile in range(32):
        t0 = tile * 128
        par = tile % 2
        xts = []
        for ci in range(8):
            xdst = ctx.FW2[:, par * 512 + (ci % 4) * 128:par * 512 + (ci % 4) * 128 + 128]
            if ci >= 4:
                xdst = ctx.FW[:, 1152 + par * 512 + (ci % 4) * 128:
                              1152 + par * 512 + (ci % 4) * 128 + 128]
            xts.append((xdst, kb.dma("q", xdst, X[ci * 128:ci * 128 + 128, t0:t0 + 128],
                                     [xfree.get((par, ci))])))
        slot = ctx.pool.take()
        ps = slot[0]
        ib = kb.on("t", lambda te, o=ps[:, 0:8], l=ctx.ones_rowf[0:1, 0:128],
                   r=brr: te.matmul(o, l, r, start=True, stop=False),
                   [slot[1], lb], inc=None)
        for ci in range(8):
            tok = kb.on("t", lambda te, o=ps[:, 0:8],
                        l=xts[ci][0],
                        r=wrf[0:128, ci * 8:ci * 8 + 8],
                        sp=(ci == 7): te.matmul(o, l, r, start=False, stop=sp),
                        [xts[ci][1]] + (wr if ci == 0 else []),
                        inc="t" if ci == 7 else None)
        lg = ctx.FW[:, 544 + tile * 8:544 + tile * 8 + 8]
        mx = ctx.FW[:, 800 + tile:800 + tile + 1]
        nmx = ctx.FW[:, 832 + tile:832 + tile + 1]
        ex = ctx.FW[:, 864 + tile * 8:864 + tile * 8 + 8]
        sm = ctx.FW[:, 1120 + tile:1120 + tile + 1]
        e0 = kb.on("v", lambda v, o=lg, i=ps[:, 0:8]: v.tensor_copy(o, i),
                   [tok], inc="v")
        slot[1] = e0
        e1 = kb.on("v", lambda v, o=mx, i=lg: v.reduce_max(o, i, axis=AX), [e0], inc="v")
        e1b = kb.on("v", lambda v, o=nmx, i=mx: v.tensor_scalar_mul(o, i, -1.0),
                    [e1], inc="v")
        e2 = kb.on("s", lambda sE, o=ex, i=lg, bb=nmx: sE.activation(
            o, i, AF.Exp, bias=bb), [e1b], inc="s")
        e3 = kb.on("v", lambda v, o=sm, i=ex: v.reduce_sum(o, i, axis=AX), [e2], inc="v")
        e4 = kb.on("v", lambda v, o=GT[:, tile:tile + 1], i=sm: v.reciprocal(o, i),
                   [e3], inc="v")
        oh_t = OH[:, tile * 8:tile * 8 + 8]
        e5 = kb.on("v", lambda v, o=oh_t, i=lg, sc=mx: v.tensor_scalar(
            o, i, sc, None, ALU.is_ge), [e4], inc="v")
        rslot = ctx.pool.take()
        rp = rslot[0]
        ohb = ctx.BIG[:, 10240 + par * 8:10240 + par * 8 + 8]
        e6 = kb.on("v", lambda v, o=ohb, i=oh_t: v.tensor_copy(o, i),
                   [e5, bk_get(ctx, ("ohb", par))], inc="v")
        i1 = kb.on("t", lambda te, o=rp[:, 0:8], l=ctx.ones_rowf[0:1, 0:128], r=carry,
                   st=True: te.matmul(o, l, r, start=True, stop=False),
                   [rslot[1], e6, crow_tok, zc], inc=None)
        i2 = kb.on("t", lambda te, o=rp[:, 0:8], l=UT, r=ohb:
                   te.matmul(o, l, r, start=False, stop=True), [z1], inc="t")
        e7 = kb.on("v", lambda v, o=RK[:, tile * 8:tile * 8 + 8], i=rp[:, 0:8]:
                   v.tensor_copy(o, i), [i2, bk_get(ctx, "rk")], inc="v")
        rslot[1] = e7
        bk_set(ctx, "rk", e7)
        bk_set(ctx, ("ohb", par), i2)
        c1 = kb.on("t", lambda te, o=cslot[0:1, 0:8], l=ctx.ones_colb[:, 0:1], r=ohb,
                   st=(tile == 0), sp=(tile == 31):
                   te.matmul(o, l, r, start=(tile == 0), stop=(tile == 31)),
                   [e6] + ([crow_tok] if tile == 0 else []), inc="t")
        crow_tok = kb.on("s", lambda sE, o=carry, i=cslot[0:1, 0:8]:
                         sE.activation(o, i, AF.Identity), [c1], inc="s")
        for ci in range(8):
            xfree[(par, ci)] = tok
    d1 = kb.dma("q", CNT[0:1, 0:8], carry, [crow_tok])
    cc3 = kb.on("g", lambda g: g.collective_compute(
        "AllGather", ALU.bypass, replica_groups=[list(range(8))],
        ins=[CNT[:, :]], outs=[CNTG[:, :]]), [d1], inc="cc3")
    kb.fence()
    pfx = ctx.FW2[0:8, 1552:1553]
    gcnt = ctx.FW2[0:8, 1560:1568]
    lg1 = kb.dma("q", pfx, io["_l2pfx"])
    lg2 = kb.dma("q", gcnt, CNTG[:, :], [("cc3", 1)])
    bslot = ctx.spool.take()
    b1 = kb.on("t", lambda te, o=bslot[0][0:8, 0:1], l=gcnt, r=pfx:
               te.matmul(o, l, r, start=True, stop=True), [lg1, lg2, bslot[1]], inc="t")
    baser = ctx.FW2[0:1, 1536:1544]
    basec = ctx.FW2[0:8, 1544:1545]
    b2 = kb.on("s", lambda sE, o=basec, i=bslot[0][0:8, 0:1]:
               sE.activation(o, i, AF.Identity), [b1], inc="s")
    bslot[1] = b2
    d2 = kb.dma("q", GRH[0:8, 0:1], basec, [b2])
    d3 = kb.dma("q", baser, GRH[0:8, 0:1].rearrange("(a b) c -> a (b c)", a=1), [d2])
    kb.fence()
    bkd = {}
    for tile in range(32):
        t0 = tile * 128
        slot = ctx.spool.take()
        bp = slot[0]
        i1 = kb.on("t", lambda te, o=bp[:, 0:8], l=ctx.ones_rowf[0:1, 0:128], r=baser:
                   te.matmul(o, l, r, start=True, stop=True), [slot[1], d3], inc="t")
        pos = ctx.FW2[:, 0:8]
        kp = ctx.FW2[:, 8:16]
        kgt = ctx.FW2[:, 16:24]
        p1 = kb.on("v", lambda v, o=pos, a=RK[:, tile * 8:tile * 8 + 8], b=bp[:, 0:8]:
                   v.tensor_add(o, a, b), [i1, bkd.get("pos")], inc="v")
        p2 = kb.on("v", lambda v, o=kp, i=pos: v.tensor_single_scalar(o, i, 4096.0, ALU.is_lt),
                   [p1], inc="v")
        p3 = kb.on("v", lambda v, o=kgt, a=OH[:, tile * 8:tile * 8 + 8],
                   sc=GT[:, tile:tile + 1], b=kp:
                   v.scalar_tensor_tensor(o, a, sc, b, ALU.mult, ALU.mult), [p2], inc="v")
        slot[1] = p3
        bkd["pos"] = p3
        tsl2 = ctx.pool.take()
        tr = kb.on("t", lambda te, o=tsl2[0][0:8, 0:128], i=kgt, idn=ctx.identf[:, :]:
                   te.transpose(o, i, idn), [p3, tsl2[1]], inc="t")
        g3 = kb.on("s", lambda sE, o=ctx.FW2[0:8, 1664:1664 + 128], i=tsl2[0][0:8, 0:128]:
                   sE.activation(o, i, AF.Identity), [tr, bkd.get("kghs")], inc="s")
        g4 = kb.dma("q", KGH[0:8, t0:t0 + 128], ctx.FW2[0:8, 1664:1664 + 128], [g3])
        tsl2[1] = g3
        bkd["kghs"] = g4
    kb.fence()
    tg = ctx.pool.take()
    tr1 = kb.on("t", lambda te, o=tg[0][0:32, 0:128], i=GT, idn=ctx.identf[:, :]:
                te.transpose(o, i, idn), [kb.tok("v"), tg[1]], inc="t")
    tg2 = kb.on("s", lambda sE, o=ctx.FW2[0:32, 1792:1792 + 128], i=tg[0][0:32, 0:128]:
                sE.activation(o, i, AF.Identity), [tr1], inc="s")
    tg[1] = tg2
    dg1 = kb.dma("q", io["_GTH"][:, :], ctx.FW2[0:32, 1792:1792 + 128], [tg2])
    grow = ctx.STATS[0:1, 4096:4096 + 4096]
    dg3 = kb.dma("q", grow, io["_GTH"][0:32, 0:128].rearrange("(a p) c -> a (p c)", a=1),
                 [dg1])
    kb.fence()
    gxf = {}
    for ti in range(NTT):
        t0 = ti * TT
        gsl = ctx.spool.take()
        gb = kb.on("t", lambda te, o=gsl[0][:, 0:TT], l=ctx.ones_rowf[0:1, 0:128],
                   r=grow[0:1, t0:t0 + TT]: te.matmul(o, l, r, start=True, stop=True),
                   [dg3, gsl[1]], inc="t")
        lastv = None
        for ci in range(NCH):
            xt = kb.dma("q", ctx.BIG[:, 8192 + ci * 512:8192 + ci * 512 + TT],
                        XB[:, ci * XBW + XBO + t0:ci * XBW + XBO + t0 + TT],
                        [gxf.get(ci)])
            v1 = kb.on("v", lambda v, o=ctx.BIG[:, 8192 + ci * 512:8192 + ci * 512 + TT],
                       a=ctx.BIG[:, 8192 + ci * 512:8192 + ci * 512 + TT], b=gsl[0][:, 0:TT]:
                       v.tensor_mul(o, a, b), [xt, gb], inc="v")
            d_ = kb.dma("g", GX[ci * 128:ci * 128 + 128, t0:t0 + TT],
                        ctx.BIG[:, 8192 + ci * 512:8192 + ci * 512 + TT], [v1])
            gxf[ci] = d_
            lastv = v1
        gsl[1] = lastv
    kb.fence()
    b1c = ctx.CF[:, 2240:2240 + 64]
    lb1 = kb.dma("q", b1c, io["L2_ffn_b1_c"])
    b2r = ctx.CAR[0:1, 4096:4096 + 8192]
    lb2r = kb.on("g", lambda g: g.dma_start(ctx.CAR[0:1, 4096:4096 + 8192],
                 io["L2_ffn_b2r"][0:1, :]), [], inc="dg", inc_amt=16)
    kgrow_f = ctx.STATS[0:1, 0:4096]
    kgrow_b = ctx.CAR[0:1, 12288:12288 + 4096]
    kb.fence()
    hfree = {}
    for e in range(8):
        we1 = load_w_group(ctx, io[f"L2_ffn_W1_{e}"], D, 0, 1024, 0,
                           deps=[bk_get(ctx, "wfree")])
        we2 = load_w_group(ctx, io[f"L2_ffn_W2_{e}"], D, 0, 1024, 8192,
                           deps=[bk_get(ctx, "wfree")])
        lkg = kb.dma("q", kgrow_f, KGH[e:e + 1, 0:4096], [bk_get(ctx, "kgrow")])
        lkgb = kb.dma("g", kgrow_b, KGH[e:e + 1, 0:4096], [bk_get(ctx, "kgrow")])
        for ti in range(NTT):
            t0 = ti * TT
            par = ti % 2
            xo = 16384 + par * 4096
            xt = kb.dma("q", ctx.BIG[:, xo:xo + 4096].rearrange("p (c t) -> p c t", t=TT),
                        GX[:, t0:t0 + TT].rearrange("(c p) t -> p c t", p=128),
                        [hfree.get(("x", par))])
            kslot = ctx.spool.take()
            kgb_ = kb.on("t", lambda te, o=kslot[0][:, 0:TT], l=ctx.ones_rowf[0:1, 0:128],
                         r=kgrow_f[0:1, t0:t0 + TT]: te.matmul(o, l, r, start=True, stop=True),
                         [lkg, kslot[1]], inc="t")
            hts = []
            for dc in range(8):
                slot = ctx.pool.take()
                ps = slot[0]
                for ci in range(8):
                    mdeps = (we1 + [xt, slot[1]]) if ci == 0 else ()
                    tok = kb.on("t", lambda te, o=ps[:, 0:TT],
                                l=slab_lhsT(ctx, 0, 1024, ci, dc * 128, 128, 128),
                                r=ctx.BIG[:, xo + ci * TT:xo + ci * TT + TT],
                                st=(ci == 0), sp=(ci == 7):
                                te.matmul(o, l, r, start=st, stop=sp),
                                mdeps, inc="t" if ci == 7 else None)
                hstg = ctx.BIG[:, 24576 + dc * 512:24576 + dc * 512 + TT]
                h1 = kb.on("s", lambda sE, o=hstg, i=ps[:, 0:TT],
                           bb=b1c[:, e * 8 + dc:e * 8 + dc + 1]:
                           sE.activation(o, i, AF.Relu, bias=bb),
                           [tok, lb1] + (hfree.get(dc) or []), inc="s")
                h2 = kb.on("v", lambda v, o=hstg, a=hstg, b=kslot[0][:, 0:TT]:
                           v.tensor_mul(o, a, b), [h1, kgb_], inc="v")
                slot[1] = h2
                hts.append(h2)
                hfree[dc] = [h2]
            kslot[1] = hts[-1]
            for dc in range(8):
                slot = ctx.pool.take()
                ps = slot[0]
                ib = kb.on("t", lambda te, o=ps[:, 0:TT],
                           l=b2r[0:1, e * 1024 + dc * 128:e * 1024 + dc * 128 + 128],
                           r=kgrow_b[0:1, t0:t0 + TT]:
                           te.matmul(o, l, r, start=True, stop=False),
                           [slot[1], lkgb], inc=None)
                for ci in range(8):
                    tok = kb.on("t", lambda te, o=ps[:, 0:TT],
                                l=slab_lhsT(ctx, 8192, 1024, ci, dc * 128, 128, 128),
                                r=ctx.BIG[:, 24576 + ci * 512:24576 + ci * 512 + TT],
                                sp=(ci == 7): te.matmul(o, l, r, start=False, stop=sp),
                                [hts[ci]] if ci == 0 else (), inc="t" if ci == 7 else None)
                ystg = ctx.FW[:, dc * 512:dc * 512 + TT]
                y1 = kb.on("s", lambda sE, o=ystg, i=ps[:, 0:TT]:
                           sE.activation(o, i, AF.Identity),
                           [tok] + (hfree.get(("y", dc)) or []), inc="s")
                slot[1] = y1
                if e == 0:
                    yd = kb.dma("g", Yacc[dc * 128:dc * 128 + 128, t0:t0 + TT], ystg, [y1])
                else:
                    yd = kb.on("g", lambda g, o=Yacc[dc * 128:dc * 128 + 128, t0:t0 + TT],
                               i=ystg: g.dma_start(o, i, accum_op=ALU.add),
                               [y1], inc="dg", inc_amt=16)
                hfree[("y", dc)] = [yd]
            hfree[("x", par)] = kb.tok("t")
        bk_set(ctx, "wfree", kb.tok("t"))
        bk_set(ctx, "kgrow", kb.tok("t"))
    kb.fence()
    postln_sum(ctx, X, Yacc, XB)


def bk_get(ctx, k):
    return getattr(ctx, "_bkd", {}).get(k)


def bk_set(ctx, k, v):
    if not hasattr(ctx, "_bkd"):
        ctx._bkd = {}
    ctx._bkd[k] = v


def col_layout(v):
    n = v.shape[0]
    nc_ = cdiv(n, 128)
    out = np.zeros((128, nc_), np.float32)
    for c in range(nc_):
        sz = min(128, n - c * 128)
        out[0:sz, c] = v[c * 128:c * 128 + sz]
    return np.ascontiguousarray(out)


def build_program(block_list, shapes):
    nc = bass.Bass("TRN2", target_bir_lowering=False, debug=False)
    es = ExitStack()
    io = {}

    def din(name, shape, dt=F32):
        if name not in io:
            io[name] = nc.dram_tensor(name, list(shape), dt, kind="ExternalInput").ap()
        return io[name]

    din("x_fm", (D, T))
    xout = nc.dram_tensor("xout", [D, T], F32, kind="ExternalOutput").ap()
    X = nc.dram_tensor("X", [D, T], F32).ap()
    XB = nc.dram_tensor("XB", [128, NCH * XBW], BF16).ap()
    H = nc.dram_tensor("Hscr", [4096, T], BF16).ap()
    io["_QS"] = nc.dram_tensor("QSscr", [D, T], BF16).ap()
    io["_KS"] = nc.dram_tensor("KSscr", [D, T], BF16).ap()
    io["_VS"] = nc.dram_tensor("VSscr", [T, D], BF16).ap()
    io["_AO"] = nc.dram_tensor("AOscr", [D, T], BF16).ap()
    io["_Q2S"] = nc.dram_tensor("Q2Sscr", [D, T], BF16).ap()
    io["_K2S"] = nc.dram_tensor("K2Sscr", [D, T], BF16).ap()
    io["_QRS"] = nc.dram_tensor("QRSscr", [D, T], BF16).ap()
    io["_KRS"] = nc.dram_tensor("KRSscr", [D, T], BF16).ap()
    io["_X0"] = nc.dram_tensor("X0scr", [D, T], F32).ap()
    io["_VS2"] = nc.dram_tensor("VS2scr", [D, T], BF16).ap()
    io["_KG"] = nc.dram_tensor("KGscr", [8 * D, T], BF16).ap()
    io["_VG"] = nc.dram_tensor("VGscr", [8 * D, T], BF16).ap()
    io["_CNT"] = nc.dram_tensor("CNTscr", [1, 8], F32).ap()
    io["_CNTG"] = nc.dram_tensor("CNTGscr", [8, 8], F32).ap()
    io["_KGH"] = nc.dram_tensor("KGHscr", [8, T], F32).ap()
    io["_GRH"] = nc.dram_tensor("GRHscr", [8, 1], F32).ap()
    io["_GTH"] = nc.dram_tensor("GTHscr", [32, 128], F32).ap()
    io["_Yacc"] = nc.dram_tensor("Yaccscr", [D, T], F32).ap()

    for bname in block_list:
        for n, shp, dt in INPUT_DEFS[bname]:
            din(n, shp, dt)

    kb = KB()
    ctx = Ctx(nc, kb, es)
    CTX_HACK.clear()
    CTX_HACK.append(ctx)
    ctx.setup()

    zrow = ctx.CF[:, 3064:3072]
    kb.on("g", lambda g: g.memset(zrow, 0.0), inc="g")
    kb.dma("q", X[:, :], io["x_fm"][:, :])
    for c in range(NCH):
        kb.dma("g", XB[:, c * XBW + XBO:c * XBW + XBO + T],
               io["x_fm"][c * 128:(c + 1) * 128, 0:T])
        kb.dma("g", XB[:, c * XBW:c * XBW + XBO], zrow[:, 0:4])
    kb.fence()

    for bname in block_list:
        EMITTERS[bname](ctx, io, X, XB, H)
        kb.fence()

    kb.dma("q", xout[:, :], X[:, :])
    if DEBUG_CM and "L2_attn" in block_list:
        dbgcm = nc.dram_tensor("dbgCM", [16, 8], F32, kind="ExternalOutput").ap()
        kb.dma("q", dbgcm, CTX_HACK[0].CF[0:16, 2240:2248])
    srcs = {"H": (H, BF16), "X": (X, F32), "XB": (XB, BF16),
            "QS": (io.get("_QS"), BF16), "KS": (io.get("_KS"), BF16),
            "VS": (io.get("_VS"), BF16), "AO": (io.get("_AO"), BF16),
            "VS2": (io.get("_VS2"), BF16), "Yacc": (io.get("_Yacc"), F32),
            "KG": (io.get("_KG"), BF16), "VG": (io.get("_VG"), BF16),
            "KGH": (io.get("_KGH"), F32), "CNTG": (io.get("_CNTG"), F32),
            "GTH": (io.get("_GTH"), F32), "GX": (io.get("_QS"), BF16)}
    for nm, src, shp in DEBUG_DUMP:
        dbg = nc.dram_tensor("dbg" + nm, list(shp), srcs[src][1], kind="ExternalOutput").ap()
        kb.dma("q", dbg, srcs[src][0][:, :])
    kb.fence()
    kb.on("v", lambda v: v.engine_nop())
    kb.emit(nc, es)
    es.close()
    return nc


DEBUG_DUMP = []
L0FFN_LAYERS = 6
L0FFN_OUTER = True
L2_SERIAL = True
DEBUG_CM = False
L2_DUMP = False
L2_DELAY = False
CTX_HACK = []

INPUT_DEFS = {
    "L2_attn": [
        ("L2_attn_qkv_wT", (1024, 3072), F32),
        ("L2_attn_qkv_b_c", (128, 24), F32),
        ("L2_attn_out_wT", (1024, 1024), F32),
        ("L2_attn_out_b_c", (128, 8), F32),
        ("_l2cm", (16, 8), F32),
        ("_ind16", (128, 128), F32),
        ("_selc", (16, 1024), F32),
    ],
    "L2_ffn": [
        ("L2_ffn_Wr", (1024, 8), F32),
        ("L2_ffn_br", (8,), F32),
        ("L2_ffn_b1_c", (128, 64), F32),
        ("L2_ffn_b2r", (1, 8192), F32),
        ("_l2pfx", (8, 1), F32),
    ] + [(f"L2_ffn_W1_{e}", (1024, 1024), F32) for e in range(8)]
      + [(f"L2_ffn_W2_{e}", (1024, 1024), F32) for e in range(8)],
    "L0_ffn": (
        [(f"L0_ffn_Wqkv_{i}", (1024, 3072), F32) for i in range(6)]
        + [(f"L0_ffn_Wo_{i}", (1024, 1024), F32) for i in range(6)]
        + [(f"L0_ffn_W1p_{i}", (1024, 5632), F32) for i in range(6)]
        + [(f"L0_ffn_W2p_{i}", (2816, 1024), F32) for i in range(6)]
        + [("L0_ffn_b1a_c", (128, 132), F32), ("L0_ffn_b1g_c", (128, 132), F32),
           ("L0_ffn_b2_c", (128, 48), F32), ("L0_ffn_lng_r", (6, 1024), F32),
           ("L0_ffn_lnb_c", (128, 48), F32), ("L0_ffn_qs_c", (6, 128), F32),
           ("L0_ffn_ks_c", (6, 128), F32),
           ("L0_ffn_qs2_c", (6, 128), F32), ("L0_ffn_ks2_c", (6, 128), F32),
           ("L0_ffn_qs_r", (6, 64), F32), ("L0_ffn_ks_r", (6, 64), F32),
           ("_xC1s", (128, 4096), F32), ("_xC2s", (128, 4096), F32),
           ("_xC1i", (128, 4096), F32), ("_xC2i", (128, 4096), F32),
           ("_rhP", (128, 128), F32), ("_selc", (16, 1024), F32),
           ("_ind16", (128, 128), F32), ("_sel16", (16, 2048), F32)]
    ),
    "L1_attn": [
        ("L1_attn_W_q", (1024, 1024), F32),
        ("L1_attn_W_k", (1024, 1024), F32),
        ("L1_attn_W_v", (1024, 1024), F32),
        ("L1_attn_W_o", (1024, 1024), F32),
        ("L1_attn_W_down_kv", (1024, 256), F32),
        ("L1_attn_W_up_k", (256, 1024), F32),
        ("L1_attn_W_up_v", (256, 1024), F32),
        ("_l1fqm", (1, 4096), F32),
        ("_l1fqp", (1, 4096), F32),
        ("_l1fkm", (1, 4096), F32),
        ("_l1fkp", (1, 4096), F32),
        ("_l1D", (128, 2048), F32),
        ("_ind16", (128, 128), F32),
        ("_sel16", (16, 2048), F32),
    ],
    "L0_attn": [
        ("L0_attn_Wq1", (1024, 1024), F32),
        ("L0_attn_Wq2", (1024, 1024), F32),
        ("L0_attn_Wk1", (1024, 128), F32),
        ("L0_attn_Wk2", (1024, 128), F32),
        ("L0_attn_Wv", (1024, 128), F32),
        ("L0_attn_Wout", (1024, 1024), F32),
        ("L0_attn_bout_c", (128, 8), F32),
        ("L0_attn_lq1", (64,), F32),
        ("L0_attn_lk1", (64,), F32),
        ("L0_attn_lq2", (64,), F32),
        ("L0_attn_lk2", (64,), F32),
        ("_r0T1q", (128, 4096), F32),
        ("_r0T2q", (128, 4096), F32),
        ("_r0T1k", (128, 4096), F32),
        ("_r0T2k", (128, 4096), F32),
        ("_r0P1", (128, 128), F32),
        ("_r0P2", (128, 128), F32),
        ("_selkv", (2, 16), F32),
        ("_ind16", (128, 128), F32),
        ("_sel16", (16, 2048), F32),
    ],
    "L3_attn": [
        ("L3_attn_W_qkv", (1024, 3072), F32),
        ("L3_attn_b_qkv", (3072,), F32),
        ("L3_attn_b_qkv_c", (128, 24), F32),
        ("L3_attn_W_out", (1024, 1024), F32),
        ("L3_attn_b_out_c", (128, 8), F32),
        ("_ind16", (128, 128), F32),
        ("_sel16", (16, 2048), F32),
    ],
    "L3_ffn": [
        ("L3_ffn_wgu", (1024, 6144), I32),
        ("L3_ffn_wd", (3072, 1024), I32),
    ],
    "L1_ffn": [
        ("L1_ffn_W1", (1024, 4096), F32),
        ("L1_ffn_W2", (4096, 1024), F32),
        ("L1_ffn_quantum_phases_c", (128, 32), F32),
        ("L1_ffn_quantum_weights_real_c", (128, 32), F32),
        ("L1_ffn_quantum_weights_imag_c", (128, 32), F32),
        ("L1_ffn_b1_c", (128, 32), F32),
        ("L1_ffn_b2_c", (128, 8), F32),
    ],
}

EMITTERS = {
    "L2_attn": emit_L2_attn,
    "L2_ffn": emit_L2_ffn,
    "L0_ffn": emit_L0_ffn,
    "L1_attn": emit_L1_attn,
    "L0_attn": emit_L0_attn,
    "L1_ffn": emit_L1_ffn,
    "L3_ffn": emit_L3_ffn,
    "L3_attn": emit_L3_attn,
}


def prep_inputs_L1_ffn(inp):
    return {
        "L1_ffn_W1": np.ascontiguousarray(inp["L1_ffn_W1"], np.float32),
        "L1_ffn_W2": np.ascontiguousarray(inp["L1_ffn_W2"], np.float32),
        "L1_ffn_quantum_phases_c": col_layout(inp["L1_ffn_quantum_phases"]),
        "L1_ffn_quantum_weights_real_c": col_layout(inp["L1_ffn_quantum_weights_real"]),
        "L1_ffn_quantum_weights_imag_c": col_layout(inp["L1_ffn_quantum_weights_imag"]),
        "L1_ffn_b1_c": col_layout(inp["L1_ffn_b1"]),
        "L1_ffn_b2_c": col_layout(inp["L1_ffn_b2"]),
    }


def prep_inputs_L3_ffn(inp):
    wg = np.ascontiguousarray(inp["L3_ffn_w_gate"].T)
    wu = np.ascontiguousarray(inp["L3_ffn_w_up"].T)
    wgu = np.empty((1024, 6144), np.int32)
    for c in range(24):
        wgu[:, c * 256:c * 256 + 128] = wg[:, c * 128:(c + 1) * 128]
        wgu[:, c * 256 + 128:c * 256 + 256] = wu[:, c * 128:(c + 1) * 128]
    return {
        "L3_ffn_wgu": wgu,
        "L3_ffn_wd": np.ascontiguousarray(inp["L3_ffn_w_down"].T),
    }


def consts_attn():
    ind16 = np.zeros((128, 128), np.float32)
    sel16 = np.zeros((16, 2048), np.float32)
    for c in range(8):
        for p in range(128):
            ind16[p, c * 16 + 2 * c + p // 64] = 1.0
    for j in range(16):
        sel16[j, j * 128:(j + 1) * 128] = 1.0
    return {"_ind16": ind16, "_sel16": sel16}


def prep_inputs_L3_attn(inp):
    d = {
        "L3_attn_W_qkv": np.ascontiguousarray(inp["L3_attn_W_qkv"], np.float32),
        "L3_attn_b_qkv": np.ascontiguousarray(inp["L3_attn_b_qkv"], np.float32),
        "L3_attn_b_qkv_c": col_layout(inp["L3_attn_b_qkv"]),
        "L3_attn_W_out": np.ascontiguousarray(inp["L3_attn_W_out"], np.float32),
        "L3_attn_b_out_c": col_layout(inp["L3_attn_b_out"]),
    }
    d.update(consts_attn())
    return d


def prep_inputs_L0_attn(inp):
    pos = np.arange(4096, dtype=np.float64)[:, None]
    div = np.exp(np.arange(0, 64, 2, dtype=np.float64) * (-np.log(10000.0) / 64))
    sin = np.sin(pos * div).astype(np.float32)
    cos = np.cos(pos * div).astype(np.float32)
    T1 = np.zeros((128, 4096), np.float32)
    T2 = np.zeros((128, 4096), np.float32)
    for h in range(2):
        b = h * 64
        T1[b:b + 32] = cos.T
        T1[b + 32:b + 64] = sin.T
        T2[b:b + 32] = -sin.T
        T2[b + 32:b + 64] = cos.T
    P1 = np.zeros((128, 128), np.float32)
    P2 = np.zeros((128, 128), np.float32)
    for h in range(2):
        b = h * 64
        for j in range(32):
            P1[b + 2 * j, b + j] = 1.0
            P1[b + 2 * j, b + 32 + j] = 1.0
            P2[b + 2 * j + 1, b + j] = 1.0
            P2[b + 2 * j + 1, b + 32 + j] = 1.0
    selkv = np.zeros((2, 16), np.float32)
    for hh in range(16):
        selkv[hh % 2, hh] = 1.0
    d = {
        "L0_attn_Wq1": np.ascontiguousarray(inp["L0_attn_Wq1"], np.float32),
        "L0_attn_Wq2": np.ascontiguousarray(inp["L0_attn_Wq2"], np.float32),
        "L0_attn_Wk1": np.ascontiguousarray(inp["L0_attn_Wk1"], np.float32),
        "L0_attn_Wk2": np.ascontiguousarray(inp["L0_attn_Wk2"], np.float32),
        "L0_attn_Wv": np.ascontiguousarray(inp["L0_attn_Wv"], np.float32),
        "L0_attn_Wout": np.ascontiguousarray(inp["L0_attn_Wout"], np.float32),
        "L0_attn_bout_c": col_layout(inp["L0_attn_bout"]),
        "L0_attn_lq1": np.ascontiguousarray(inp["L0_attn_lq1"], np.float32),
        "L0_attn_lk1": np.ascontiguousarray(inp["L0_attn_lk1"], np.float32),
        "L0_attn_lq2": np.ascontiguousarray(inp["L0_attn_lq2"], np.float32),
        "L0_attn_lk2": np.ascontiguousarray(inp["L0_attn_lk2"], np.float32),
        "_r0T1q": T1 * 0.125, "_r0T2q": T2 * 0.125, "_r0T1k": T1, "_r0T2k": T2,
        "_r0P1": P1, "_r0P2": P2, "_selkv": selkv,
    }
    d.update(consts_attn())
    return d


def prep_inputs_L1_attn(inp):
    lam = 0.01
    i = np.arange(4096, dtype=np.float64)
    fqm = (np.exp(-lam * i) / 32.0).astype(np.float32)[None, :]
    fqp = (np.exp(lam * i) / 32.0).astype(np.float32)[None, :]
    fkm = np.exp(-lam * i).astype(np.float32)[None, :]
    fkp = np.exp(lam * i).astype(np.float32)[None, :]
    Dt = np.zeros((128, 2048), np.float32)
    kl = np.arange(128)[:, None]
    ql = np.arange(512)[None, :]
    for di in range(4):
        Dt[:, di * 512:(di + 1) * 512] = np.exp(-lam * np.abs(di * 128 + kl - ql)) / 32.0
    d = {
        "_l1fqm": fqm, "_l1fqp": fqp, "_l1fkm": fkm, "_l1fkp": fkp, "_l1D": Dt,
    }
    for n in ["W_q", "W_k", "W_v", "W_o", "W_down_kv", "W_up_k", "W_up_v"]:
        d["L1_attn_" + n] = np.ascontiguousarray(inp["L1_attn_" + n], np.float32)
    d.update(consts_attn())
    return d


def prep_inputs_L0_ffn(inp):
    d = {}
    W1 = np.asarray(inp["L0_ffn_W1"], np.float32)
    W2 = np.asarray(inp["L0_ffn_W2"], np.float32)
    b1 = np.asarray(inp["L0_ffn_b1"], np.float32)
    for i in range(6):
        d[f"L0_ffn_Wqkv_{i}"] = np.ascontiguousarray(inp["L0_ffn_Wqkv"][i], np.float32)
        d[f"L0_ffn_Wo_{i}"] = np.ascontiguousarray(inp["L0_ffn_Wo"][i], np.float32)
        a = np.zeros((1024, 2816), np.float32)
        g = np.zeros((1024, 2816), np.float32)
        a[:, :2730] = W1[i][:, :2730]
        g[:, :2730] = W1[i][:, 2730:]
        w1p = np.zeros((1024, 5632), np.float32)
        for c in range(22):
            w1p[:, c * 256:c * 256 + 128] = a[:, c * 128:(c + 1) * 128]
            w1p[:, c * 256 + 128:c * 256 + 256] = g[:, c * 128:(c + 1) * 128]
        d[f"L0_ffn_W1p_{i}"] = w1p
        w2p = np.zeros((2816, 1024), np.float32)
        w2p[:2730] = W2[i]
        d[f"L0_ffn_W2p_{i}"] = w2p
    b1a = np.zeros((128, 132), np.float32)
    b1g = np.zeros((128, 132), np.float32)
    for i in range(6):
        av = np.zeros(2816, np.float32)
        gv = np.zeros(2816, np.float32)
        av[:2730] = b1[i][:2730]
        gv[:2730] = b1[i][2730:]
        b1a[:, i * 22:(i + 1) * 22] = col_layout(av)
        b1g[:, i * 22:(i + 1) * 22] = col_layout(gv)
    d["L0_ffn_b1a_c"] = b1a
    d["L0_ffn_b1g_c"] = b1g
    d["L0_ffn_b2_c"] = np.concatenate(
        [col_layout(np.asarray(inp["L0_ffn_b2"][i], np.float32)) for i in range(6)], 1)
    d["L0_ffn_lng_r"] = np.ascontiguousarray(inp["L0_ffn_ln_g"], np.float32)
    d["L0_ffn_lnb_c"] = np.concatenate(
        [col_layout(np.asarray(inp["L0_ffn_ln_b"][i], np.float32)) for i in range(6)], 1)
    qs = np.asarray(inp["L0_ffn_q_scale"], np.float32)
    ks = np.asarray(inp["L0_ffn_k_scale"], np.float32)
    d["L0_ffn_qs_c"] = np.ascontiguousarray(np.tile(qs, (1, 2)))
    d["L0_ffn_ks_c"] = np.ascontiguousarray(np.tile(ks, (1, 2)))
    qs2 = np.roll(qs, 32, axis=1)
    ks2 = np.roll(ks, 32, axis=1)
    d["L0_ffn_qs2_c"] = np.ascontiguousarray(np.tile(qs2, (1, 2)))
    d["L0_ffn_qs_r"] = np.ascontiguousarray(qs)
    d["L0_ffn_ks_r"] = np.ascontiguousarray(ks)
    d["L0_ffn_ks2_c"] = np.ascontiguousarray(np.tile(ks2, (1, 2)))
    n, DH, W = 4096, 64, 256
    half = np.arange(0, DH, 2, dtype=np.float64)
    inv_freq = 1.0 / 10000.0 ** (half / DH)
    t = np.arange(n, dtype=np.float64)
    freqs = t[:, None] * inv_freq[None, :]
    freqs = np.concatenate([freqs, freqs], -1)
    base_scale = (half + 0.4 * DH) / (1.4 * DH)
    power = (t - n // 2) / (W // 2)
    scale = base_scale[None, :] ** power[:, None]
    scale = np.concatenate([scale, scale], -1)
    C1s = (np.cos(freqs) * scale * 8.0).astype(np.float32).T
    C2s = (np.sin(freqs) * scale * 8.0).astype(np.float32).T
    C1i = (np.cos(freqs) / scale).astype(np.float32).T
    C2i = (np.sin(freqs) / scale).astype(np.float32).T
    for nm, tab in [("_xC1s", C1s), ("_xC2s", C2s), ("_xC1i", C1i), ("_xC2i", C2i)]:
        d[nm] = np.ascontiguousarray(np.tile(tab, (2, 1)))
    rh = np.zeros((128, 128), np.float32)
    for hh in range(2):
        b_ = hh * 64
        for j in range(32):
            rh[b_ + 32 + j, b_ + j] = -1.0
            rh[b_ + j, b_ + 32 + j] = 1.0
    d["_rhP"] = rh
    selc = np.zeros((16, 1024), np.float32)
    for c in range(8):
        for p in range(128):
            selc[2 * c + p // 64, c * 128 + p] = 1.0
    d["_selc"] = selc
    d.update(consts_attn())
    return d


def prep_inputs_L2_attn(inp):
    d = {
        "L2_attn_qkv_wT": np.ascontiguousarray(np.asarray(inp["L2_attn_qkv_w"], np.float32).T),
        "L2_attn_qkv_b_c": col_layout(np.asarray(inp["L2_attn_qkv_b"], np.float32)),
        "L2_attn_out_wT": np.ascontiguousarray(np.asarray(inp["L2_attn_out_w"], np.float32).T),
        "L2_attn_out_b_c": col_layout(np.asarray(inp["L2_attn_out_b"], np.float32)),
    }
    d.update(consts_attn())
    selc = np.zeros((16, 1024), np.float32)
    for c in range(8):
        for p in range(128):
            selc[2 * c + p // 64, c * 128 + p] = 1.0
    d["_selc"] = selc
    return d


def prep_inputs_L2_ffn(inp):
    d = {
        "L2_ffn_Wr": np.ascontiguousarray(np.asarray(inp["L2_ffn_Wr"], np.float32)),
        "L2_ffn_br": np.ascontiguousarray(np.asarray(inp["L2_ffn_br"], np.float32)),
        "L2_ffn_b1_c": np.concatenate(
            [col_layout(np.asarray(inp["L2_ffn_b1"][e], np.float32)) for e in range(8)], 1),
        "L2_ffn_b2r": np.ascontiguousarray(
            np.asarray(inp["L2_ffn_b2"], np.float32).reshape(1, 8192)),
    }
    for e in range(8):
        d[f"L2_ffn_W1_{e}"] = np.ascontiguousarray(np.asarray(inp["L2_ffn_W1"][e], np.float32))
        d[f"L2_ffn_W2_{e}"] = np.ascontiguousarray(np.asarray(inp["L2_ffn_W2"][e], np.float32))
    return d


PREP = {"L2_attn": prep_inputs_L2_attn, "L2_ffn": prep_inputs_L2_ffn,
        "L1_ffn": prep_inputs_L1_ffn, "L3_ffn": prep_inputs_L3_ffn,
        "L3_attn": prep_inputs_L3_attn, "L0_attn": prep_inputs_L0_attn,
        "L1_attn": prep_inputs_L1_attn, "L0_ffn": prep_inputs_L0_ffn}


def run_blocks(block_list, x, inp, cores=None):
    B = x.shape[0]
    cores = cores if cores is not None else list(range(B))
    nc = build_program(block_list, None)
    common = {}
    for b in block_list:
        common.update(PREP[b](inp))
    in_maps = []
    for c in range(len(cores)):
        m = dict(common)
        m["x_fm"] = np.ascontiguousarray(x[c].T)
        if "L2_attn" in block_list:
            cm = np.zeros((16, 8), np.float32)
            cm[:, :c + 1] = 1.0
            m["_l2cm"] = cm
        if "L2_ffn" in block_list:
            pf = np.zeros((8, 1), np.float32)
            pf[:c] = 1.0
            m["_l2pfx"] = pf
        in_maps.append(m)
    res = run_bass_kernel_spmd(nc, in_maps, core_ids=list(range(len(cores))))
    return np.stack([r["xout"].T for r in res.results])


ALL_BLOCKS = ["L0_attn", "L0_ffn", "L1_attn", "L1_ffn",
              "L2_attn", "L2_ffn", "L3_attn", "L3_ffn"]


def kernel(**inputs):
    x = np.asarray(inputs["L0_attn_x"], np.float32)
    return run_blocks(ALL_BLOCKS, x, inputs)
